# version 10
# baseline (speedup 1.0000x reference)
import numpy as np
import ml_dtypes
from contextlib import ExitStack

DBG = False
SERIALIZE_DMA = False

import concourse.bass as bass
import concourse.mybir as mybir
from concourse.bass_utils import run_bass_kernel_spmd

F32 = mybir.dt.float32
BF16 = mybir.dt.bfloat16
AF = mybir.ActivationFunctionType
ALU = mybir.AluOpType
AX = mybir.AxisListType
BF = ml_dtypes.bfloat16

T = 4096
D = 1024
NCC = D // 128
PADW = 4104
NTT = T // 128
NT5 = T // 512


ENGINES = ("tensor", "vector", "scalar", "gpsimd", "sync")
DMA_ENGINES = ("gpsimd", "sync", "scalar")


NDSLOT = 8


class Sched:
    def __init__(self, nc, stack):
        self.nc = nc
        self.q = {e: [] for e in ENGINES}
        self.cnt = {e: 0 for e in ENGINES}
        self.sem = {e: stack.enter_context(nc.semaphore(f"se_{e}")) for e in ENGINES}
        self.dsem = {e: [stack.enter_context(nc.semaphore(f"sd_{e}{i}"))
                         for i in range(NDSLOT)] for e in DMA_ENGINES}
        self.duse = {e: [0] * NDSLOT for e in DMA_ENGINES}
        self.didx = {e: 0 for e in DMA_ENGINES}
        self.known = {e: {} for e in ENGINES}
        self.lastw = {}
        self.readers = {}
        self.nwaits = 0

    def _wait(self, e, tok):
        kind, se, val = tok
        if kind == "e" and se == e == "tensor":
            return
        k = (kind, se)
        if self.known[e].get(k, 0) >= val:
            return
        self.known[e][k] = val
        sem = self.sem[se] if kind == "e" else self.dsem[se[0]][se[1]]
        self.q[e].append(("w", sem, val))
        self.nwaits += 1

    def emit(self, e, fn, reads=(), writes=(), dma=False, defer_inc=False):
        deps = []
        for r in reads:
            w = self.lastw.get(r)
            if w is not None:
                deps.append(w)
        for wk in writes:
            w = self.lastw.get(wk)
            if w is not None:
                deps.append(w)
            rd = self.readers.get(wk)
            if rd:
                for (kind, se), val in rd.items():
                    deps.append((kind, se, val))
        if e == "tensor" and not dma and getattr(self, "defer_pending", False):
            need = False
            for kind, se, val in deps:
                if kind == "e" and se == e == "tensor":
                    continue
                k = (kind, se)
                if self.known[e].get(k, 0) < val:
                    need = True
                    break
            if need:
                self.cnt[e] += 1
                self.q[e].append(("i", lambda eng: eng.nop(),
                                  self.sem[e], 1))
                self.defer_pending = False
        for d in deps:
            self._wait(e, d)
        if dma:
            slot = self.didx[e] % NDSLOT
            self.didx[e] += 1
            if self.duse[e][slot]:
                self._wait(e, ("d", (e, slot), 16 * self.duse[e][slot]))
            self.duse[e][slot] += 1
            tok = ("d", (e, slot), 16 * self.duse[e][slot])
            self.q[e].append(("i", fn, self.dsem[e][slot], 16))
            if SERIALIZE_DMA:
                self._wait(e, tok)
        elif defer_inc:
            tok = ("e", e, self.cnt[e] + 1)
            self.q[e].append(("n", fn))
            self.defer_pending = True
        else:
            self.cnt[e] += 1
            tok = ("e", e, self.cnt[e])
            self.q[e].append(("i", fn, self.sem[e], 1))
            if e == "tensor":
                self.defer_pending = False
        for r in reads:
            d = self.readers.setdefault(r, {})
            k = tok[:2]
            if d.get(k, 0) < tok[2]:
                d[k] = tok[2]
        for wk in writes:
            self.lastw[wk] = tok
            self.readers[wk] = {}
        return tok

    def barrier_on(self, e, toks):
        for tok in toks:
            self._wait(e, tok)

    def flush_defer(self):
        if getattr(self, "defer_pending", False):
            self.cnt["tensor"] += 1
            self.q["tensor"].append(
                ("i", lambda eng: eng.nop(), self.sem["tensor"], 1))
            self.defer_pending = False

    def global_barrier(self):
        self.flush_defer()
        snap_e = {e: self.cnt[e] for e in ENGINES}
        snap_d = {e: list(self.duse[e]) for e in DMA_ENGINES}
        for e in ENGINES:
            for o in ENGINES:
                if snap_e[o]:
                    self._wait(e, ("e", o, snap_e[o]))
            for o in DMA_ENGINES:
                for i in range(NDSLOT):
                    if snap_d[o][i]:
                        self._wait(e, ("d", (o, i), 16 * snap_d[o][i]))

    def finish(self):
        if getattr(self, "defer_pending", False):
            self.cnt["tensor"] += 1
            self.q["tensor"].append(
                ("i", lambda eng: eng.nop(), self.sem["tensor"], 1))
            self.defer_pending = False
        for e in DMA_ENGINES:
            for i in range(NDSLOT):
                if self.duse[e][i]:
                    self.q["sync"].append(("w", self.dsem[e][i],
                                           16 * self.duse[e][i]))
        for e in ENGINES:
            if e == "sync":
                continue
            if self.cnt[e]:
                self.q["sync"].append(("w", self.sem[e], self.cnt[e]))

    def replay(self, block):
        def mk(items):
            def f(eng):
                for it in items:
                    if it[0] == "w":
                        eng.wait_ge(it[1], it[2])
                    elif it[0] == "n":
                        it[1](eng)
                    else:
                        it[1](eng).then_inc(it[2], it[3])
            return f
        block.tensor(mk(self.q["tensor"]))
        block.vector(mk(self.q["vector"]))
        block.scalar(mk(self.q["scalar"]))
        block.gpsimd(mk(self.q["gpsimd"]))
        block.sync(mk(self.q["sync"]))


class Ctx:

    def __init__(self, nc, sched, stack):
        self.nc = nc
        self.s = sched
        self.stack = stack
        self.pf = [stack.enter_context(nc.psum_tensor(f"pf{i}", [128, 512], F32))
                   for i in range(6)]
        self.pb = [stack.enter_context(nc.psum_tensor(f"pb{i}", [128, 1024], BF16))
                   for i in range(2)]
        self._rot_f = 0
        self._rot_b = 0

    def pfree(self):
        i = self._rot_f % 3
        self._rot_f += 1
        return self.pf[i], ("psF", i)

    def pbank(self, i):
        return self.pf[i], ("psF", i)

    def pfree4(self):
        self._rot_f += 1
        i = self._rot_f % 4
        return self.pf[i], ("psF", i)

    def pbf(self):
        i = self._rot_b % 2
        self._rot_b += 1
        return self.pb[i], ("psB", i)

    _N = [0]

    def sb(self, name, shape, dtype):
        Ctx._N[0] += 1
        return self.stack.enter_context(
            self.nc.sbuf_tensor(f"{name}_{Ctx._N[0]}", shape, dtype))



def e_dma(ctx, qeng, dst, src, reads=(), writes=(), accum=False):
    if accum:
        return ctx.s.emit(qeng,
                          lambda e: e.dma_start(dst, src, accum_op=ALU.add),
                          reads=reads, writes=writes, dma=True)
    return ctx.s.emit(qeng, lambda e: e.dma_start(dst, src),
                      reads=reads, writes=writes, dma=True)


def e_mm(ctx, out, lhsT, rhs, start, stop, reads=(), writes=(), tile_position=None):
    return ctx.s.emit(
        "tensor",
        lambda e: e.matmul(out, lhsT, rhs, start=start, stop=stop,
                           tile_position=tile_position),
        reads=reads, writes=writes)


def e_tr(ctx, out, in_, ident, reads=(), writes=()):
    return ctx.s.emit("tensor", lambda e: e.transpose(out, in_, ident),
                      reads=reads, writes=writes)


def _imm(v):
    return np.float32(v) if isinstance(v, float) else v


def e_act(ctx, out, in_, func, reads=(), writes=(), bias=0.0, scale=1.0,
          accum_out=None):
    bias, scale = _imm(bias), _imm(scale)
    return ctx.s.emit(
        "scalar",
        lambda e: e.activation(out, in_, func, bias=bias, scale=scale,
                               accum_out=accum_out),
        reads=reads, writes=writes)


def e_tt(ctx, op, out, a, b, reads=(), writes=(), eng="vector"):
    return ctx.s.emit(eng, lambda e: e.tensor_tensor(out, a, b, op),
                      reads=reads, writes=writes)


def e_ts(ctx, out, in0, s1, op0, s2=None, op1=ALU.bypass, reads=(), writes=(),
         eng="vector", accum_out=None):
    return ctx.s.emit(
        eng,
        lambda e: e.tensor_scalar(out, in0, s1, s2, op0, op1,
                                  accum_out=accum_out),
        reads=reads, writes=writes)


def e_stt(ctx, out, in0, scalar, in1, op0, op1, reads=(), writes=(),
          eng="vector", accum_out=None):
    return ctx.s.emit(
        eng,
        lambda e: e.scalar_tensor_tensor(out, in0, scalar, in1, op0, op1,
                                         accum_out=accum_out),
        reads=reads, writes=writes)


def e_copy(ctx, out, in_, reads=(), writes=(), eng="vector"):
    return ctx.s.emit(eng, lambda e: e.tensor_copy(out, in_),
                      reads=reads, writes=writes)


def e_red(ctx, out, in_, op, reads=(), writes=(), negate=None):
    return ctx.s.emit(
        "vector",
        lambda e: e.tensor_reduce(out, in_, AX.X, op, negate=negate),
        reads=reads, writes=writes)


def e_memset(ctx, ap, val, reads=(), writes=(), eng="vector"):
    return ctx.s.emit(eng, lambda e: e.memset(ap, val),
                      reads=reads, writes=writes)


def e_scan(ctx, out, d0, d1, init, op0, op1, reads=(), writes=()):
    return ctx.s.emit(
        "vector",
        lambda e: e.tensor_tensor_scan(out, d0, d1, init, op0, op1),
        reads=reads, writes=writes)



def load_wcols(ctx, name, w_ap, c0, c1, o0, o1, part=128):
    ncc = (c1 - c0) // part
    wb = ctx.sb(name, [part, ncc, o1 - o0], BF16)
    key = ("w", name)
    src = w_ap[c0:c1, o0:o1].rearrange("(cc p) o -> p cc o", p=part)
    e_dma(ctx, "gpsimd", wb[:, :, :], src, writes=[key])
    return wb, key


class WPool:

    def __init__(self, ctx, name, nslots, shape):
        self.ctx = ctx
        self.bufs = [ctx.sb(f"{name}{i}", shape, BF16) for i in range(nslots)]
        self.keys = [(f"wp_{name}", i) for i in range(nslots)]
        self.i = 0

    def load(self, src_ap):
        i = self.i % len(self.bufs)
        self.i += 1
        wb, key = self.bufs[i], self.keys[i]
        shp = src_ap.shape
        if len(wb.shape) == 3:
            dst = wb[:, 0:shp[1], 0:shp[2]]
        else:
            dst = wb[:, 0:shp[1]]
        e_dma(self.ctx, "gpsimd", dst, src_ap, writes=[key])
        return wb, key


def wcols_src(w_ap, c0, c1, o0, o1, part=128):
    return w_ap[c0:c1, o0:o1].rearrange("(cc p) o -> p cc o", p=part)


def emit_ln(ctx, x_in, accs, x_out, xn, xnkey, stats=None, bufs=None):
    nc = ctx.nc
    ident = bufs["ident"]
    xa = bufs["epi_a"]
    nacc = len(accs)
    for tt in range(NTT):
        t0 = tt * 128
        sl = (slice(t0, t0 + 128), slice(0, D))
        ka = ("epi_a", tt % 3)
        e_dma(ctx, "sync", xa[tt % 3][:, :], x_in[sl], writes=[ka])
        for ai, (acc, acck) in enumerate(accs):
            kb = ("epi_b", ai, tt % 2)
            e_dma(ctx, "sync", bufs["epi_b"][ai][tt % 2][:, :], acc[sl],
                  writes=[kb], reads=[(acck, tt)])
            e_tt(ctx, ALU.add, xa[tt % 3][:, :], xa[tt % 3][:, :],
                 bufs["epi_b"][ai][tt % 2][:, :], reads=[ka, kb], writes=[ka])
        if x_out is not None:
            e_dma(ctx, "scalar", x_out[sl], xa[tt % 3][:, :], reads=[ka])
        if xn is None:
            continue
        sl2 = tt % 2
        mkey = ("epi_m", sl2)
        m = bufs["epi_m"][sl2]
        e_red(ctx, m[:, 0:1], xa[tt % 3][:, :], ALU.add, reads=[ka], writes=[mkey])
        e_ts(ctx, m[:, 1:2], m[:, 0:1], 1.0 / D, ALU.mult, reads=[mkey],
             writes=[mkey])
        xc = bufs["epi_xc"][sl2]
        e_ts(ctx, xc[:, :], xa[tt % 3][:, :], m[:, 1:2], ALU.subtract,
             reads=[ka, mkey], writes=[("epi_xc", sl2)])
        e_act(ctx, bufs["epi_sq"][sl2][:, :], xc[:, :], AF.Square,
              reads=[("epi_xc", sl2)], writes=[("epi_sq", sl2), ("epi_ss", sl2)],
              accum_out=bufs["epi_ss"][sl2][:, 0:1])
        e_act(ctx, m[:, 2:3], bufs["epi_ss"][sl2][:, 0:1], AF.Ln, bias=1e-5,
              scale=1.0 / D, reads=[("epi_ss", sl2)], writes=[mkey])
        e_act(ctx, m[:, 3:4], m[:, 2:3], AF.Exp, scale=-0.5, reads=[mkey],
              writes=[mkey])
        if stats is not None:
            e_copy(ctx, stats[0][:, tt:tt + 1], m[:, 1:2], reads=[mkey],
                   writes=[("stats",)])
            e_copy(ctx, stats[1][:, tt:tt + 1], m[:, 3:4], reads=[mkey],
                   writes=[("stats",)])
        xtm = bufs["epi_tm"][sl2]
        e_ts(ctx, xtm[:, :], xc[:, :], m[:, 3:4], ALU.mult,
             reads=[("epi_xc", sl2), mkey], writes=[("epi_tm", sl2)])
        pbt, pbk = ctx.pbf()
        for cc in range(NCC):
            e_tr(ctx, pbt[:, cc * 128:(cc + 1) * 128],
                 xtm[:, cc * 128:(cc + 1) * 128], ident[:, :],
                 reads=[("epi_tm", sl2), ("ident",)], writes=[pbk])
        e_copy(ctx, xn[:, :, 1 + t0:1 + t0 + 128],
               pbt[:, :].rearrange("p (cc t) -> p cc t", cc=NCC),
               reads=[pbk], writes=[(xnkey, tt)])


def xn_all(xnkey):
    return [(xnkey, tt) for tt in range(NTT)] + [(xnkey, "all")]



def emit_l0_attn(ctx, aps, xn, xnkey, bufs):
    nc = ctx.nc
    ident = bufs["ident"]
    XN = xn_all(xnkey)

    qp = ctx.sb("l0_qp", [128, T], BF16)
    k1p = ctx.sb("l0_k1p", [128, T], BF16)
    vp = ctx.sb("l0_vp", [128, T], BF16)
    k2p = ctx.sb("l0_k2p", [128, T], BF16)
    q2p = ctx.sb("l0_q2p", [128, T], BF16)
    o1p = ctx.sb("l0_o1p", [128, T], BF16)
    y = ctx.sb("l0_y", [128, NCC, T], BF16)
    kt = ctx.sb("l0_kt", [128, 128], BF16)
    vt = ctx.sb("l0_vt", [128, 128], BF16)
    k2t = ctx.sb("l0_k2t", [128, 128], BF16)
    xnt = ctx.sb("l0_xnt", [128, 128], BF16)
    et = ctx.sb("l0_et", [128, 128], BF16)
    asb = ctx.sb("l0_asb", [128, 2, 128], BF16)
    s1sb = ctx.sb("l0_s1sb", [128, 64], BF16)
    s2sb = ctx.sb("l0_s2sb", [128, 64], BF16)
    s1f = ctx.sb("l0_s1f", [128, 64], F32)
    s2f = ctx.sb("l0_s2f", [128, 64], F32)

    wq = WPool(ctx, "l0wq", 2, [128, NCC, 128])
    mask1 = bufs["mask_triu"]
    maskq2 = bufs["mask_triu8"]

    bk2s = ctx.sb("l0_bk2s", [128, NCC], F32)
    e_ts(ctx, bk2s[:, :], bufs["b_k2"][:, :], 0.02 / 32.0, ALU.mult,
         reads=[("b_k2",)], writes=[("bk2s",)])

    for pr in range(8):
        projs = [("q", aps["Wq"], qp, ("l0qp",), bufs["b_q"], "idb"),
                 ("k1", aps["Wk1"], k1p, ("l0k1p",), bufs["b_k1"], "idb"),
                 ("v", aps["Wv"], vp, ("l0vp",), bufs["b_v"], "idb"),
                 ("k2", aps["Wk2"], k2p, ("l0k2p",), bk2s, "sig")]
        for pname, wap, dst, dkey, bvec, mode in projs:
            wb, wkey = wq.load(wcols_src(wap, 0, D, pr * 128, pr * 128 + 128))
            for tt in range(NT5):
                ps, pk = ctx.pfree()
                for cc in range(NCC):
                    e_mm(ctx, ps[:, :], wb[:, cc, :],
                         xn[:, cc, 1 + tt * 512:1 + tt * 512 + 512],
                         start=(cc == 0), stop=(cc == NCC - 1),
                         reads=[wkey, (xnkey, "all")] + XN, writes=[pk])
                if mode == "idb":
                    e_act(ctx, dst[:, tt * 512:(tt + 1) * 512], ps[:, :],
                          AF.Identity, bias=bvec[:, pr:pr + 1],
                          reads=[pk, ("bias_l0",)], writes=[dkey])
                else:
                    e_act(ctx, dst[:, tt * 512:(tt + 1) * 512], ps[:, :],
                          AF.Sigmoid, bias=bvec[:, pr:pr + 1], scale=0.000625,
                          reads=[pk, ("bk2s",)], writes=[dkey])
        e_stt(ctx, q2p[:, :], qp[:, :], 0.02, qp[:, :], ALU.mult, ALU.min,
              reads=[("l0qp",)], writes=[("l0q2p",)])

        stb = [ctx.pbank(3), ctx.pbank(4)]
        e_memset(ctx, s1f[:, :], 0.0, writes=[("s1f",)])
        for c in range(32):
            lo = c * 128
            cs = slice(lo, lo + 128)
            pb, pbk = ctx.pbf()
            e_tr(ctx, pb[:, 0:128], k1p[:, cs], ident[:, :],
                 reads=[("l0k1p",), ("ident",)], writes=[pbk])
            e_tr(ctx, pb[:, 128:256], vp[:, cs], ident[:, :],
                 reads=[("l0vp",), ("ident",)], writes=[pbk])
            e_copy(ctx, kt[:, :], pb[:, 0:128], reads=[pbk], writes=[("kt",)])
            e_copy(ctx, vt[:, :], pb[:, 128:256], reads=[pbk], writes=[("vt",)])
            psO, kO = ctx.pbank(5)
            HS = [slice(0, 64), slice(64, 128)]
            psA2 = []
            for hl in range(2):
                hsl = HS[hl]
                if c > 0:
                    e_act(ctx, s1sb[hsl, :], s1f[hsl, :], AF.Identity,
                          reads=[("s1f",)], writes=[("s1sb", hl)])
                psA, kA = ctx.pfree()
                e_mm(ctx, psA[:, 0:128], k1p[hsl, cs], qp[hsl, cs],
                     start=True, stop=True, reads=[("l0k1p",), ("l0qp",)],
                     writes=[kA])
                psA2.append((psA, kA))
            for hl in range(2):
                e_tt(ctx, ALU.mult, asb[:, hl, :], psA2[hl][0][:, 0:128],
                     mask1[:, :], reads=[psA2[hl][1], ("mask",)],
                     writes=[("asb", hl)])
            for hl in range(2):
                hb = 64 * hl
                hsl = HS[hl]
                e_mm(ctx, psO[hsl, 0:128], vt[:, hb:hb + 64], asb[:, hl, :],
                     start=True, stop=(c == 0),
                     reads=[("vt",), ("asb", hl)], writes=[kO])
                if c > 0:
                    e_mm(ctx, psO[hsl, 0:128], s1sb[hsl, 0:64], qp[hsl, cs],
                         start=False, stop=True,
                         reads=[("s1sb", hl), ("l0qp",)], writes=[kO])
            for hl in range(2):
                hb = 64 * hl
                hsl = HS[hl]
                psS1, kS1 = stb[hl]
                e_mm(ctx, psS1[hsl, 0:64], kt[:, hb:hb + 64], vt[:, hb:hb + 64],
                     start=True, stop=True,
                     reads=[("kt",), ("vt",)], writes=[kS1])
            for hl in range(2):
                hsl = HS[hl]
                psS1, kS1 = stb[hl]
                e_tt(ctx, ALU.add, s1f[hsl, :], s1f[hsl, :], psS1[hsl, 0:64],
                     reads=[("s1f",), kS1], writes=[("s1f",)])
            e_act(ctx, o1p[:, cs], psO[:, 0:128], AF.Identity,
                  reads=[kO], writes=[("o1p",)])

        st2 = [ctx.pbank(3), ctx.pbank(4)]
        e_memset(ctx, s2f[:, :], 0.0, writes=[("s2f",)])
        cc = pr
        for c in range(32):
            lo = c * 128
            hi = min(lo + 128, T - 1)
            n = hi - lo
            ns = slice(lo, lo + n)
            pb, pbk = ctx.pbf()
            e_tr(ctx, pb[0:n, 0:128], k2p[:, ns], ident[:, :],
                 reads=[("l0k2p",), ("ident",)], writes=[pbk])
            e_tr(ctx, pb[0:n, 128:256], xn[:, cc, 2 + lo:2 + lo + n],
                 ident[:, :], reads=XN + [("ident",)], writes=[pbk])
            e_tr(ctx, pb[0:n, 256:384], o1p[:, ns], ident[:, :],
                 reads=[("o1p",), ("ident",)], writes=[pbk])
            e_copy(ctx, k2t[0:n, :], pb[0:n, 0:128], reads=[pbk],
                   writes=[("k2t",)])
            e_copy(ctx, xnt[0:n, :], pb[0:n, 128:256], reads=[pbk],
                   writes=[("xnt",)])
            e_tt(ctx, ALU.subtract, et[0:n, :], xnt[0:n, :],
                 pb[0:n, 256:384], reads=[("xnt",), pbk], writes=[("et",)])
            psO, kO = ctx.pbank(5)
            for hl in range(2):
                hb = 64 * hl
                hsl = slice(hb, hb + 64)
                psS2, kS2 = st2[hl]
                if c > 0:
                    e_act(ctx, s2sb[hsl, :], s2f[hsl, :], AF.Identity,
                          scale=0.125, reads=[("s2f",)], writes=[("s2sb", hl)])
                psA, kA = ctx.pfree()
                e_mm(ctx, psA[0:n, 0:n], k2p[hsl, ns], q2p[hsl, ns],
                     start=True, stop=True, reads=[("l0k2p",), ("l0q2p",)],
                     writes=[kA])
                e_tt(ctx, ALU.mult, asb[0:n, hl, 0:n], psA[0:n, 0:n],
                     maskq2[0:n, 0:n], reads=[kA, ("mask",)],
                     writes=[("asb", hl)])
                e_mm(ctx, psO[hsl, 0:n], et[0:n, hb:hb + 64], asb[0:n, hl, 0:n],
                     start=True, stop=(c == 0),
                     reads=[("et",), ("asb", hl)], writes=[kO])
                if c > 0:
                    e_mm(ctx, psO[hsl, 0:n], s2sb[hsl, 0:64], q2p[hsl, ns],
                         start=False, stop=True,
                         reads=[("s2sb", hl), ("l0q2p",)], writes=[kO])
                e_mm(ctx, psS2[hsl, 0:64], k2t[0:n, hb:hb + 64],
                     et[0:n, hb:hb + 64], start=True, stop=True,
                     reads=[("k2t",), ("et",)], writes=[kS2])
                e_tt(ctx, ALU.add, s2f[hsl, :], s2f[hsl, :], psS2[hsl, 0:64],
                     reads=[("s2f",), kS2], writes=[("s2f",)])
            e_tt(ctx, ALU.add, y[:, cc, lo + 1:lo + 1 + n], psO[:, 0:n],
                 o1p[:, lo + 1:lo + 1 + n], reads=[kO, ("o1p",)],
                 writes=[("y", cc)])
        e_copy(ctx, y[:, cc, 0:1], o1p[:, 0:1], reads=[("o1p",)],
               writes=[("y", cc)])
        if DBG and pr == 0:
            e_dma(ctx, "sync", aps["dbg_q"], qp[:, :], reads=[("l0qp",)])
            e_dma(ctx, "sync", aps["dbg_k1"], k1p[:, :], reads=[("l0k1p",)])
            e_dma(ctx, "sync", aps["dbg_v"], vp[:, :], reads=[("l0vp",)])
            e_dma(ctx, "sync", aps["dbg_k2"], k2p[:, :], reads=[("l0k2p",)])
            e_dma(ctx, "sync", aps["dbg_q2"], q2p[:, :], reads=[("l0q2p",)])
            e_dma(ctx, "sync", aps["dbg_o1"], o1p[:, :], reads=[("o1p",)])

    if DBG:
        e_dma(ctx, "sync", aps["dbg_y"], y[:, :, :],
              reads=[("y", c) for c in range(NCC)])
    wpool = WPool(ctx, "l0wp", 1, [128, NCC, 512])
    stage = bufs["stage"]
    for ob in range(2):
        wb, wkey = wpool.load(wcols_src(aps["Wp"], 0, D, ob * 512, ob * 512 + 512))
        for tt in range(NTT):
            ps, pk = ctx.pfree()
            for cc in range(NCC):
                e_mm(ctx, ps[:, :], y[:, cc, tt * 128:(tt + 1) * 128],
                     wb[:, cc, :], start=(cc == 0), stop=(cc == NCC - 1),
                     reads=[wkey, ("y", cc)], writes=[pk])
            skey = ("stage", tt % 2, ob)
            e_tt(ctx, ALU.add, stage[tt % 2][:, ob * 512:(ob + 1) * 512],
                 ps[:, :], bufs["bp_rep"][:, ob * 512:(ob + 1) * 512],
                 reads=[pk, ("bp_rep",)], writes=[skey])
            e_dma(ctx, "scalar",
                  aps["acc_a"][tt * 128:(tt + 1) * 128, ob * 512:(ob + 1) * 512],
                  stage[tt % 2][:, ob * 512:(ob + 1) * 512], reads=[skey],
                  writes=[("acc_a", tt)])


def emit_l0_ffn(ctx, aps, xn, xnkey, bufs):
    XN = xn_all(xnkey)
    xr = ctx.sb("l0_xr", [128, NCC, 512], BF16)
    xk = ctx.sb("l0_xk", [128, NCC, 512], BF16)
    tmp = ctx.sb("l0_tmp", [128, 512], F32)
    kact = ctx.sb("l0_kact", [128, NCC, 512], BF16)
    krel = ctx.sb("l0_krel", [128, 512], BF16)
    rtm = ctx.sb("l0_rtm", [128, 4, 1024], BF16)
    stage = bufs["stage"]

    wr = [load_wcols(ctx, f"l0wr{ob}", aps["WrT"], 0, D, ob * 512, ob * 512 + 512)
          for ob in range(2)]
    wk = [load_wcols(ctx, f"l0wk{ob}", aps["WkT"], 0, D, ob * 512, ob * 512 + 512)
          for ob in range(2)]
    wv = [load_wcols(ctx, f"l0wv{ob}", aps["WvT"], 0, D, ob * 512, ob * 512 + 512)
          for ob in range(2)]

    for tt in range(NT5):
        t0 = tt * 512
        for cc in range(NCC):
            cur = xn[:, cc, 1 + t0:1 + t0 + 512]
            prev = xn[:, cc, t0:t0 + 512]
            e_tt(ctx, ALU.subtract, tmp[:, :], cur, prev,
                 reads=XN, writes=[("l0tmp",)])
            e_stt(ctx, xr[:, cc, :], tmp[:, :], bufs["mu_r"][:, cc:cc + 1],
                  prev, ALU.mult, ALU.add,
                  reads=[("l0tmp",), ("mu",)] + XN, writes=[("l0xr", cc)])
            e_stt(ctx, xk[:, cc, :], tmp[:, :], bufs["mu_k"][:, cc:cc + 1],
                  prev, ALU.mult, ALU.add,
                  reads=[("l0tmp",), ("mu",)] + XN, writes=[("l0xk", cc)])
        if DBG and tt == 0:
            e_dma(ctx, "sync", aps["dbg_xr"], xr[:, :, :],
                  reads=[("l0xr", c) for c in range(NCC)])
            e_dma(ctx, "sync", aps["dbg_xk"], xk[:, :, :],
                  reads=[("l0xk", c) for c in range(NCC)])
        for t4 in range(4):
            tsl = slice(t4 * 128, t4 * 128 + 128)
            for ob in range(2):
                ps, pk = ctx.pfree()
                for cc in range(NCC):
                    e_mm(ctx, ps[:, :], xr[:, cc, tsl], wr[ob][0][:, cc, :],
                         start=(cc == 0), stop=(cc == NCC - 1),
                         reads=[("l0xr", cc), wr[ob][1]], writes=[pk])
                e_act(ctx, rtm[:, t4, ob * 512:(ob + 1) * 512], ps[:, :],
                      AF.Sigmoid, reads=[pk], writes=[("rtm", t4)])
        for ko in range(NCC):
            ps, pk = ctx.pfree()
            for cc in range(NCC):
                e_mm(ctx, ps[:, :], wk[ko // 4][0][:, cc,
                                                   (ko % 4) * 128:(ko % 4) * 128 + 128],
                     xk[:, cc, :], start=(cc == 0), stop=(cc == NCC - 1),
                     reads=[("l0xk", cc), wk[ko // 4][1]], writes=[pk])
            e_act(ctx, krel[:, :], ps[:, :], AF.Relu, reads=[pk],
                  writes=[("krel",)])
            e_act(ctx, kact[:, ko, :], krel[:, :], AF.Square,
                  reads=[("krel",)], writes=[("l0kact", ko)])
        if DBG and tt == 0:
            e_dma(ctx, "sync", aps["dbg_kact"], kact[:, :, :],
                  reads=[("l0kact", c) for c in range(NCC)])
            e_dma(ctx, "sync", aps["dbg_rtm"], rtm[:, :, :],
                  reads=[("rtm", t) for t in range(4)])
        for t4 in range(4):
            tsl = slice(t4 * 128, t4 * 128 + 128)
            for ob in range(2):
                ps, pk = ctx.pfree()
                for ko in range(NCC):
                    e_mm(ctx, ps[:, :], kact[:, ko, tsl], wv[ob][0][:, ko, :],
                         start=(ko == 0), stop=(ko == NCC - 1),
                         reads=[("l0kact", ko), wv[ob][1]], writes=[pk])
                skey = ("stage", t4 % 2, ob)
                e_tt(ctx, ALU.mult, stage[t4 % 2][:, ob * 512:(ob + 1) * 512],
                     ps[:, :], rtm[:, t4, ob * 512:(ob + 1) * 512],
                     reads=[pk, ("rtm", t4)], writes=[skey])
            e_dma(ctx, "scalar",
                  aps["acc_f"][t0 + t4 * 128:t0 + t4 * 128 + 128, :],
                  stage[t4 % 2][:, :],
                  reads=[("stage", t4 % 2, 0), ("stage", t4 % 2, 1)],
                  writes=[("acc_f", tt * 4 + t4)])



RS512 = float(1.0 / np.sqrt(512.0))


def emit_l1_attn(ctx, aps, xn, xnkey, bufs):
    XN = xn_all(xnkey)
    onesb = bufs["ones_bf"]
    onesf = bufs["ones_f32"]
    dmask = bufs["diagmask"]

    qf = ctx.sb("l1_qf", [128, 4, T], BF16)
    kf = ctx.sb("l1_kf", [128, 4, T], BF16)
    vtm = ctx.sb("l1_vtm", [128, NTT, 520], BF16)
    nug = ctx.sb("l1_nug", [1, T], BF16)
    kms = ctx.sb("l1_kms", [1, 4], F32)
    stage = bufs["stage"]

    with ExitStack() as wst:
        wctx = sub_ctx(ctx, wst)
        sq = wctx.sb("l1_sq", [128, 512], BF16)
        rowA = wctx.sb("l1_rowA", [1, T], BF16)
        rowB = nug
        wpool = WPool(wctx, "l1wq", 1, [128, NCC, 256])
        for kind, dst, dkey, scl, base in (("q", qf, ("l1qf",), RS512, 0),
                                           ("k", kf, ("l1kf",), 1.0, 512)):
            for qc in range(4):
                wb, wkey = wpool.load(
                    wcols_src(aps["wqkvT"], 0, D,
                              base + qc * 128, base + qc * 128 + 128))
                for tt in range(NT5):
                    ps, pk = ctx.pfree()
                    for cc in range(NCC):
                        e_mm(ctx, ps[:, :], wb[:, cc, 0:128],
                             xn[:, cc, 1 + tt * 512:1 + tt * 512 + 512],
                             start=(cc == 0), stop=(cc == NCC - 1),
                             reads=[wkey] + XN, writes=[pk])
                    e_act(ctx, dst[:, qc, tt * 512:(tt + 1) * 512], ps[:, :],
                          AF.Identity, scale=scl, reads=[pk], writes=[dkey])
        for vh in range(2):
            wv, wvk = wpool.load(wcols_src(aps["wqkvT"], 0, D,
                                           1024 + vh * 256, 1024 + vh * 256 + 256))
            for t3 in range(NTT):
                ps, pk = ctx.pfree()
                for cc in range(NCC):
                    e_mm(ctx, ps[:, 0:256],
                         xn[:, cc, 1 + t3 * 128:1 + t3 * 128 + 128],
                         wv[:, cc, 0:256], start=(cc == 0),
                         stop=(cc == NCC - 1), reads=[wvk] + XN, writes=[pk])
                e_act(ctx, vtm[:, t3, vh * 256:vh * 256 + 256], ps[:, 0:256],
                      AF.Identity, reads=[pk], writes=[("l1vtm",)])
        e_memset(ctx, vtm[:, :, 512:513], 1.0, writes=[("l1vtm",)])
        def norm2_row(srcf, srckey, dst, dkey):
            for tt in range(NT5):
                ps, pk = ctx.pfree()
                for qc in range(4):
                    e_act(ctx, sq[:, :], srcf[:, qc, tt * 512:(tt + 1) * 512],
                          AF.Square, reads=[srckey], writes=[("l1sq",)])
                    e_mm(ctx, ps[0:1, 0:512], onesb[:, 0:1], sq[:, :],
                         start=(qc == 0), stop=(qc == 3),
                         reads=[("l1sq",), ("ones",)], writes=[pk])
                e_copy(ctx, dst[0:1, tt * 512:(tt + 1) * 512], ps[0:1, 0:512],
                       reads=[pk], writes=[dkey])

        norm2_row(kf, ("l1kf",), rowA, ("l1rowA",))
        e_red(ctx, kms[0:1, 0:1], rowA[0:1, :], ALU.max, reads=[("l1rowA",)],
              writes=[("l1kms",)])
        e_act(ctx, kms[0:1, 1:2], kms[0:1, 0:1], AF.Ln, reads=[("l1kms",)],
              writes=[("l1kms",)])
        e_act(ctx, kms[0:1, 2:3], kms[0:1, 1:2], AF.Exp, scale=0.5,
              reads=[("l1kms",)], writes=[("l1kms",)])
        e_ts(ctx, kms[0:1, 3:4], kms[0:1, 2:3], -1.0, ALU.mult,
             reads=[("l1kms",)], writes=[("l1kms",)])
        norm2_row(qf, ("l1qf",), rowA, ("l1rowA",))
        e_act(ctx, rowB[0:1, :], rowA[0:1, :], AF.Ln, reads=[("l1rowA",)],
              writes=[("l1nug",)])
        e_act(ctx, rowA[0:1, :], rowB[0:1, :], AF.Exp, scale=0.5,
              reads=[("l1nug",)], writes=[("l1rowA",)])
        e_ts(ctx, nug[0:1, :], rowA[0:1, :], kms[0:1, 3:4], ALU.mult,
             reads=[("l1rowA",), ("l1kms",)], writes=[("l1nug",)])

    ctx.s.global_barrier()

    lsb = ctx.sb("l1_lsb", [1, 512], F32)
    linv = ctx.sb("l1_linv", [1, 512], F32)
    rep = ctx.sb("l1_rep", [128, 512], F32)
    pt = [ctx.sb(f"l1_pt{i}", [128, 512], BF16) for i in range(2)]
    osb = ctx.sb("l1_osb", [128, 4, 512], BF16)
    wo = [load_wcols(ctx, f"l1wo{ob}", aps["w_outT"], 0, 512,
                     ob * 512, ob * 512 + 512) for ob in range(2)]

    for qg in range(8):
        kcs = 4 * (qg + 1)
        tqs = slice(qg * 512, (qg + 1) * 512)
        e_memset(ctx, lsb[0:1, :], 0.0, writes=[("l1lsb",)])
        for kci in range(kcs):
            Sb, Sk = ctx.pbank(4 + kci % 2)
            for dc in range(4):
                e_mm(ctx, Sb[:, :], kf[:, dc, kci * 128:(kci + 1) * 128],
                     qf[:, dc, tqs], start=(dc == 0), stop=False,
                     reads=[("l1kf",), ("l1qf",)], writes=[Sk])
            e_mm(ctx, Sb[:, :], onesb[0:1, 0:128], nug[0:1, tqs],
                 start=False, stop=True,
                 reads=[("ones",), ("l1nug",)], writes=[Sk])
            ptb = pt[kci % 2]
            ptk = ("l1pt", kci % 2)
            e_act(ctx, ptb[:, :], Sb[:, :], AF.Exp, reads=[Sk], writes=[ptk])
            if kci >= kcs - 4:
                e_tt(ctx, ALU.mult, ptb[:, :], ptb[:, :],
                     dmask[:, kci - (kcs - 4), :], reads=[ptk, ("dmask",)],
                     writes=[ptk])
            for oc in range(4):
                po, pok = ctx.pbank(oc)
                e_mm(ctx, po[:, :], vtm[:, kci, oc * 128:(oc + 1) * 128],
                     ptb[:, :], start=(kci == 0), stop=(kci == kcs - 1),
                     reads=[("l1vtm",), ptk], writes=[pok])
            e_mm(ctx, Sb[0:1, 0:512], onesb[:, 0:1], ptb[:, :],
                 start=True, stop=True, reads=[("ones",), ptk], writes=[Sk])
            e_tt(ctx, ALU.add, lsb[0:1, :], lsb[0:1, :], Sb[0:1, 0:512],
                 reads=[("l1lsb",), Sk], writes=[("l1lsb",)])
        e_act(ctx, linv[0:1, :], lsb[0:1, :], AF.Ln, reads=[("l1lsb",)],
              writes=[("l1linv",)])
        e_act(ctx, linv[0:1, :], linv[0:1, :], AF.Exp, scale=-1.0,
              reads=[("l1linv",)], writes=[("l1linv",)])
        pr_, prk = ctx.pbank(4)
        e_mm(ctx, pr_[:, :], onesf[0:1, 0:128], linv[0:1, :],
             start=True, stop=True, reads=[("ones",), ("l1linv",)],
             writes=[prk])
        e_copy(ctx, rep[:, :], pr_[:, :], reads=[prk], writes=[("l1rep",)])
        for oc in range(4):
            po, pok = ctx.pbank(oc)
            e_tt(ctx, ALU.mult, osb[:, oc, :], po[:, :], rep[:, :],
                 reads=[pok, ("l1rep",)], writes=[("l1osb",)])
        for t4 in range(4):
            for ob in range(2):
                ps, pk = ctx.pbank(4 + ob)
                for oc in range(4):
                    e_mm(ctx, ps[:, :], osb[:, oc, t4 * 128:(t4 + 1) * 128],
                         wo[ob][0][:, oc, :], start=(oc == 0), stop=(oc == 3),
                         reads=[("l1osb",), wo[ob][1]], writes=[pk])
                skey = ("stage", t4 % 2, ob)
                e_tt(ctx, ALU.add, stage[t4 % 2][:, ob * 512:(ob + 1) * 512],
                     ps[:, :], bufs["bo_rep"][:, ob * 512:(ob + 1) * 512],
                     reads=[pk, ("borep",)], writes=[skey])
            tt = qg * 4 + t4
            e_dma(ctx, "scalar", aps["acc_a"][tt * 128:(tt + 1) * 128, :],
                  stage[t4 % 2][:, :],
                  reads=[("stage", t4 % 2, 0), ("stage", t4 % 2, 1)],
                  writes=[("acc_a", tt)])


def emit_l1_ffn(ctx, aps, xn, xnkey, bufs):
    XN = xn_all(xnkey)
    row1 = bufs["row1"]
    stage = bufs["stage"]

    gwb, gwk = load_wcols(ctx, "l1gw", aps["gwT"], 0, D, 0, 8)
    gbr = ctx.sb("l1_gbr", [128, 8], F32)
    e_dma(ctx, "gpsimd", gbr[:, :], aps["gb_rep"], writes=[("l1gbr",)])
    rw = ctx.sb("l1_rw", [128, NTT, 8], F32)
    e8 = ctx.sb("l1_e8", [128, 8], F32)
    sc8 = ctx.sb("l1_sc8", [128, 8], F32)
    bi8 = ctx.sb("l1_bi8", [128, 8], F32)
    g8 = ctx.sb("l1_g8", [128, 8], F32)
    r1 = ctx.sb("l1_r1", [128, 4], F32)

    for t3 in range(NTT):
        ps, pk = ctx.pfree()
        for cc in range(NCC):
            e_mm(ctx, ps[:, 0:8], xn[:, cc, 1 + t3 * 128:1 + t3 * 128 + 128],
                 gwb[:, cc, :], start=(cc == 0), stop=(cc == NCC - 1),
                 reads=[gwk] + XN, writes=[pk])
        e_red(ctx, r1[:, 0:1], ps[:, 0:8], ALU.max, negate=True,
              reads=[pk], writes=[("l1r1",)])
        e_act(ctx, e8[:, :], ps[:, 0:8], AF.Exp, bias=r1[:, 0:1],
              reads=[pk, ("l1r1",)], writes=[("l1e8",)])
        e_red(ctx, r1[:, 1:2], e8[:, :], ALU.add, reads=[("l1e8",)],
              writes=[("l1r1",)])
        ctx.s.emit("vector", lambda e, o=r1[:, 2:3], i=r1[:, 1:2]:
                   e.reciprocal(o, i), reads=[("l1r1",)], writes=[("l1r1",)])
        e_ts(ctx, sc8[:, :], e8[:, :], r1[:, 2:3], ALU.mult,
             reads=[("l1e8",), ("l1r1",)], writes=[("l1sc8",)])
        e_tt(ctx, ALU.add, bi8[:, :], sc8[:, :], gbr[:, :],
             reads=[("l1sc8",), ("l1gbr",)], writes=[("l1bi8",)])
        for e in range(8):
            e_ts(ctx, g8[:, :], bi8[:, :], bi8[:, e:e + 1], ALU.is_gt,
                 reads=[("l1bi8",)], writes=[("l1g8",)])
            e_red(ctx, r1[:, 3:4], g8[:, :], ALU.add, reads=[("l1g8",)],
                  writes=[("l1r1",)])
            e_ts(ctx, r1[:, 3:4], r1[:, 3:4], 2.0, ALU.is_lt,
                 reads=[("l1r1",)], writes=[("l1r1",)])
            e_ts(ctx, rw[:, t3, e:e + 1], sc8[:, e:e + 1], r1[:, 3:4],
                 ALU.mult, reads=[("l1sc8",), ("l1r1",)], writes=[("l1rw",)])

    w1b = ctx.sb("l1_w1b", [128, 9, 1408], BF16)
    w3b = ctx.sb("l1_w3b", [128, 9, 1408], BF16)
    w2b = ctx.sb("l1_w2b", [128, 12, 1024], BF16)
    hmid = ctx.sb("l1_hmid", [128, 12, 512], BF16)
    usb = ctx.sb("l1_usb", [128, 512], BF16)
    tmp = ctx.sb("l1_tmp", [128, 512], F32)
    e_memset(ctx, hmid[:, 11, :], 0.0, writes=[("l1hm", 11)])
    e_memset(ctx, hmid[0:1, 11, :], 1.0, writes=[("l1hm", 11)])

    for e in range(9):
        s1 = aps["w1a"][e * 1152:(e + 1) * 1152, :]
        s3 = aps["w3a"][e * 1152:(e + 1) * 1152, :]
        s2 = aps["w2a"][e * 1536:(e + 1) * 1536, :]
        e_dma(ctx, "gpsimd", w1b[:, :, :],
              s1.rearrange("(cc p) o -> p cc o", p=128), writes=[("l1w1b",)])
        e_dma(ctx, "gpsimd", w3b[:, :, :],
              s3.rearrange("(cc p) o -> p cc o", p=128), writes=[("l1w3b",)])
        e_dma(ctx, "gpsimd", w2b[:, :, :],
              s2.rearrange("(cc p) o -> p cc o", p=128), writes=[("l1w2b",)])
        for tt in range(NT5):
            t0 = tt * 512
            for hc in range(11):
                psU, pkU = ctx.pbank(tt % 2)
                psV, pkV = ctx.pbank(2 + tt % 2)
                for cc in range(9):
                    rhs = (xn[:, cc, 1 + t0:1 + t0 + 512] if cc < 8
                           else row1[:, :])
                    rds = (XN if cc < 8 else [("row1",)])
                    e_mm(ctx, psU[:, :], w1b[:, cc, hc * 128:(hc + 1) * 128],
                         rhs, start=(cc == 0), stop=(cc == 8),
                         reads=[("l1w1b",)] + rds, writes=[pkU])
                e_act(ctx, usb[:, :], psU[:, :], AF.Identity, reads=[pkU],
                      writes=[("l1usb",)])
                for cc in range(9):
                    rhs = (xn[:, cc, 1 + t0:1 + t0 + 512] if cc < 8
                           else row1[:, :])
                    rds = (XN if cc < 8 else [("row1",)])
                    e_mm(ctx, psV[:, :], w3b[:, cc, hc * 128:(hc + 1) * 128],
                         rhs, start=(cc == 0), stop=(cc == 8),
                         reads=[("l1w3b",)] + rds, writes=[pkV])
                e_tt(ctx, ALU.mult, tmp[:, :], psV[:, :], usb[:, :],
                     reads=[pkV, ("l1usb",)], writes=[("l1tmp",)])
                e_act(ctx, hmid[:, hc, :], tmp[:, :], AF.Silu,
                      reads=[("l1tmp",)], writes=[("l1hm", hc)])
            for t4 in range(4):
                t3 = tt * 4 + t4
                for ob in range(2):
                    ps, pk = ctx.pbank(4 + ob)
                    for hc in range(12):
                        e_mm(ctx, ps[:, :],
                             hmid[:, hc, t4 * 128:(t4 + 1) * 128],
                             w2b[:, hc, ob * 512:(ob + 1) * 512],
                             start=(hc == 0), stop=(hc == 11),
                             reads=[("l1w2b",), ("l1hm", hc)], writes=[pk])
                    skey = ("stage", t4 % 2, ob)
                    if e < 8:
                        e_ts(ctx, stage[t4 % 2][:, ob * 512:(ob + 1) * 512],
                             ps[:, :], rw[:, t3, e:e + 1], ALU.mult,
                             reads=[pk, ("l1rw",)], writes=[skey])
                    else:
                        e_copy(ctx, stage[t4 % 2][:, ob * 512:(ob + 1) * 512],
                               ps[:, :], reads=[pk], writes=[skey])
                e_dma(ctx, "gpsimd",
                      aps["acc_f"][t3 * 128:(t3 + 1) * 128, :],
                      stage[t4 % 2][:, :],
                      reads=[("stage", t4 % 2, 0), ("stage", t4 % 2, 1)],
                      writes=[("acc_f", t3)], accum=(e > 0))



def emit_l2_attn(ctx, aps, xn, xnkey, bufs):
    XN = xn_all(xnkey)
    onesb = bufs["ones_bf"]
    onesf = bufs["ones_f32"]
    nmask = bufs["negmask"]
    stage = bufs["stage"]

    with ExitStack() as st:
        cctx = sub_ctx(ctx, st)
        fa = cctx.sb("l2_fa", [16, T], F32)
        fb = cctx.sb("l2_fb", [16, T], F32)
        zr = cctx.sb("l2_zr", [16, 512], F32)
        csp = cctx.sb("l2_csp", [16, T], BF16)
        wf, wfk = load_wcols(cctx, "l2wf", aps["Wf"], 0, D, 0, 16)
        e_memset(cctx, zr[:, :], 0.0, writes=[("l2zr",)])
        for tt in range(NT5):
            ps, pk = ctx.pfree()
            for cc in range(NCC):
                e_mm(ctx, ps[0:16, :], wf[:, cc, :],
                     xn[:, cc, 1 + tt * 512:1 + tt * 512 + 512],
                     start=(cc == 0), stop=(cc == NCC - 1),
                     reads=[wfk] + XN, writes=[pk])
            e_copy(ctx, fa[:, tt * 512:(tt + 1) * 512], ps[0:16, :],
                   reads=[pk], writes=[("l2fa",)])
        e_act(ctx, fb[:, :], fa[:, :], AF.Abs, reads=[("l2fa",)],
              writes=[("l2fb",)])
        e_act(ctx, fb[:, :], fb[:, :], AF.Exp, scale=-1.0, reads=[("l2fb",)],
              writes=[("l2fb",)])
        e_act(ctx, fb[:, :], fb[:, :], AF.Ln, bias=1.0, reads=[("l2fb",)],
              writes=[("l2fb",)])
        e_ts(ctx, fa[:, :], fa[:, :], 0.0, ALU.min, reads=[("l2fa",)],
             writes=[("l2fa",)])
        e_tt(ctx, ALU.subtract, fa[:, :], fa[:, :], fb[:, :],
             reads=[("l2fa",), ("l2fb",)], writes=[("l2fa",)])
        for tt in range(NT5):
            sl = slice(tt * 512, (tt + 1) * 512)
            init = 0.0 if tt == 0 else fb[:, tt * 512 - 1:tt * 512]
            e_scan(ctx, fb[:, sl], fa[:, sl], zr[:, :], init,
                   ALU.add, ALU.add, reads=[("l2fa",), ("l2zr",), ("l2fb",)],
                   writes=[("l2fb",)])
        e_dma(ctx, "scalar", bufs["l2_c"][:, :], fb[:, :], reads=[("l2fb",)],
              writes=[("l2c",)])
        e_ts(ctx, fa[:, :], fb[:, :], -1.0, ALU.mult, reads=[("l2fb",)],
             writes=[("l2fa",)])
        for s_ in range(3):
            e_copy(ctx, csp[:, :], fa[:, :], reads=[("l2fa",)],
                   writes=[("l2csp",)])
            e_dma(ctx, "scalar", bufs["l2_csp"][:, s_ * T:(s_ + 1) * T],
                  csp[:, :], reads=[("l2csp",)], writes=[("l2cspP",)])
            if s_ < 2:
                e_tt(ctx, ALU.subtract, fa[:, :], fa[:, :], csp[:, :],
                     reads=[("l2fa",), ("l2csp",)], writes=[("l2fa",)])
    ctx.s.global_barrier()
    cfull = bufs["l2_c"]
    cspP = bufs["l2_csp"]
    kaug = ctx.sb("l2_kaug", [64, T], BF16)
    qaug = ctx.sb("l2_qaug", [64, 2048], BF16)
    kaux3 = ctx.sb("l2_kaux3", [3, T], BF16)
    ones4 = ctx.sb("l2_ones4", [4, 2048], BF16)
    e_memset(ctx, ones4[:, :], 1.0, writes=[("l2ones4",)])
    vtm = ctx.sb("l2_vtm", [128, NTT, 72], BF16)
    negu = ctx.sb("l2_negu", [1, 2048], BF16)
    rowA = ctx.sb("l2_rowA", [1, T], BF16)
    crow0 = ctx.sb("l2_crow0", [1, T], BF16)
    sq = ctx.sb("l2_sq", [128, 512], BF16)
    kms = ctx.sb("l2_kms", [1, 4], F32)
    linv = [ctx.sb("l2_linv", [128, 512], F32) for _ in range(2)]
    rep = [ctx.sb("l2_rep", [64, 512], F32) for _ in range(2)]
    osb = [ctx.sb("l2_osb", [64, 512], BF16) for _ in range(2)]
    pt = [ctx.sb(f"l2_pt{i}", [128, 512], BF16) for i in range(4)]
    outsb = ctx.sb("l2_outsb", [128, 16, D], BF16)
    wo_sb = ctx.sb("l2_wo", [64, D], BF16)
    wh = WPool(ctx, "l2wh", 2, [128, NCC, 192])

    for sweep in range(2):
        kr = 2048 * (sweep + 1)
        q0 = 2048 * sweep
        e_memset(ctx, outsb[:, :, :], 0.0, writes=[("l2out",)])
        for h in range(16):
            wb, wk = wh.load(wcols_src(aps["Wcat"], 0, D,
                                       192 * h, 192 * h + 192))
            e_dma(ctx, "gpsimd", wo_sb[:, :],
                  aps["wo_head"][:, h * D:(h + 1) * D], writes=[("l2wo",)])
            for tt in range(kr // 512):
                ps, pk = ctx.pfree()
                for cc in range(NCC):
                    e_mm(ctx, ps[0:64, :], wb[:, cc, 64:128],
                         xn[:, cc, 1 + tt * 512:1 + tt * 512 + 512],
                         start=(cc == 0), stop=(cc == NCC - 1),
                         reads=[wk] + XN, writes=[pk])
                e_act(ctx, kaug[:, tt * 512:(tt + 1) * 512], ps[0:64, :],
                      AF.Identity, reads=[pk], writes=[("l2k",)])
            e_dma(ctx, "sync", kaux3[0:3, 0:T],
                  cspP[h:h + 1, :].rearrange("h (s t) -> (h s) t", s=3),
                  reads=[("l2cspP",)], writes=[("l2kaux",)])
            e_dma(ctx, "gpsimd", crow0[0:1, :], cfull[h:h + 1, :],
                  reads=[("l2c",)], writes=[("l2crow0",)])
            for tt in range(4):
                ps, pk = ctx.pfree()
                for cc in range(NCC):
                    e_mm(ctx, ps[0:64, :], wb[:, cc, 0:64],
                         xn[:, cc, 1 + q0 + tt * 512:1 + q0 + tt * 512 + 512],
                         start=(cc == 0), stop=(cc == NCC - 1),
                         reads=[wk] + XN, writes=[pk])
                e_act(ctx, qaug[:, tt * 512:(tt + 1) * 512], ps[0:64, :],
                      AF.Identity, scale=0.125, reads=[pk], writes=[("l2q",)])

            for t3 in range(kr // 128):
                ps, pk = ctx.pfree()
                for cc in range(NCC):
                    e_mm(ctx, ps[:, 0:64],
                         xn[:, cc, 1 + t3 * 128:1 + t3 * 128 + 128],
                         wb[:, cc, 128:192], start=(cc == 0),
                         stop=(cc == NCC - 1), reads=[wk] + XN, writes=[pk])
                e_act(ctx, vtm[:, t3, 0:64], ps[:, 0:64], AF.Identity,
                      reads=[pk], writes=[("l2v",)])
            e_memset(ctx, vtm[:, 0:kr // 128, 64:65], 1.0, writes=[("l2v",)])
            for tt in range(kr // 512):
                ps, pk = ctx.pfree()
                e_act(ctx, sq[0:64, :], kaug[:, tt * 512:(tt + 1) * 512],
                      AF.Square, reads=[("l2k",)], writes=[("l2sq",)])
                e_mm(ctx, ps[0:1, 0:512], onesb[0:64, 0:1], sq[0:64, :],
                     start=True, stop=True, reads=[("l2sq",), ("ones",)],
                     writes=[pk])
                e_copy(ctx, rowA[0:1, tt * 512:(tt + 1) * 512],
                       ps[0:1, 0:512], reads=[pk], writes=[("l2rowA",)])
            e_red(ctx, kms[0:1, 0:1], rowA[0:1, 0:kr], ALU.max,
                  reads=[("l2rowA",)], writes=[("l2kms",)])
            e_act(ctx, kms[0:1, 1:2], kms[0:1, 0:1], AF.Ln,
                  reads=[("l2kms",)], writes=[("l2kms",)])
            e_act(ctx, kms[0:1, 2:3], kms[0:1, 1:2], AF.Exp, scale=0.5,
                  reads=[("l2kms",)], writes=[("l2kms",)])
            e_ts(ctx, kms[0:1, 3:4], kms[0:1, 2:3], -1.0, ALU.mult,
                 reads=[("l2kms",)], writes=[("l2kms",)])
            for tt in range(4):
                ps, pk = ctx.pfree()
                e_act(ctx, sq[0:64, :], qaug[:, tt * 512:(tt + 1) * 512],
                      AF.Square, reads=[("l2q",)], writes=[("l2sq",)])
                e_mm(ctx, ps[0:1, 0:512], onesb[0:64, 0:1], sq[0:64, :],
                     start=True, stop=True, reads=[("l2sq",), ("ones",)],
                     writes=[pk])
                e_copy(ctx, rowA[0:1, tt * 512:(tt + 1) * 512],
                       ps[0:1, 0:512], reads=[pk], writes=[("l2rowA",)])
            e_act(ctx, negu[0:1, :], rowA[0:1, 0:2048], AF.Ln,
                  reads=[("l2rowA",)], writes=[("l2negu",)])
            e_act(ctx, rowA[0:1, 0:2048], negu[0:1, :], AF.Exp, scale=0.5,
                  reads=[("l2negu",)], writes=[("l2rowA",)])
            e_ts(ctx, rowA[0:1, 0:2048], rowA[0:1, 0:2048], kms[0:1, 3:4],
                 ALU.mult, reads=[("l2rowA",), ("l2kms",)],
                 writes=[("l2rowA",)])
            e_tt(ctx, ALU.add, negu[0:1, :], rowA[0:1, 0:2048],
                 crow0[0:1, q0:q0 + 2048], reads=[("l2rowA",), ("l2crow0",)],
                 writes=[("l2negu",)])

            for lg in range(4):
                g = sweep * 4 + lg
                sl2 = lg % 2
                kcs = 4 * (g + 1)
                tqs = slice(lg * 512, (lg + 1) * 512)
                psO, pkO = ctx.pbank(4 + sl2)
                for kci in range(kcs):
                    Sb, Sk = ctx.pbank(kci % 2)
                    e_mm(ctx, Sb[:, :], kaug[:, kci * 128:(kci + 1) * 128],
                         qaug[:, tqs], start=True, stop=False,
                         reads=[("l2k",), ("l2q",)], writes=[Sk])
                    e_mm(ctx, Sb[:, :], kaux3[:, kci * 128:(kci + 1) * 128],
                         ones4[0:3, tqs], start=False, stop=False,
                         reads=[("l2kaux",), ("l2ones4",)], writes=[Sk])
                    e_mm(ctx, Sb[:, :], onesb[0:1, 0:128], negu[0:1, tqs],
                         start=False, stop=True,
                         reads=[("ones",), ("l2negu",)], writes=[Sk])
                    if kci >= kcs - 4:
                        e_tt(ctx, ALU.add, Sb[:, :], Sb[:, :],
                             nmask[:, kci - (kcs - 4), :],
                             reads=[Sk, ("nmask",)], writes=[Sk])
                    ptb, ptk = pt[kci % 4], ("l2pt", kci % 4)
                    e_act(ctx, ptb[:, :], Sb[:, :], AF.Exp, reads=[Sk],
                          writes=[ptk])
                    e_mm(ctx, psO[0:65, :], vtm[:, kci, 0:65], ptb[:, :],
                         start=(kci == 0), stop=(kci == kcs - 1),
                         reads=[("l2v",), ptk], writes=[pkO])
                e_act(ctx, linv[sl2][64:65, :], psO[64:65, :], AF.Ln,
                      reads=[pkO], writes=[("l2linv", sl2)])
                e_act(ctx, linv[sl2][64:65, :], linv[sl2][64:65, :], AF.Exp,
                      scale=-1.0, reads=[("l2linv", sl2)],
                      writes=[("l2linv", sl2)])
                psR, pkR = ctx.pfree()
                e_mm(ctx, psR[0:64, :], onesf[64:65, 0:64],
                     linv[sl2][64:65, :], start=True, stop=True,
                     reads=[("ones",), ("l2linv", sl2)], writes=[pkR])
                e_copy(ctx, rep[sl2][:, :], psR[0:64, :], reads=[pkR],
                       writes=[("l2rep", sl2)])
                e_tt(ctx, ALU.mult, osb[sl2][:, :], psO[0:64, :],
                     rep[sl2][:, :], reads=[pkO, ("l2rep", sl2)],
                     writes=[("l2osb", sl2)])
                for t4 in range(4):
                    for ob in range(2):
                        psP, pkP = ctx.pbank(2 + ob)
                        e_mm(ctx, psP[:, :],
                             osb[sl2][:, t4 * 128:(t4 + 1) * 128],
                             wo_sb[:, ob * 512:(ob + 1) * 512],
                             start=True, stop=True,
                             reads=[("l2osb", sl2), ("l2wo",)], writes=[pkP])
                        e_tt(ctx, ALU.add,
                             outsb[:, lg * 4 + t4, ob * 512:(ob + 1) * 512],
                             outsb[:, lg * 4 + t4, ob * 512:(ob + 1) * 512],
                             psP[:, :], reads=[("l2out",), pkP],
                             writes=[("l2out",)])
        for t in range(16):
            t3 = sweep * 16 + t
            skey = ("stage", t % 2, 0)
            e_copy(ctx, stage[t % 2][:, :], outsb[:, t, :],
                   reads=[("l2out",)], writes=[skey])
            e_dma(ctx, "scalar", aps["acc_a"][t3 * 128:(t3 + 1) * 128, :],
                  stage[t % 2][:, :], reads=[skey], writes=[("acc_a", t3)])



def _l2f_shift_rhs(xn, cc, c0, n):
    if cc < 4:
        return xn[:, cc, 1 + c0:1 + c0 + n]
    return xn[:, cc, c0:c0 + n]


def emit_l2_ffn(ctx, aps, xn, xnkey, bufs):
    XN = xn_all(xnkey)
    onesb = bufs["ones_bf"]
    onesf = bufs["ones_f32"]
    ident = bufs["ident"]
    nm_low = bufs["nm_low"]
    nm_up = bufs["nm_up"]
    row1 = bufs["row1"]
    stage = bufs["stage"]

    for l in range(2):
        with ExitStack() as ast:
            actx = sub_ctx(ctx, ast)
            qf = actx.sb("l2f_qf", [65, T], BF16)
            xcop = actx.sb("l2f_xcop", [128, NCC, 1032], BF16)
            kf = actx.sb("l2f_kf", [65, T], BF16)
            vtm = actx.sb("l2f_vtm", [128, NTT, 72], BF16)
            outsb = actx.sb("l2f_out", [128, 16, D], BF16)
            wo_sb = actx.sb("l2f_wo", [64, 8, D], BF16)
            qraw = actx.sb("l2f_qraw", [64, 512], BF16)
            qn_t = actx.sb("l2f_qn", [64, 512], BF16)
            sqh = actx.sb("l2f_sqh", [64, 512], BF16)
            nrow = actx.sb("l2f_nrow", [1, 512], F32)
            tabs = [actx.sb(f"l2f_tab{i}", [64, 512], BF16) for i in range(2)]
            rotm = actx.sb("l2f_rotm", [64, 64], BF16)
            qsks = actx.sb("l2f_qsks", [64, 2], F32)
            negm = [actx.sb("l2f_negm", [128, 1], BF16) for _ in range(2)]
            amax = [actx.sb("l2f_amax", [128, 2], F32) for _ in range(2)]
            linv = [actx.sb("l2f_linv", [128, 128], F32) for _ in range(2)]
            rep = [actx.sb("l2f_rep", [64, 128], F32) for _ in range(2)]
            osb = [actx.sb("l2f_osb", [64, 128], BF16) for _ in range(2)]
            pt = [actx.sb("l2f_pt", [128, 128], BF16) for _ in range(2)]
            wh = WPool(actx, "l2f_wh", 2, [128, NCC, 192])
            e_dma(ctx, "gpsimd", rotm[:, :], aps["rotm"], writes=[("rotm",)])
            e_dma(ctx, "gpsimd", wo_sb[:, :, :],
                  aps["wo2_head"][:, l * 8192:(l + 1) * 8192]
                  .rearrange("p (h o) -> p h o", h=8), writes=[("l2fwo",)])

            def rhs_sh(cc, c0, n, sweep):
                a = c0 - (1 if cc >= 4 else 0)
                if sweep == 1 and a < 2048:
                    return xcop[:, cc, a - 1535:a - 1535 + n], [("l2fxcop",)]
                return _l2f_shift_rhs(xn, cc, c0, n), XN

            def qk_stage(kind, h, t0, t1):
                dst = qf if kind == "q" else kf
                dkey = ("l2fqf",) if kind == "q" else ("l2fkf",)
                wof = 0 if kind == "q" else 64
                tb0 = 0 if kind == "q" else 2
                for tt in range((t1 - t0) // 512):
                    c0 = t0 + tt * 512
                    ps, pk = ctx.pfree()
                    for cc in range(NCC):
                        rhs, rds = rhs_sh(cc, c0, 512, sweep)
                        e_mm(ctx, ps[0:64, :], wb[:, cc, wof:wof + 64],
                             rhs, start=(cc == 0), stop=(cc == NCC - 1),
                             reads=[wk] + rds, writes=[pk])
                    e_act(ctx, qraw[:, :], ps[0:64, :], AF.Identity,
                          reads=[pk], writes=[("l2fqraw",)])
                    e_act(ctx, sqh[:, :], qraw[:, :], AF.Square,
                          reads=[("l2fqraw",)], writes=[("l2fsqh",)])
                    ps2, pk2 = ctx.pfree()
                    e_mm(ctx, ps2[0:1, 0:512], onesb[0:64, 0:1], sqh[:, :],
                         start=True, stop=True,
                         reads=[("l2fsqh",), ("ones",)], writes=[pk2])
                    e_act(ctx, nrow[0:1, :], ps2[0:1, 0:512], AF.Ln,
                          bias=1e-24, reads=[pk2], writes=[("l2fnrow",)])
                    e_act(ctx, nrow[0:1, :], nrow[0:1, :], AF.Exp, scale=-0.5,
                          reads=[("l2fnrow",)], writes=[("l2fnrow",)])
                    if kind == "q":
                        e_ts(ctx, nrow[0:1, :], nrow[0:1, :], 8.0, ALU.mult,
                             reads=[("l2fnrow",)], writes=[("l2fnrow",)])
                    psr, pkr = ctx.pfree()
                    e_mm(ctx, psr[0:64, 0:512], onesf[0:1, 0:64],
                         nrow[0:1, :], start=True, stop=True,
                         reads=[("ones",), ("l2fnrow",)], writes=[pkr])
                    e_stt(ctx, qn_t[:, :], qraw[:, :], qsks[:, tb0 // 2:tb0 // 2 + 1],
                          psr[0:64, 0:512], ALU.mult, ALU.mult,
                          reads=[("l2fqraw",), ("l2fqsks",), pkr],
                          writes=[("l2fqn",)])
                    psrot, pkrot = ctx.pfree()
                    e_mm(ctx, psrot[0:64, 0:512], rotm[:, :], qn_t[:, :],
                         start=True, stop=True, reads=[("rotm",), ("l2fqn",)],
                         writes=[pkrot])
                    e_dma(ctx, "gpsimd", tabs[0][:, :],
                          aps["xtab"][tb0 * 64:tb0 * 64 + 64, c0:c0 + 512],
                          writes=[("l2ftab", 0)])
                    e_dma(ctx, "gpsimd", tabs[1][:, :],
                          aps["xtab"][(tb0 + 1) * 64:(tb0 + 1) * 64 + 64,
                                      c0:c0 + 512],
                          writes=[("l2ftab", 1)])
                    e_tt(ctx, ALU.mult, qraw[:, :], qn_t[:, :], tabs[0][:, :],
                         reads=[("l2fqn",), ("l2ftab", 0)],
                         writes=[("l2fqraw",)])
                    e_tt(ctx, ALU.mult, sqh[:, :], psrot[0:64, 0:512],
                         tabs[1][:, :], reads=[pkrot, ("l2ftab", 1)],
                         writes=[("l2fsqh",)])
                    e_tt(ctx, ALU.add, dst[0:64, c0:c0 + 512], qraw[:, :],
                         sqh[:, :], reads=[("l2fqraw",), ("l2fsqh",)],
                         writes=[dkey])

            for sweep in range(2):
                q0 = 2048 * sweep
                k0 = max(0, q0 - 512)
                e_memset(ctx, outsb[:, :, :], 0.0, writes=[("l2fout",)])
                for h in range(8):
                    wb, wk = wh.load(
                        wcols_src(aps["wcat2"], l * D, l * D + D,
                                  192 * h, 192 * h + 192))
                    e_dma(ctx, "gpsimd", qsks[:, :],
                          aps["qsks"][:, 2 * l:2 * l + 2], writes=[("l2fqsks",)])
                    qk_stage("q", h, q0, q0 + 2048)
                    qk_stage("k", h, k0, q0 + 2048)
                    e_memset(ctx, kf[64:65, :], 1.0, writes=[("l2fkf",)])
                    for t3 in range(k0 // 128, (q0 + 2048) // 128):
                        ps, pk = ctx.pfree()
                        for cc in range(NCC):
                            rhs, rds = rhs_sh(cc, t3 * 128, 128, sweep)
                            e_mm(ctx, ps[:, 0:64], rhs,
                                 wb[:, cc, 128:192], start=(cc == 0),
                                 stop=(cc == NCC - 1), reads=[wk] + rds,
                                 writes=[pk])
                        e_act(ctx, vtm[:, t3, 0:64], ps[:, 0:64], AF.Identity,
                              reads=[pk], writes=[("l2fv",)])
                        e_memset(ctx, vtm[:, t3, 64:65], 1.0,
                                 writes=[("l2fv",)])
                    if DBG and l == 0 and sweep == 0 and h == 0:
                        e_dma(ctx, "sync", aps["dbg_qf"], qf[:, :],
                              reads=[("l2fqf",)])
                        e_dma(ctx, "sync", aps["dbg_kf"], kf[:, :],
                              reads=[("l2fkf",)])
                        e_dma(ctx, "sync", aps["dbg_vt"], vtm[:, :, :],
                              reads=[("l2fv",)])
                    def stage_a(qt):
                        sl2 = qt % 2
                        tq3 = sweep * 16 + qt
                        nkc = min(5, tq3 + 1)
                        kclo = tq3 - nkc + 1
                        tqs = slice(tq3 * 128, tq3 * 128 + 128)
                        pa, pka = ctx.pfree4()
                        pb_, pkb = ctx.pfree4()
                        if nkc > 1:
                            e_mm(ctx, pa[:, 0:(nkc - 1) * 128],
                                 qf[0:64, tqs],
                                 kf[0:64, kclo * 128:tq3 * 128],
                                 start=True, stop=True,
                                 reads=[("l2fqf",), ("l2fkf",)], writes=[pka])
                            if tq3 >= 4:
                                e_tt(ctx, ALU.add, pa[:, 0:128], pa[:, 0:128],
                                     nm_low[:, :], reads=[pka, ("nm",)],
                                     writes=[pka])
                        e_mm(ctx, pb_[:, 0:128], qf[0:64, tqs],
                             kf[0:64, tq3 * 128:tq3 * 128 + 128],
                             start=True, stop=True,
                             reads=[("l2fqf",), ("l2fkf",)], writes=[pkb])
                        e_tt(ctx, ALU.add, pb_[:, 0:128], pb_[:, 0:128],
                             nm_up[:, :], reads=[pkb, ("nm",)], writes=[pkb])
                        e_red(ctx, amax[sl2][:, 0:1], pb_[:, 0:128],
                              ALU.max, negate=True, reads=[pkb],
                              writes=[("l2famax", sl2)])
                        if nkc > 1:
                            e_red(ctx, amax[sl2][:, 1:2],
                                  pa[:, 0:(nkc - 1) * 128],
                                  ALU.max, negate=True, reads=[pka],
                                  writes=[("l2famax", sl2)])
                            e_tt(ctx, ALU.min, amax[sl2][:, 0:1],
                                 amax[sl2][:, 0:1], amax[sl2][:, 1:2],
                                 reads=[("l2famax", sl2)],
                                 writes=[("l2famax", sl2)])
                        e_copy(ctx, negm[sl2][:, 0:1], amax[sl2][:, 0:1],
                               reads=[("l2famax", sl2)],
                               writes=[("l2fnegm", sl2)])
                        pbt, pbtk = ctx.pbf()
                        e_tr(ctx, pbt[64:65, 0:128], negm[sl2][:, 0:1],
                             ident[:, :], reads=[("l2fnegm", sl2), ("ident",)],
                             writes=[pbtk])
                        e_act(ctx, qf[64:65, tqs], pbt[64:65, 0:128],
                              AF.Identity, reads=[pbtk],
                              writes=[("l2fq64", qt % 4)])

                    def stage_b(qt):
                        sl2 = qt % 2
                        tq3 = sweep * 16 + qt
                        nkc = min(5, tq3 + 1)
                        kclo = tq3 - nkc + 1
                        tqs = slice(tq3 * 128, tq3 * 128 + 128)
                        psO, pkO = ctx.pbank(4 + sl2)
                        for ik in range(nkc):
                            kc = kclo + ik
                            Sb, Sk = ctx.pfree4()
                            e_mm(ctx, Sb[:, 0:128],
                                 kf[:, kc * 128:(kc + 1) * 128],
                                 qf[:, tqs], start=True, stop=True,
                                 reads=[("l2fkf",), ("l2fqf",),
                                        ("l2fq64", qt % 4)], writes=[Sk])
                            if kc == tq3 - 4:
                                e_tt(ctx, ALU.add, Sb[:, 0:128], Sb[:, 0:128],
                                     nm_up[:, :], reads=[Sk, ("nm",)],
                                     writes=[Sk])
                            if kc == tq3:
                                e_tt(ctx, ALU.add, Sb[:, 0:128], Sb[:, 0:128],
                                     nm_low[:, :], reads=[Sk, ("nm",)],
                                     writes=[Sk])
                            e_act(ctx, pt[qt % 2][:, :], Sb[:, 0:128],
                                  AF.Exp, reads=[Sk],
                                  writes=[("l2fpt", qt % 2)])
                            e_mm(ctx, psO[0:65, 0:128], vtm[:, kc, 0:65],
                                 pt[qt % 2][:, :], start=(ik == 0),
                                 stop=(ik == nkc - 1),
                                 reads=[("l2fv",), ("l2fpt", qt % 2)],
                                 writes=[pkO])

                    def stage_c(qt):
                        sl2 = qt % 2
                        psO, pkO = ctx.pbank(4 + sl2)
                        e_act(ctx, linv[sl2][64:65, 0:128],
                              psO[64:65, 0:128], AF.Ln, reads=[pkO],
                              writes=[("l2flinv", sl2)])
                        e_act(ctx, linv[sl2][64:65, 0:128],
                              linv[sl2][64:65, 0:128], AF.Exp, scale=-1.0,
                              reads=[("l2flinv", sl2)],
                              writes=[("l2flinv", sl2)])
                        psR, pkR = ctx.pfree4()
                        e_mm(ctx, psR[0:64, 0:128], onesf[64:65, 0:64],
                             linv[sl2][64:65, 0:128], start=True, stop=True,
                             reads=[("ones",), ("l2flinv", sl2)], writes=[pkR])
                        e_copy(ctx, rep[sl2][:, :], psR[0:64, 0:128],
                               reads=[pkR], writes=[("l2frep", sl2)])
                        e_tt(ctx, ALU.mult, osb[sl2][:, :], psO[0:64, 0:128],
                             rep[sl2][:, :], reads=[pkO, ("l2frep", sl2)],
                             writes=[("l2fosb", sl2)])
                        for ob in range(2):
                            psP, pkP = ctx.pfree4()
                            e_mm(ctx, psP[:, :], osb[sl2][:, :],
                                 wo_sb[:, h, ob * 512:(ob + 1) * 512],
                                 start=True, stop=True,
                                 reads=[("l2fosb", sl2), ("l2fwo",)],
                                 writes=[pkP])
                            e_tt(ctx, ALU.add,
                                 outsb[:, qt, ob * 512:(ob + 1) * 512],
                                 outsb[:, qt, ob * 512:(ob + 1) * 512],
                                 psP[:, :], reads=[("l2fout",), pkP],
                                 writes=[("l2fout",)])

                    stage_a(0)
                    stage_a(1)
                    for qt in range(16):
                        stage_b(qt)
                        if qt + 2 < 16:
                            stage_a(qt + 2)
                        stage_c(qt)
                if DBG and l == 0 and sweep == 0:
                    e_dma(ctx, "sync", aps["dbg_osb"], outsb[:, :, :],
                          reads=[("l2fout",)])
                if sweep == 0:
                    e_copy(ctx, xcop[:, :, 0:1025],
                           xn[:, :, 1 + 1535:1 + 1535 + 1025],
                           reads=XN, writes=[("l2fxcop",)])
                for qt in range(16):
                    t3 = sweep * 16 + qt
                    pbt, pbtk = ctx.pbf()
                    for j in range(NCC):
                        e_tr(ctx, pbt[:, j * 128:(j + 1) * 128],
                             outsb[:, qt, j * 128:(j + 1) * 128], ident[:, :],
                             reads=[("l2fout",), ("ident",)], writes=[pbtk])
                    e_tt(ctx, ALU.add, xn[:, :, 1 + t3 * 128:1 + t3 * 128 + 128],
                         xn[:, :, 1 + t3 * 128:1 + t3 * 128 + 128],
                         pbt[:, :].rearrange("p (j t) -> p j t", j=NCC),
                         reads=XN + [pbtk], writes=[(xnkey, t3)])
        if DBG and l == 0:
            e_dma(ctx, "sync", aps["dbg_xi0"], xn[:, :, :], reads=xn_all(xnkey))
        ctx.s.global_barrier()

        with ExitStack() as fst:
            fctx = sub_ctx(ctx, fst)
            xln = fctx.sb("l2f_xln", [128, NCC, 1024], BF16)
            hmid = fctx.sb("l2f_hmid", [128, 25, 1024], BF16)
            ggel = fctx.sb("l2f_gg", [128, 512], BF16)
            lnp = fctx.sb("l2f_lnp", [128, 2 * NCC], F32)
            rrow = fctx.sb("l2f_rrow", [1, 1024], F32)
            mrep = fctx.sb("l2f_mrep", [128, 512], BF16)
            rrep = fctx.sb("l2f_rrep", [128, 512], BF16)
            sqf = fctx.sb("l2f_sqf", [128, 512], BF16)
            outq_bufs = {}
            w1p = WPool(fctx, "l2f_w1p", 2, [128, 9, 512])
            w2p = WPool(fctx, "l2f_w2p", 2, [128, 13, 512])
            e_dma(ctx, "gpsimd", lnp[:, :],
                  aps["lnp"][:, 2 * NCC * l:2 * NCC * (l + 1)],
                  writes=[("l2flnp",)])
            e_memset(ctx, hmid[:, 24, :], 0.0, writes=[("l2fhm", 24)])
            e_memset(ctx, hmid[0:1, 24, :], 1.0, writes=[("l2fhm", 24)])
            for q4 in range(4):
                t0 = q4 * 1024
                for t5 in range(2):
                    c0 = t0 + t5 * 512
                    xsl = slice(t5 * 512, t5 * 512 + 512)
                    psS, pkS = ctx.pbank(4)
                    psQ, pkQ = ctx.pbank(5)
                    for cc in range(NCC):
                        e_mm(ctx, psS[0:1, 0:512], onesb[:, 0:1],
                             _l2f_shift_rhs(xn, cc, c0, 512),
                             start=(cc == 0), stop=(cc == NCC - 1),
                             reads=[("ones",)] + XN, writes=[pkS])
                    for cc in range(NCC):
                        e_act(ctx, sqf[:, :], _l2f_shift_rhs(xn, cc, c0, 512),
                              AF.Square, reads=XN, writes=[("l2fsqf",)])
                        e_mm(ctx, psQ[0:1, 0:512], onesb[:, 0:1], sqf[:, :],
                             start=(cc == 0), stop=(cc == NCC - 1),
                             reads=[("l2fsqf",), ("ones",)], writes=[pkQ])
                    e_ts(ctx, rrow[0:1, 0:512], psS[0:1, 0:512], 1.0 / D,
                         ALU.mult, reads=[pkS], writes=[("l2frrow",)])
                    e_act(ctx, rrow[0:1, 512:1024], rrow[0:1, 0:512],
                          AF.Square, reads=[("l2frrow",)],
                          writes=[("l2frrow",)])
                    e_ts(ctx, rrow[0:1, 512:1024], rrow[0:1, 512:1024], -1.0,
                         ALU.mult, reads=[("l2frrow",)], writes=[("l2frrow",)])
                    e_stt(ctx, rrow[0:1, 512:1024], psQ[0:1, 0:512], 1.0 / D,
                          rrow[0:1, 512:1024], ALU.mult, ALU.add,
                          reads=[pkQ, ("l2frrow",)], writes=[("l2frrow",)])
                    e_act(ctx, rrow[0:1, 512:1024], rrow[0:1, 512:1024],
                          AF.Ln, bias=1e-5, reads=[("l2frrow",)],
                          writes=[("l2frrow",)])
                    e_act(ctx, rrow[0:1, 512:1024], rrow[0:1, 512:1024],
                          AF.Exp, scale=-0.5, reads=[("l2frrow",)],
                          writes=[("l2frrow",)])
                    psM, pkM = ctx.pbank(4)
                    e_mm(ctx, psM[:, 0:512], onesf[0:1, 0:128],
                         rrow[0:1, 0:512], start=True, stop=True,
                         reads=[("ones",), ("l2frrow",)], writes=[pkM])
                    e_copy(ctx, mrep[:, :], psM[:, 0:512], reads=[pkM],
                           writes=[("l2fmrep",)])
                    psM2, pkM2 = ctx.pbank(5)
                    e_mm(ctx, psM2[:, 0:512], onesf[0:1, 0:128],
                         rrow[0:1, 512:1024], start=True, stop=True,
                         reads=[("ones",), ("l2frrow",)], writes=[pkM2])
                    e_copy(ctx, rrep[:, :], psM2[:, 0:512], reads=[pkM2],
                           writes=[("l2frrep",)])
                    for cc in range(NCC):
                        e_tt(ctx, ALU.subtract, xln[:, cc, xsl],
                             _l2f_shift_rhs(xn, cc, c0, 512), mrep[:, :],
                             reads=XN + [("l2fmrep",)], writes=[("l2fxln",)])
                        e_tt(ctx, ALU.mult, xln[:, cc, xsl], xln[:, cc, xsl],
                             rrep[:, :], reads=[("l2fxln",), ("l2frrep",)],
                             writes=[("l2fxln",)])
                        e_ts(ctx, xln[:, cc, xsl], xln[:, cc, xsl],
                             lnp[:, 2 * cc:2 * cc + 1], ALU.mult,
                             s2=lnp[:, 2 * cc + 1:2 * cc + 2], op1=ALU.add,
                             reads=[("l2fxln",), ("l2flnp",)],
                             writes=[("l2fxln",)])
                e_dma(ctx, "scalar",
                      aps["xln_scr"].rearrange("(cc p) t -> p cc t", p=128)
                      [:, :, t0:t0 + 1024], xln[:, :, :],
                      reads=[("l2fxln",)], writes=[("l2fxlnD", q4)])
            for q4 in range(4):
                t0 = q4 * 1024
                e_dma(ctx, "sync", xln[:, :, :],
                      aps["xln_scr"].rearrange("(cc p) t -> p cc t", p=128)
                      [:, :, t0:t0 + 1024],
                      reads=[("l2fxlnD", q4)], writes=[("l2fxln",)])
                if DBG and l == 0 and q4 == 0:
                    e_dma(ctx, "sync", aps["dbg_xln"], xln[:, :, :],
                          reads=[("l2fxln",)])
                for i in range(6):
                    wA, wAk = w1p.load(
                        aps["w1p"][l * 1152:(l + 1) * 1152, :]
                        .rearrange("(cc p) o -> p cc o", p=128)
                        [:, :, 512 * i:512 * i + 512])
                    wG, wGk = w1p.load(
                        aps["w1p"][l * 1152:(l + 1) * 1152, :]
                        .rearrange("(cc p) o -> p cc o", p=128)
                        [:, :, 3072 + 512 * i:3072 + 512 * i + 512])
                    for hc4 in range(4):
                        hc = 4 * i + hc4
                        for t5 in range(2):
                            xsl = slice(t5 * 512, t5 * 512 + 512)
                            psU, pkU = ctx.pbank(4)
                            psG, pkG = ctx.pbank(5)
                            for cc in range(9):
                                rhs = (xln[:, cc, xsl] if cc < 8
                                       else row1[:, :])
                                rds = ([("l2fxln",)] if cc < 8
                                       else [("row1",)])
                                e_mm(ctx, psU[:, :],
                                     wA[:, cc, hc4 * 128:(hc4 + 1) * 128],
                                     rhs, start=(cc == 0), stop=(cc == 8),
                                     reads=[wAk] + rds, writes=[pkU])
                            for cc in range(9):
                                rhs = (xln[:, cc, xsl] if cc < 8
                                       else row1[:, :])
                                rds = ([("l2fxln",)] if cc < 8
                                       else [("row1",)])
                                e_mm(ctx, psG[:, :],
                                     wG[:, cc, hc4 * 128:(hc4 + 1) * 128],
                                     rhs, start=(cc == 0), stop=(cc == 8),
                                     reads=[wGk] + rds, writes=[pkG])
                            e_act(ctx, ggel[:, :], psG[:, :], AF.Gelu,
                                  reads=[pkG], writes=[("l2fgg",)])
                            e_tt(ctx, ALU.mult, hmid[:, hc, xsl], psU[:, :],
                                 ggel[:, :], reads=[pkU, ("l2fgg",)],
                                 writes=[("l2fhm", hc)])
                if q4 == 0:
                    outq_bufs[0] = fctx.sb("l2f_outq", [128, 4, D], BF16)
                outq = outq_bufs[0]
                for half8 in range(2):
                    for ob in range(2):
                        for piece in range(2):
                            h0 = 13 * piece
                            h1 = min(25, h0 + 13)
                            w2, w2k = w2p.load(
                                aps["w2p"][l * 3200 + h0 * 128:
                                           l * 3200 + h1 * 128, :]
                                .rearrange("(cc p) o -> p cc o", p=128)
                                [:, :, ob * 512:ob * 512 + 512])
                            for t7 in range(half8 * 4, half8 * 4 + 4):
                                ps, pk = ctx.pbank(2 + (t7 % 4))
                                for hc in range(h0, h1):
                                    e_mm(ctx, ps[:, :],
                                         hmid[:, hc, t7 * 128:(t7 + 1) * 128],
                                         w2[:, hc - h0, :],
                                         start=(hc == 0), stop=(hc == 24),
                                         reads=[w2k, ("l2fhm", hc)],
                                         writes=[pk])
                                if piece == 1:
                                    e_copy(ctx, outq[:, t7 - half8 * 4,
                                                     ob * 512:(ob + 1) * 512],
                                           ps[:, :], reads=[pk],
                                           writes=[("l2foutq",)])
                    for t7 in range(half8 * 4, half8 * 4 + 4):
                        t3 = q4 * 8 + t7
                        pbt, pbtk = ctx.pbf()
                        for j in range(NCC):
                            e_tr(ctx, pbt[:, j * 128:(j + 1) * 128],
                                 outq[:, t7 - half8 * 4,
                                      j * 128:(j + 1) * 128], ident[:, :],
                                 reads=[("l2foutq",), ("ident",)],
                                 writes=[pbtk])
                        e_tt(ctx, ALU.add,
                             xn[:, :, 1 + t3 * 128:1 + t3 * 128 + 128],
                             xn[:, :, 1 + t3 * 128:1 + t3 * 128 + 128],
                             pbt[:, :].rearrange("p (j t) -> p j t", j=NCC),
                             reads=XN + [pbtk], writes=[(xnkey, t3)])
            if DBG and l == 0:
                e_dma(ctx, "sync", aps["dbg_xi1"], xn[:, :, :],
                      reads=xn_all(xnkey))
        ctx.s.global_barrier()

    for t3 in range(NTT):
        pbt, pbtk = ctx.pbf()
        for j in range(NCC):
            e_tr(ctx, pbt[:, j * 128:(j + 1) * 128],
                 xn[:, j, 1 + t3 * 128:1 + t3 * 128 + 128], ident[:, :],
                 reads=XN + [("ident",)], writes=[pbtk])
        skey = ("stage", t3 % 2, 0)
        e_copy(ctx, stage[t3 % 2][:, :], pbt[:, :], reads=[pbtk],
               writes=[skey])
        e_dma(ctx, "scalar", aps["acc_f"][t3 * 128:(t3 + 1) * 128, :],
              stage[t3 % 2][:, :], reads=[skey], writes=[("acc_f", t3)])


def build_l2(fctx=None, prefix="", x_in_ap=None, x_out_ap=None):
    nc = fctx.nc if fctx is not None else bass.Bass(
        "TRN2", target_bir_lowering=False)
    g = lambda n, sh: nc.dram_tensor(prefix + n, sh, F32,
                                     kind="ExternalInput").ap()
    gb = lambda n, sh: nc.dram_tensor(prefix + n, sh, BF16,
                                      kind="ExternalInput").ap()
    aps = {
        "x_in": x_in_ap if x_in_ap is not None else g("x_in", [T, D]),
        "Wcat": g("Wcat", [D, 16 * 192]),
        "Wf": g("Wf", [D, 16]),
        "wo_head": g("wo_head", [64, 16 * D]),
        "wcat2": g("wcat2", [2 * D, 8 * 192]),
        "wo2_head": g("wo2_head", [64, 2 * 8 * D]),
        "qsks": g("qsks", [64, 4]),
        "lnp": g("lnp", [128, 2 * 2 * NCC]),
        "w1p": g("w1p", [2 * 1152, 6144]),
        "w2p": g("w2p", [2 * 3200, 1024]),
        "xtab": g("xtab", [256, T]),
        "negmask": gb("negmask", [128, 4 * 512]),
        "nm_low": g("nm_low", [128, 128]),
        "nm_up": g("nm_up", [128, 128]),
        "ones_f32": g("ones_f32", [128, 136]),
        "ident": gb("ident", [128, 128]),
        "ones_bf": gb("ones_bf", [128, 640]),
        "row1": gb("row1", [128, 512]),
        "rotm": g("rotm", [64, 64]),
    }
    aps["negmask"] = aps["negmask"].rearrange("p (m q) -> p m q", m=4)
    aps["x_out"] = x_out_ap if x_out_ap is not None else nc.dram_tensor(
        prefix + "x_out", [T, D], F32, kind="ExternalOutput").ap()
    acck = "ExternalOutput" if DBG else "Internal"
    aps["acc_a"] = nc.dram_tensor(prefix + "acc_a", [T, D], F32,
                                  kind=acck).ap()
    aps["acc_f"] = nc.dram_tensor(prefix + "acc_f", [T, D], F32,
                                  kind=acck).ap()
    if DBG:
        aps["dbg_qf"] = nc.dram_tensor("dbg_qf", [65, T], BF16,
                                       kind="ExternalOutput").ap()
        aps["dbg_kf"] = nc.dram_tensor("dbg_kf", [65, T], BF16,
                                       kind="ExternalOutput").ap()
        aps["dbg_vt"] = nc.dram_tensor("dbg_vt", [128, NTT, 72], BF16,
                                       kind="ExternalOutput").ap()
        aps["dbg_osb"] = nc.dram_tensor("dbg_osb", [128, 16, D], BF16,
                                        kind="ExternalOutput").ap()
        aps["dbg_xi0"] = nc.dram_tensor("dbg_xi0", [128, NCC, PADW], BF16,
                                        kind="ExternalOutput").ap()
        aps["dbg_xi1"] = nc.dram_tensor("dbg_xi1", [128, NCC, PADW], BF16,
                                        kind="ExternalOutput").ap()
        aps["dbg_xln"] = nc.dram_tensor("dbg_xln", [128, NCC, 1024], BF16,
                                        kind="ExternalOutput").ap()
    aps["c_scr"] = nc.dram_tensor(prefix + "c_scr", [16, T], F32).ap()
    aps["xln_scr"] = nc.dram_tensor(prefix + "xln_scr", [D, T], BF16).ap()
    aps["csp_scr"] = nc.dram_tensor(prefix + "csp_scr", [16, 3 * T],
                                    BF16).ap()

    def body(ctx):
        sched = ctx.s
        bufs = make_common_bufs(ctx, aps, 2)
        for nm, key, shp, dt_ in (
                ("ones_bf", "ones", [128, 640], BF16),
                ("ones_f32", "ones", [128, 136], F32),
                ("negmask", "nmask", [128, 4, 512], BF16),
                ("nm_low", "nm", [128, 128], F32),
                ("nm_up", "nm", [128, 128], F32),
                ("row1", "row1", [128, 512], BF16)):
            bufs[nm] = ctx.sb(f"sb_{nm}", shp, dt_)
            e_dma(ctx, "gpsimd",
                  bufs[nm][:, :, :] if len(shp) == 3 else bufs[nm][:, :],
                  aps[nm], writes=[(key,)])
        bufs["l2_c"] = aps["c_scr"]
        bufs["l2_csp"] = aps["csp_scr"]

        xn = ctx.sb("xn", [128, NCC, PADW], BF16)
        xnkey = "xn"
        e_memset(ctx, xn[:, :, 0:1], 0.0, writes=[(xnkey, "all")])
        e_memset(ctx, xn[:, :, 1 + T:PADW], 0.0, writes=[(xnkey, "all")])

        with ExitStack() as st:
            pctx = sub_ctx(ctx, st)
            eb = make_epi_bufs(pctx, 0)
            emit_ln(pctx, aps["x_in"], [], None, xn, xnkey, bufs={**bufs, **eb})
        sched.global_barrier()
        with ExitStack() as st:
            emit_l2_attn(sub_ctx(ctx, st), aps, xn, xnkey, bufs)
        sched.global_barrier()
        with ExitStack() as st:
            emit_l2_ffn(sub_ctx(ctx, st), aps, xn, xnkey, bufs)
        sched.global_barrier()
        with ExitStack() as st:
            fctx = sub_ctx(ctx, st)
            eb = make_epi_bufs(fctx, 2)
            emit_ln(fctx, aps["x_in"],
                    [(aps["acc_a"], "acc_a"), (aps["acc_f"], "acc_f")],
                    aps["x_out"], None, xnkey, bufs={**bufs, **eb})
    if fctx is None:
        with ExitStack() as stack:
            sched = Sched(nc, stack)
            ctx = Ctx(nc, sched, stack)
            body(ctx)
            sched.finish()
            with nc.Block() as block:
                sched.replay(block)
        return nc
    with ExitStack() as lst:
        body(sub_ctx(fctx, lst))
    fctx.s.global_barrier()


def l2_host_tables():
    DIM_HEAD, WINDOW, n = 64, 512, T
    half_idx = np.arange(0, DIM_HEAD, 2, dtype=np.float64)
    inv_freq = 1.0 / 10000.0 ** (half_idx / DIM_HEAD)
    t = np.arange(n, dtype=np.float64)
    freqs = t[:, None] * inv_freq[None, :]
    freqs = np.concatenate([freqs, freqs], axis=-1)
    base = (half_idx + 0.4 * DIM_HEAD) / (1.4 * DIM_HEAD)
    power = (t - n // 2) / float(WINDOW // 2)
    sc = base[None, :] ** power[:, None]
    sc = np.concatenate([sc, sc], axis=-1)
    cos, sin = np.cos(freqs), np.sin(freqs)
    tabs = np.stack([cos * sc, sin * sc, cos / sc, sin / sc])
    return np.ascontiguousarray(
        tabs.transpose(0, 2, 1).reshape(256, T).astype(np.float32))


def l2_inputs(inputs, xb):
    c = host_consts()
    Wq, Wk, Wv = (inputs["L2_attn_Wq"], inputs["L2_attn_Wk"],
                  inputs["L2_attn_Wv"])
    wcat = np.concatenate(
        [np.concatenate([Wq[:, h * 64:(h + 1) * 64],
                         Wk[:, h * 64:(h + 1) * 64],
                         Wv[:, h * 64:(h + 1) * 64]], axis=1)
         for h in range(16)], axis=1)
    wo_head = np.ascontiguousarray(
        inputs["L2_attn_Wo"].reshape(16, 64, D).transpose(1, 0, 2)
        .reshape(64, 16 * D))
    wq2 = inputs["L2_ffn_wqkv"]
    wcat2 = np.concatenate(
        [np.concatenate(
            [np.concatenate([wq2[l][:, h * 64:(h + 1) * 64],
                             wq2[l][:, 512 + h * 64:512 + (h + 1) * 64],
                             wq2[l][:, 1024 + h * 64:1024 + (h + 1) * 64]],
                            axis=1) for h in range(8)], axis=1)
         for l in range(2)], axis=0)
    wo2 = inputs["L2_ffn_wo"]
    wo2_head = np.concatenate(
        [wo2[l].reshape(8, 64, D).transpose(1, 0, 2).reshape(64, 8 * D)
         for l in range(2)], axis=1)
    qsks = np.stack([inputs["L2_ffn_q_scale"][0], inputs["L2_ffn_k_scale"][0],
                     inputs["L2_ffn_q_scale"][1], inputs["L2_ffn_k_scale"][1]],
                    axis=1)
    lnp = np.zeros((128, 2 * 2 * NCC), np.float32)
    for l in range(2):
        gp = vec_p(inputs["L2_ffn_ln_g"][l])
        bp = vec_p(inputs["L2_ffn_ln_b"][l])
        for cc in range(NCC):
            lnp[:, 2 * NCC * l + 2 * cc] = gp[:, cc]
            lnp[:, 2 * NCC * l + 2 * cc + 1] = bp[:, cc]
    w1 = inputs["L2_ffn_w1"]; b1 = inputs["L2_ffn_b1"]
    w2 = inputs["L2_ffn_w2"]; b2 = inputs["L2_ffn_b2"]
    HID = 2730
    w1p = np.zeros((2 * 1152, 6144), np.float32)
    w2p = np.zeros((2 * 3200, 1024), np.float32)
    for l in range(2):
        w1p[l * 1152:l * 1152 + 1024, 0:HID] = w1[l][:, 0:HID]
        w1p[l * 1152 + 1024, 0:HID] = b1[l][0:HID]
        w1p[l * 1152:l * 1152 + 1024, 3072:3072 + HID] = w1[l][:, HID:]
        w1p[l * 1152 + 1024, 3072:3072 + HID] = b1[l][HID:]
        w2p[l * 3200:l * 3200 + HID, :] = w2[l]
        w2p[l * 3200 + 3072, :] = b2[l]
    p = np.arange(128)[:, None, None]
    mm_ = np.arange(4)[None, :, None]
    q = np.arange(512)[None, None, :]
    negmask = np.where((128 * mm_ + p) <= q, 0.0, -1e30).astype(BF)
    pq = np.arange(128)
    nm_low = np.where(pq[:, None] > pq[None, :], -1e30, 0.0).astype(np.float32)
    nm_up = np.where(pq[:, None] < pq[None, :], -1e30, 0.0).astype(np.float32)
    rotm = np.zeros((64, 64), np.float32)
    for d in range(32):
        rotm[32 + d, d] = -1.0
        rotm[d, 32 + d] = 1.0
    return {
        "x_in": np.ascontiguousarray(xb),
        "Wcat": np.ascontiguousarray(wcat),
        "Wf": inputs["L2_attn_Wf"],
        "wo_head": wo_head,
        "wcat2": np.ascontiguousarray(wcat2),
        "wo2_head": np.ascontiguousarray(wo2_head),
        "qsks": np.ascontiguousarray(qsks),
        "lnp": lnp, "w1p": w1p, "w2p": w2p,
        "xtab": l2_host_tables(),
        "negmask": negmask.reshape(128, 4 * 512),
        "nm_low": nm_low, "nm_up": nm_up, "rotm": rotm,
        "ident": c["ident"], "ones_bf": c["ones_bf"],
        "ones_f32": c["ones_f32"], "row1": c["row1"],
    }



def emit_l3_attn(ctx, aps, xn, xnkey, bufs):
    XN = xn_all(xnkey)
    onesb = bufs["ones_bf"]
    onesf = bufs["ones_f32"]
    dmask = bufs["diagmask"]
    stage = bufs["stage"]

    kaug = ctx.sb("l3_kaug", [65, T], BF16)
    qaug = ctx.sb("l3_qaug", [65, 2048], BF16)
    vtm = ctx.sb("l3_vtm", [128, NTT, 72], BF16)
    negu = ctx.sb("l3_negu", [1, 2048], BF16)
    rowA = ctx.sb("l3_rowA", [1, T], BF16)
    sq = ctx.sb("l3_sq", [128, 512], BF16)
    kms = ctx.sb("l3_kms", [1, 4], F32)
    linv = [ctx.sb("l3_linv", [128, 512], F32) for _ in range(2)]
    rep = [ctx.sb("l3_rep", [64, 512], F32) for _ in range(2)]
    osb = [ctx.sb("l3_osb", [64, 512], BF16) for _ in range(2)]
    pt = [ctx.sb(f"l3_pt{i}", [128, 512], BF16) for i in range(4)]
    outsb = ctx.sb("l3_outsb", [128, 16, D], BF16)
    brow = ctx.sb("l3_brow", [1, 192], BF16)
    bqs = ctx.sb("l3_bqs", [64, 48], F32)
    e_ts(ctx, bqs[:, :], bufs["bqkv_p"][:, :], 0.125, ALU.mult,
         reads=[("bqkvp",)], writes=[("l3bqs",)])
    wo_sb = ctx.sb("l3_wo", [64, 16, D], BF16)
    e_dma(ctx, "gpsimd", wo_sb[:, :, :],
          aps["wo_head"].rearrange("p (h o) -> p h o", h=16),
          writes=[("l3wo",)])
    wh = WPool(ctx, "l3wh", 2, [128, NCC, 192])

    for sweep in range(2):
        kr = 2048 * (sweep + 1)
        q0 = 2048 * sweep
        e_memset(ctx, outsb[:, :, :], 0.0, writes=[("l3out",)])
        for h in range(16):
            wb, wk = wh.load(wcols_src(aps["W_qkv"], 0, D,
                                       192 * h, 192 * h + 192))
            e_dma(ctx, "gpsimd", brow[0:1, :], aps["bqkv_rows"][h:h + 1, :],
                  writes=[("l3brow",)])
            for tt in range(kr // 512):
                ps, pk = ctx.pfree()
                for cc in range(NCC):
                    e_mm(ctx, ps[0:64, :], wb[:, cc, 64:128],
                         xn[:, cc, 1 + tt * 512:1 + tt * 512 + 512],
                         start=(cc == 0), stop=(cc == NCC - 1),
                         reads=[wk] + XN, writes=[pk])
                e_act(ctx, kaug[0:64, tt * 512:(tt + 1) * 512], ps[0:64, :],
                      AF.Identity, bias=bufs["bqkv_p"][:, 3 * h + 1:3 * h + 2],
                      reads=[pk, ("bqkvp",)], writes=[("l3k",)])
            e_memset(ctx, kaug[64:65, 0:kr], 1.0, writes=[("l3k",)])
            for tt in range(4):
                ps, pk = ctx.pfree()
                for cc in range(NCC):
                    e_mm(ctx, ps[0:64, :], wb[:, cc, 0:64],
                         xn[:, cc, 1 + q0 + tt * 512:1 + q0 + tt * 512 + 512],
                         start=(cc == 0), stop=(cc == NCC - 1),
                         reads=[wk] + XN, writes=[pk])
                e_act(ctx, qaug[0:64, tt * 512:(tt + 1) * 512], ps[0:64, :],
                      AF.Identity, scale=0.125,
                      bias=bqs[:, 3 * h:3 * h + 1],
                      reads=[pk, ("l3bqs",)], writes=[("l3q",)])
            for t3 in range(kr // 128):
                ps, pk = ctx.pfree()
                for cc in range(NCC):
                    e_mm(ctx, ps[:, 0:64], xn[:, cc, 1 + t3 * 128:1 + t3 * 128 + 128],
                         wb[:, cc, 128:192], start=(cc == 0), stop=False,
                         reads=[wk] + XN, writes=[pk])
                e_mm(ctx, ps[:, 0:64], onesb[0:1, 0:128],
                     brow[0:1, 128:192], start=False, stop=True,
                     reads=[("ones",), ("l3brow",)], writes=[pk])
                e_act(ctx, vtm[:, t3, 0:64], ps[:, 0:64], AF.Identity,
                      reads=[pk], writes=[("l3v",)])
            e_memset(ctx, vtm[:, 0:kr // 128, 64:65], 1.0, writes=[("l3v",)])
            for tt in range(kr // 512):
                ps, pk = ctx.pfree()
                e_act(ctx, sq[0:64, :], kaug[0:64, tt * 512:(tt + 1) * 512],
                      AF.Square, reads=[("l3k",)], writes=[("l3sq",)])
                e_mm(ctx, ps[0:1, 0:512], onesb[0:64, 0:1], sq[0:64, :],
                     start=True, stop=True, reads=[("l3sq",), ("ones",)],
                     writes=[pk])
                e_copy(ctx, rowA[0:1, tt * 512:(tt + 1) * 512], ps[0:1, 0:512],
                       reads=[pk], writes=[("l3rowA",)])
            e_red(ctx, kms[0:1, 0:1], rowA[0:1, 0:kr], ALU.max,
                  reads=[("l3rowA",)], writes=[("l3kms",)])
            e_act(ctx, kms[0:1, 1:2], kms[0:1, 0:1], AF.Ln, reads=[("l3kms",)],
                  writes=[("l3kms",)])
            e_act(ctx, kms[0:1, 2:3], kms[0:1, 1:2], AF.Exp, scale=0.5,
                  reads=[("l3kms",)], writes=[("l3kms",)])
            e_ts(ctx, kms[0:1, 3:4], kms[0:1, 2:3], -1.0, ALU.mult,
                 reads=[("l3kms",)], writes=[("l3kms",)])
            for tt in range(4):
                ps, pk = ctx.pfree()
                e_act(ctx, sq[0:64, :], qaug[0:64, tt * 512:(tt + 1) * 512],
                      AF.Square, reads=[("l3q",)], writes=[("l3sq",)])
                e_mm(ctx, ps[0:1, 0:512], onesb[0:64, 0:1], sq[0:64, :],
                     start=True, stop=True, reads=[("l3sq",), ("ones",)],
                     writes=[pk])
                e_copy(ctx, rowA[0:1, tt * 512:(tt + 1) * 512], ps[0:1, 0:512],
                       reads=[pk], writes=[("l3rowA",)])
            e_act(ctx, negu[0:1, :], rowA[0:1, 0:2048], AF.Ln,
                  reads=[("l3rowA",)], writes=[("l3negu",)])
            e_act(ctx, rowA[0:1, 0:2048], negu[0:1, :], AF.Exp, scale=0.5,
                  reads=[("l3negu",)], writes=[("l3rowA",)])
            e_ts(ctx, negu[0:1, :], rowA[0:1, 0:2048], kms[0:1, 3:4],
                 ALU.mult, reads=[("l3rowA",), ("l3kms",)], writes=[("l3negu",)])
            e_dma(ctx, "sync", qaug[64:65, :], negu[0:1, :],
                  reads=[("l3negu",)], writes=[("l3q",)])
            for lg in range(4):
                g = sweep * 4 + lg
                sl2 = lg % 2
                kcs = 4 * (g + 1)
                tqs = slice(lg * 512, (lg + 1) * 512)
                psO, pkO = ctx.pbank(4 + sl2)
                for kci in range(kcs):
                    Sb, Sk = ctx.pbank(kci % 2)
                    e_mm(ctx, Sb[:, :], kaug[:, kci * 128:(kci + 1) * 128],
                         qaug[:, tqs], start=True, stop=True,
                         reads=[("l3k",), ("l3q",)], writes=[Sk])
                    ptb, ptk = pt[kci % 4], ("l3pt", kci % 4)
                    e_act(ctx, ptb[:, :], Sb[:, :], AF.Exp, reads=[Sk],
                          writes=[ptk])
                    if kci >= kcs - 4:
                        e_tt(ctx, ALU.mult, ptb[:, :], ptb[:, :],
                             dmask[:, kci - (kcs - 4), :],
                             reads=[ptk, ("dmask",)], writes=[ptk])
                    e_mm(ctx, psO[0:65, :], vtm[:, kci, 0:65], ptb[:, :],
                         start=(kci == 0), stop=(kci == kcs - 1),
                         reads=[("l3v",), ptk], writes=[pkO])
                e_act(ctx, linv[sl2][64:65, :], psO[64:65, :], AF.Ln,
                      reads=[pkO], writes=[("l3linv", sl2)])
                e_act(ctx, linv[sl2][64:65, :], linv[sl2][64:65, :], AF.Exp,
                      scale=-1.0, reads=[("l3linv", sl2)],
                      writes=[("l3linv", sl2)])
                psR, pkR = ctx.pfree()
                e_mm(ctx, psR[0:64, :], onesf[64:65, 0:64], linv[sl2][64:65, :],
                     start=True, stop=True, reads=[("ones",), ("l3linv", sl2)],
                     writes=[pkR])
                e_copy(ctx, rep[sl2][:, :], psR[0:64, :], reads=[pkR],
                       writes=[("l3rep", sl2)])
                e_tt(ctx, ALU.mult, osb[sl2][:, :], psO[0:64, :],
                     rep[sl2][:, :], reads=[pkO, ("l3rep", sl2)],
                     writes=[("l3osb", sl2)])
                for t4 in range(4):
                    for ob in range(2):
                        psP, pkP = ctx.pbank(2 + ob)
                        e_mm(ctx, psP[:, :],
                             osb[sl2][:, t4 * 128:(t4 + 1) * 128],
                             wo_sb[:, h, ob * 512:(ob + 1) * 512],
                             start=True, stop=True,
                             reads=[("l3osb", sl2), ("l3wo",)], writes=[pkP])
                        e_tt(ctx, ALU.add,
                             outsb[:, lg * 4 + t4, ob * 512:(ob + 1) * 512],
                             outsb[:, lg * 4 + t4, ob * 512:(ob + 1) * 512],
                             psP[:, :], reads=[("l3out",), pkP],
                             writes=[("l3out",)])
        for t in range(16):
            t3 = sweep * 16 + t
            skey = ("stage", t % 2, 0)
            e_tt(ctx, ALU.add, stage[t % 2][:, :], outsb[:, t, :],
                 bufs["bo_rep"][:, :], reads=[("l3out",), ("borep",)],
                 writes=[skey])
            e_dma(ctx, "scalar", aps["acc_a"][t3 * 128:(t3 + 1) * 128, :],
                  stage[t % 2][:, :], reads=[skey], writes=[("acc_a", t3)])


def emit_quant_tile(ctx, vq, scr, key, nblk=32):
    s32, av, g2, xi = scr["s32"], scr["av"], scr["g2"], scr["xi"]
    W = nblk * 32
    v3 = vq[:, :].rearrange("p (b j) -> p b j", b=nblk)
    e_red(ctx, s32[:, 0:nblk], v3, ALU.max, reads=[key],
          writes=[("q_s32",)])
    e_ts(ctx, s32[:, 0:nblk], s32[:, 0:nblk], 1.0 / 7.5, ALU.mult,
         s2=1e-30, op1=ALU.max, reads=[("q_s32",)], writes=[("q_s32",)])
    ctx.s.emit("vector", lambda e: e.reciprocal(s32[:, nblk:2 * nblk],
                                                s32[:, 0:nblk]),
               reads=[("q_s32",)], writes=[("q_s32",)])
    for b in range(nblk):
        e_ts(ctx, vq[:, b * 32:(b + 1) * 32], vq[:, b * 32:(b + 1) * 32],
             s32[:, nblk + b:nblk + b + 1], ALU.mult,
             reads=[key, ("q_s32",)], writes=[key])
    e_act(ctx, av[:, 0:W], vq[:, :], AF.Abs, reads=[key], writes=[("q_av",)])
    e_ts(ctx, g2[:, 0:W], av[:, 0:W], 2.0, ALU.is_ge, reads=[("q_av",)],
         writes=[("q_g2",)])
    e_ts(ctx, av[:, 0:W], av[:, 0:W], 4.0, ALU.is_ge, reads=[("q_av",)],
         writes=[("q_av",)])
    e_ts(ctx, scr["isv"][:, 0:W], av[:, 0:W], -2.0, ALU.mult, s2=8.0,
         op1=ALU.add, reads=[("q_av",)], writes=[("q_isv",)])
    e_stt(ctx, scr["isv"][:, 0:W], g2[:, 0:W], -4.0, scr["isv"][:, 0:W],
          ALU.mult, ALU.add, reads=[("q_g2",), ("q_isv",)], writes=[("q_isv",)])
    e_ts(ctx, av[:, 0:W], av[:, 0:W], 0.25, ALU.mult, s2=0.125, op1=ALU.add,
         reads=[("q_av",)], writes=[("q_av",)])
    e_stt(ctx, g2[:, 0:W], g2[:, 0:W], 0.125, av[:, 0:W], ALU.mult, ALU.add,
          reads=[("q_g2",), ("q_av",)], writes=[("q_g2",)])
    e_tt(ctx, ALU.mult, vq[:, :], vq[:, :], scr["isv"][:, 0:W],
         reads=[key, ("q_isv",)], writes=[key])
    e_copy(ctx, xi[:, 0:W], vq[:, :], reads=[key], writes=[("q_xi",)])
    e_copy(ctx, vq[:, :], xi[:, 0:W], reads=[("q_xi",)], writes=[key])
    e_tt(ctx, ALU.mult, vq[:, :], vq[:, :], g2[:, 0:W], reads=[key, ("q_g2",)],
         writes=[key])
    for b in range(nblk):
        e_ts(ctx, vq[:, b * 32:(b + 1) * 32], vq[:, b * 32:(b + 1) * 32],
             s32[:, b:b + 1], ALU.mult, reads=[key, ("q_s32",)], writes=[key])


def emit_l3_ffn(ctx, aps, stats, bufs):
    ident = bufs["ident"]
    stage = bufs["stage"]
    scr = {"s32": ctx.sb("l3f_s32", [128, 64], F32),
           "av": ctx.sb("l3f_av", [128, 1024], F32),
           "g2": ctx.sb("l3f_g2", [128, 1024], F32),
           "isv": ctx.sb("l3f_isv", [128, 1024], F32),
           "xi": ctx.sb("l3f_xi", [128, 1024], mybir.dt.int32)}
    wtile = ctx.sb("l3f_wt", [128, 1024], F32)
    wdb = ctx.sb("l3f_wdb", [128, 1024], BF16)
    wdT = ctx.sb("l3f_wdT", [128, NCC, 1024], BF16)
    xtile = ctx.sb("l3f_xt", [128, 1024], F32)
    xdb = ctx.sb("l3f_xdb", [128, 1024], BF16)
    xdf = ctx.sb("l3f_xdf", [128, NCC, 128], BF16)

    for wc in range(NCC):
        e_dma(ctx, "gpsimd", wtile[:, :],
              aps["fw"][wc * 128:(wc + 1) * 128, :], writes=[("l3fw",)])
        emit_quant_tile(ctx, wtile, scr, ("l3fw",))
        e_copy(ctx, wdb[:, :], wtile[:, :], reads=[("l3fw",)],
               writes=[("l3wdb",)])
        pb, pbk = ctx.pbf()
        for j in range(NCC):
            e_tr(ctx, pb[:, j * 128:(j + 1) * 128],
                 wdb[:, j * 128:(j + 1) * 128], ident[:, :],
                 reads=[("l3wdb",), ("ident",)], writes=[pbk])
        e_copy(ctx, wdT[:, :, wc * 128:(wc + 1) * 128],
               pb[:, :].rearrange("p (j t) -> p j t", j=NCC),
               reads=[pbk], writes=[("l3wdT",)])

    for t3 in range(NTT):
        e_dma(ctx, "sync", xtile[:, :],
              aps["x_in"][t3 * 128:(t3 + 1) * 128, :], writes=[("l3xt",)])
        e_ts(ctx, xtile[:, :], xtile[:, :], stats[0][:, t3:t3 + 1],
             ALU.subtract, s2=None, op1=ALU.bypass,
             reads=[("l3xt",), ("stats",)], writes=[("l3xt",)])
        e_ts(ctx, xtile[:, :], xtile[:, :], stats[1][:, t3:t3 + 1],
             ALU.mult, reads=[("l3xt",), ("stats",)], writes=[("l3xt",)])
        emit_quant_tile(ctx, xtile, scr, ("l3xt",))
        e_copy(ctx, xdb[:, :], xtile[:, :], reads=[("l3xt",)],
               writes=[("l3xdb",)])
        pb, pbk = ctx.pbf()
        for j in range(NCC):
            e_tr(ctx, pb[:, j * 128:(j + 1) * 128],
                 xdb[:, j * 128:(j + 1) * 128], ident[:, :],
                 reads=[("l3xdb",), ("ident",)], writes=[pbk])
        e_copy(ctx, xdf[:, :, :],
               pb[:, :].rearrange("p (j t) -> p j t", j=NCC),
               reads=[pbk], writes=[("l3xdf",)])
        for ob in range(2):
            ps, pk = ctx.pbank(4 + ob)
            for cc in range(NCC):
                e_mm(ctx, ps[:, :], xdf[:, cc, :],
                     wdT[:, cc, ob * 512:(ob + 1) * 512],
                     start=(cc == 0), stop=(cc == NCC - 1),
                     reads=[("l3xdf",), ("l3wdT",)], writes=[pk])
            skey = ("stage", t3 % 2, ob)
            e_tt(ctx, ALU.add, stage[t3 % 2][:, ob * 512:(ob + 1) * 512],
                 ps[:, :], bufs["fb_rep"][:, ob * 512:(ob + 1) * 512],
                 reads=[pk, ("fbrep",)], writes=[skey])
        e_dma(ctx, "scalar", aps["acc_f"][t3 * 128:(t3 + 1) * 128, :],
              stage[t3 % 2][:, :],
              reads=[("stage", t3 % 2, 0), ("stage", t3 % 2, 1)],
              writes=[("acc_f", t3)])


def build_l3(fctx=None, prefix="", x_in_ap=None, x_out_ap=None):
    nc = fctx.nc if fctx is not None else bass.Bass(
        "TRN2", target_bir_lowering=False)
    g = lambda n, sh: nc.dram_tensor(prefix + n, sh, F32,
                                     kind="ExternalInput").ap()
    gb = lambda n, sh: nc.dram_tensor(prefix + n, sh, BF16,
                                      kind="ExternalInput").ap()
    aps = {
        "x_in": x_in_ap if x_in_ap is not None else g("x_in", [T, D]),
        "W_qkv": g("W_qkv", [D, 3 * D]),
        "bqkv_p": g("bqkv_p", [64, 48]),
        "bqkv_rows": g("bqkv_rows", [16, 192]),
        "wo_head": g("wo_head", [64, 16 * D]),
        "bo_rep": g("bo_rep", [128, D]),
        "fw": g("fw", [D, D]),
        "fb_rep": g("fb_rep", [128, D]),
        "ones_f32": g("ones_f32", [128, 136]),
        "ident": gb("ident", [128, 128]),
        "ones_bf": gb("ones_bf", [128, 640]),
        "diagmask": gb("diagmask", [128, 4 * 512]),
        "row1": gb("row1", [128, 512]),
    }
    aps["diagmask"] = aps["diagmask"].rearrange("p (m q) -> p m q", m=4)
    aps["x_out"] = x_out_ap if x_out_ap is not None else nc.dram_tensor(
        prefix + "x_out", [T, D], F32, kind="ExternalOutput").ap()
    acck = "ExternalOutput" if DBG else "Internal"
    aps["acc_a"] = nc.dram_tensor(prefix + "acc_a", [T, D], F32,
                                  kind=acck).ap()
    aps["acc_f"] = nc.dram_tensor(prefix + "acc_f", [T, D], F32,
                                  kind=acck).ap()

    def body(ctx):
        sched = ctx.s
        bufs = make_common_bufs(ctx, aps, 3)
        for nm, key, shp, dt_ in (
                ("ones_bf", "ones", [128, 640], BF16),
                ("ones_f32", "ones", [128, 136], F32),
                ("diagmask", "dmask", [128, 4, 512], BF16),
                ("row1", "row1", [128, 512], BF16),
                ("bo_rep", "borep", [128, D], F32),
                ("fb_rep", "fbrep", [128, D], F32),
                ("bqkv_p", "bqkvp", [64, 48], F32)):
            bufs[nm] = ctx.sb(f"sb_{nm}", shp, dt_)
            e_dma(ctx, "gpsimd",
                  bufs[nm][:, :, :] if len(shp) == 3 else bufs[nm][:, :],
                  aps[nm], writes=[(key,)])

        xn = ctx.sb("xn", [128, NCC, PADW], BF16)
        xnkey = "xn"
        e_memset(ctx, xn[:, :, 0:1], 0.0, writes=[(xnkey, "all")])
        e_memset(ctx, xn[:, :, 1 + T:PADW], 0.0, writes=[(xnkey, "all")])
        msb = ctx.sb("l3_msb", [128, NTT], F32)
        rsb = ctx.sb("l3_rsb", [128, NTT], F32)

        with ExitStack() as st:
            pctx = sub_ctx(ctx, st)
            eb = make_epi_bufs(pctx, 0)
            emit_ln(pctx, aps["x_in"], [], None, xn, xnkey,
                    stats=(msb, rsb), bufs={**bufs, **eb})
        sched.global_barrier()
        with ExitStack() as st:
            emit_l3_attn(sub_ctx(ctx, st), aps, xn, xnkey, bufs)
        sched.global_barrier()
        with ExitStack() as st:
            emit_l3_ffn(sub_ctx(ctx, st), aps, (msb, rsb), bufs)
        sched.global_barrier()
        with ExitStack() as st:
            fctx = sub_ctx(ctx, st)
            eb = make_epi_bufs(fctx, 2)
            emit_ln(fctx, aps["x_in"],
                    [(aps["acc_a"], "acc_a"), (aps["acc_f"], "acc_f")],
                    aps["x_out"], None, xnkey, bufs={**bufs, **eb})
    if fctx is None:
        with ExitStack() as stack:
            sched = Sched(nc, stack)
            ctx = Ctx(nc, sched, stack)
            body(ctx)
            sched.finish()
            with nc.Block() as block:
                sched.replay(block)
        return nc
    with ExitStack() as lst:
        body(sub_ctx(fctx, lst))
    fctx.s.global_barrier()



def make_common_bufs(ctx, aps, layer):
    bufs = {}
    bufs["ident"] = ctx.sb("sb_ident", [128, 128], BF16)
    e_dma(ctx, "gpsimd", bufs["ident"][:, :], aps["ident"], writes=[("ident",)])
    bufs["stage"] = [ctx.sb(f"stg{i}", [128, D], F32) for i in range(2)]
    return bufs


_EPI_N = [0]


def make_epi_bufs(ctx, nacc):
    n = _EPI_N[0]
    _EPI_N[0] += 1
    bufs = {}
    bufs["epi_a"] = [ctx.sb(f"epia{n}_{i}", [128, D], F32) for i in range(3)]
    bufs["epi_b"] = [[ctx.sb(f"epib{n}_{a}_{i}", [128, D], F32)
                      for i in range(2)] for a in range(nacc)]
    bufs["epi_m"] = [ctx.sb(f"epim{n}_{i}", [128, 4], F32) for i in range(2)]
    bufs["epi_xc"] = [ctx.sb(f"epixc{n}_{i}", [128, D], F32) for i in range(2)]
    bufs["epi_sq"] = [ctx.sb(f"episq{n}_{i}", [128, D], F32) for i in range(2)]
    bufs["epi_ss"] = [ctx.sb(f"episs{n}_{i}", [128, 1], F32) for i in range(2)]
    bufs["epi_tm"] = [ctx.sb(f"epitm{n}_{i}", [128, D], BF16) for i in range(2)]
    return bufs


def sub_ctx(ctx, stack):
    c = Ctx.__new__(Ctx)
    c.__dict__.update(ctx.__dict__)
    c.stack = stack
    return c


def build_l0(fctx=None, prefix="", x_in_ap=None, x_out_ap=None):
    nc = fctx.nc if fctx is not None else bass.Bass(
        "TRN2", target_bir_lowering=False)
    g = lambda n, sh: nc.dram_tensor(prefix + n, sh, F32,
                                     kind="ExternalInput").ap()
    aps = {
        "x_in": x_in_ap if x_in_ap is not None else g("x_in", [T, D]),
        "Wq": g("Wq", [D, D]), "Wk1": g("Wk1", [D, D]), "Wv": g("Wv", [D, D]),
        "Wk2": g("Wk2", [D, D]), "Wp": g("Wp", [D, D]),
        "WrT": g("WrT", [D, D]), "WkT": g("WkT", [D, D]), "WvT": g("WvT", [D, D]),
        "b_q": g("b_q", [128, NCC]), "b_k1": g("b_k1", [128, NCC]),
        "b_v": g("b_v", [128, NCC]), "b_k2": g("b_k2", [128, NCC]),
        "bp_rep": g("bp_rep", [128, D]),
        "mu_r": g("mu_r", [128, NCC]), "mu_k": g("mu_k", [128, NCC]),
        "mask_triu": g("mask_triu", [128, 128]),
        "mask_triu8": g("mask_triu8", [128, 128]),
    }
    aps["ident"] = nc.dram_tensor(prefix + "ident", [128, 128], BF16,
                                  kind="ExternalInput").ap()
    aps["x_out"] = x_out_ap if x_out_ap is not None else nc.dram_tensor(
        prefix + "x_out", [T, D], F32, kind="ExternalOutput").ap()
    acck = "ExternalOutput" if DBG else "Internal"
    aps["acc_a"] = nc.dram_tensor(prefix + "acc_a", [T, D], F32,
                                  kind=acck).ap()
    aps["acc_f"] = nc.dram_tensor(prefix + "acc_f", [T, D], F32,
                                  kind=acck).ap()
    if DBG:
        for nm, sh, dt in (("dbg_xn", [128, NCC, PADW], BF16),
                           ("dbg_q", [128, T], BF16), ("dbg_k1", [128, T], BF16),
                           ("dbg_v", [128, T], BF16), ("dbg_k2", [128, T], BF16),
                           ("dbg_q2", [128, T], BF16), ("dbg_o1", [128, T], BF16),
                           ("dbg_y", [128, NCC, T], BF16),
                           ("dbg_xr", [128, NCC, 512], BF16),
                           ("dbg_xk", [128, NCC, 512], BF16),
                           ("dbg_kact", [128, NCC, 512], BF16),
                           ("dbg_rtm", [128, 4, 1024], BF16)):
            aps[nm] = nc.dram_tensor(nm, sh, dt, kind="ExternalOutput").ap()

    def body(ctx):
        sched = ctx.s
        bufs = make_common_bufs(ctx, aps, 0)
        for nm, key in (("b_q", "bias_l0"), ("b_k1", "bias_l0"),
                        ("b_v", "bias_l0"), ("b_k2", "b_k2"),
                        ("mu_r", "mu"), ("mu_k", "mu")):
            bufs[nm] = ctx.sb(f"sb_{nm}", [128, NCC], F32)
            e_dma(ctx, "gpsimd", bufs[nm][:, :], aps[nm], writes=[(key,)])
        bufs["bp_rep"] = ctx.sb("sb_bp_rep", [128, D], F32)
        e_dma(ctx, "gpsimd", bufs["bp_rep"][:, :], aps["bp_rep"],
              writes=[("bp_rep",)])
        bufs["mask_triu"] = ctx.sb("sb_mask", [128, 128], F32)
        e_dma(ctx, "gpsimd", bufs["mask_triu"][:, :], aps["mask_triu"],
              writes=[("mask",)])
        bufs["mask_triu8"] = ctx.sb("sb_mask8", [128, 128], F32)
        e_dma(ctx, "gpsimd", bufs["mask_triu8"][:, :], aps["mask_triu8"],
              writes=[("mask",)])

        xn = ctx.sb("xn", [128, NCC, PADW], BF16)
        xnkey = "xn"
        e_memset(ctx, xn[:, :, 0:1], 0.0, writes=[(xnkey, "all")])
        e_memset(ctx, xn[:, :, 1 + T:PADW], 0.0, writes=[(xnkey, "all")])

        with ExitStack() as st:
            pctx = sub_ctx(ctx, st)
            eb = make_epi_bufs(pctx, 0)
            emit_ln(pctx, aps["x_in"], [], None, xn, xnkey, bufs={**bufs, **eb})
        sched.global_barrier()
        if DBG:
            e_dma(ctx, "sync", aps["dbg_xn"], xn[:, :, :], reads=xn_all(xnkey))
        with ExitStack() as attn_stack:
            emit_l0_attn(sub_ctx(ctx, attn_stack), aps, xn, xnkey, bufs)
        sched.global_barrier()
        with ExitStack() as ffn_stack:
            emit_l0_ffn(sub_ctx(ctx, ffn_stack), aps, xn, xnkey, bufs)
        sched.global_barrier()
        with ExitStack() as st:
            fctx = sub_ctx(ctx, st)
            eb = make_epi_bufs(fctx, 2)
            emit_ln(fctx, aps["x_in"],
                    [(aps["acc_a"], "acc_a"), (aps["acc_f"], "acc_f")],
                    aps["x_out"], None, xnkey, bufs={**bufs, **eb})
    if fctx is None:
        with ExitStack() as stack:
            sched = Sched(nc, stack)
            ctx = Ctx(nc, sched, stack)
            body(ctx)
            sched.finish()
            with nc.Block() as block:
                sched.replay(block)
        return nc
    with ExitStack() as lst:
        body(sub_ctx(fctx, lst))
    fctx.s.global_barrier()


def build_l1(fctx=None, prefix="", x_in_ap=None, x_out_ap=None):
    nc = fctx.nc if fctx is not None else bass.Bass(
        "TRN2", target_bir_lowering=False)
    g = lambda n, sh: nc.dram_tensor(prefix + n, sh, F32,
                                     kind="ExternalInput").ap()
    gb = lambda n, sh: nc.dram_tensor(prefix + n, sh, BF16,
                                      kind="ExternalInput").ap()
    aps = {
        "x_in": x_in_ap if x_in_ap is not None else g("x_in", [T, D]),
        "wqkvT": g("wqkvT", [D, 1536]),
        "w_outT": g("w_outT", [512, D]),
        "bo_rep": g("bo_rep", [128, D]),
        "gwT": g("gwT", [D, 8]),
        "gb_rep": g("gb_rep", [128, 8]),
        "w1a": g("w1a", [9 * 1152, 1408]),
        "w3a": g("w3a", [9 * 1152, 1408]),
        "w2a": g("w2a", [9 * 1536, 1024]),
        "ones_f32": g("ones_f32", [128, 136]),
        "ident": gb("ident", [128, 128]),
        "ones_bf": gb("ones_bf", [128, 640]),
        "diagmask": gb("diagmask", [128, 4 * 512]),
        "row1": gb("row1", [128, 512]),
    }
    aps["diagmask"] = aps["diagmask"].rearrange("p (m q) -> p m q", m=4)
    aps["x_out"] = x_out_ap if x_out_ap is not None else nc.dram_tensor(
        prefix + "x_out", [T, D], F32, kind="ExternalOutput").ap()
    acck = "ExternalOutput" if DBG else "Internal"
    aps["acc_a"] = nc.dram_tensor(prefix + "acc_a", [T, D], F32,
                                  kind=acck).ap()
    aps["acc_f"] = nc.dram_tensor(prefix + "acc_f", [T, D], F32,
                                  kind=acck).ap()

    def body(ctx):
        sched = ctx.s
        bufs = make_common_bufs(ctx, aps, 1)
        for nm, key in (("ones_bf", "ones"), ("ones_f32", "ones"),
                        ("diagmask", "dmask"), ("row1", "row1"),
                        ("bo_rep", "borep")):
            shp = {"ones_bf": [128, 640], "ones_f32": [128, 136],
                   "diagmask": [128, 4, 512], "row1": [128, 512],
                   "bo_rep": [128, D]}[nm]
            dt_ = F32 if nm in ("ones_f32", "bo_rep") else BF16
            bufs[nm] = ctx.sb(f"sb_{nm}", shp, dt_)
            e_dma(ctx, "gpsimd",
                  bufs[nm][:, :, :] if len(shp) == 3 else bufs[nm][:, :],
                  aps[nm], writes=[(key,)])

        xn = ctx.sb("xn", [128, NCC, PADW], BF16)
        xnkey = "xn"
        e_memset(ctx, xn[:, :, 0:1], 0.0, writes=[(xnkey, "all")])
        e_memset(ctx, xn[:, :, 1 + T:PADW], 0.0, writes=[(xnkey, "all")])

        with ExitStack() as st:
            pctx = sub_ctx(ctx, st)
            eb = make_epi_bufs(pctx, 0)
            emit_ln(pctx, aps["x_in"], [], None, xn, xnkey, bufs={**bufs, **eb})
        sched.global_barrier()
        with ExitStack() as st:
            emit_l1_attn(sub_ctx(ctx, st), aps, xn, xnkey, bufs)
        sched.global_barrier()
        with ExitStack() as st:
            emit_l1_ffn(sub_ctx(ctx, st), aps, xn, xnkey, bufs)
        sched.global_barrier()
        with ExitStack() as st:
            fctx = sub_ctx(ctx, st)
            eb = make_epi_bufs(fctx, 2)
            emit_ln(fctx, aps["x_in"],
                    [(aps["acc_a"], "acc_a"), (aps["acc_f"], "acc_f")],
                    aps["x_out"], None, xnkey, bufs={**bufs, **eb})
    if fctx is None:
        with ExitStack() as stack:
            sched = Sched(nc, stack)
            ctx = Ctx(nc, sched, stack)
            body(ctx)
            sched.finish()
            with nc.Block() as block:
                sched.replay(block)
        return nc
    with ExitStack() as lst:
        body(sub_ctx(fctx, lst))
    fctx.s.global_barrier()



def vec_p(v):
    return np.ascontiguousarray(v.reshape(NCC, 128).T)


def host_consts():
    c = {}
    c["ident"] = np.eye(128, dtype=BF)
    tri = np.triu(np.ones((128, 128), np.float32))
    c["mask_triu"] = tri
    c["mask_triu8"] = (tri * 0.125).astype(np.float32)
    c["ones_bf"] = np.ones((128, 640), dtype=BF)
    c["ones_f32"] = np.ones((128, 136), dtype=np.float32)
    p = np.arange(128)[:, None, None]
    mm_ = np.arange(4)[None, :, None]
    q = np.arange(512)[None, None, :]
    c["diagmask"] = ((128 * mm_ + p) <= q).astype(BF).reshape(128, 4 * 512)
    r1 = np.zeros((128, 512), dtype=BF)
    r1[0, :] = 1
    c["row1"] = r1
    return c


def _aug_pack(w, b, rows_to):
    K, N = w.shape
    out = np.zeros((rows_to, N), w.dtype)
    out[:K] = w
    out[K] = b
    return out


def l1_inputs(inputs, xb):
    c = host_consts()
    w1a = np.concatenate(
        [_aug_pack(inputs["L1_ffn_w1"][e], inputs["L1_ffn_b1"][e], 1152)
         for e in range(8)] +
        [_aug_pack(inputs["L1_ffn_sw1"], inputs["L1_ffn_sb1"], 1152)], axis=0)
    w3a = np.concatenate(
        [_aug_pack(inputs["L1_ffn_w3"][e], inputs["L1_ffn_b3"][e], 1152)
         for e in range(8)] +
        [_aug_pack(inputs["L1_ffn_sw3"], inputs["L1_ffn_sb3"], 1152)], axis=0)
    w2a = np.concatenate(
        [_aug_pack(inputs["L1_ffn_w2"][e], inputs["L1_ffn_b2"][e], 1536)
         for e in range(8)] +
        [_aug_pack(inputs["L1_ffn_sw2"], inputs["L1_ffn_sb2"], 1536)], axis=0)
    return {
        "x_in": np.ascontiguousarray(xb),
        "wqkvT": np.ascontiguousarray(inputs["L1_attn_w_qkv"].T),
        "w_outT": np.ascontiguousarray(inputs["L1_attn_w_out"].T),
        "bo_rep": np.ascontiguousarray(
            np.broadcast_to(inputs["L1_attn_b_out"], (128, D))),
        "gwT": np.ascontiguousarray(inputs["L1_ffn_gate_w"].T),
        "gb_rep": np.ascontiguousarray(
            np.broadcast_to(inputs["L1_ffn_gate_b"], (128, 8))),
        "w1a": w1a, "w3a": w3a, "w2a": w2a,
        "ident": c["ident"], "ones_bf": c["ones_bf"],
        "ones_f32": c["ones_f32"], "diagmask": c["diagmask"],
        "row1": c["row1"],
    }


def l0_inputs(inputs, xb):
    c = host_consts()
    return {
        "x_in": np.ascontiguousarray(xb),
        "Wq": inputs["L0_attn_Wq"], "Wk1": inputs["L0_attn_Wk1"],
        "Wv": inputs["L0_attn_Wv"], "Wk2": inputs["L0_attn_Wk2"],
        "Wp": inputs["L0_attn_Wp"],
        "WrT": np.ascontiguousarray(inputs["L0_ffn_W_r"].T),
        "WkT": np.ascontiguousarray(inputs["L0_ffn_W_k"].T),
        "WvT": np.ascontiguousarray(inputs["L0_ffn_W_v"].T),
        "b_q": vec_p(inputs["L0_attn_bq"]), "b_k1": vec_p(inputs["L0_attn_bk1"]),
        "b_v": vec_p(inputs["L0_attn_bv"]), "b_k2": vec_p(inputs["L0_attn_bk2"]),
        "bp_rep": np.ascontiguousarray(
            np.broadcast_to(inputs["L0_attn_bp"], (128, D))),
        "mu_r": vec_p(inputs["L0_ffn_mu_r"]), "mu_k": vec_p(inputs["L0_ffn_mu_k"]),
        "ident": c["ident"], "mask_triu": c["mask_triu"],
        "mask_triu8": c["mask_triu8"],
    }


def l3_inputs(inputs, xb):
    c = host_consts()
    return {
        "x_in": np.ascontiguousarray(xb),
        "W_qkv": inputs["L3_attn_W_qkv"],
        "bqkv_p": np.ascontiguousarray(
            inputs["L3_attn_b_qkv"].reshape(16, 3, 64).transpose(2, 0, 1)
            .reshape(64, 48)),
        "bqkv_rows": np.ascontiguousarray(
            inputs["L3_attn_b_qkv"].reshape(16, 192)),
        "wo_head": np.ascontiguousarray(
            inputs["L3_attn_W_out"].reshape(16, 64, D).transpose(1, 0, 2)
            .reshape(64, 16 * D)),
        "bo_rep": np.ascontiguousarray(
            np.broadcast_to(inputs["L3_attn_b_out"], (128, D))),
        "fw": inputs["L3_ffn_weight"],
        "fb_rep": np.ascontiguousarray(
            np.broadcast_to(inputs["L3_ffn_bias"], (128, D))),
        "ident": c["ident"], "ones_bf": c["ones_bf"],
        "ones_f32": c["ones_f32"], "diagmask": c["diagmask"],
        "row1": c["row1"],
    }


def build_fused():
    nc = bass.Bass("TRN2", target_bir_lowering=False)
    x_in = nc.dram_tensor("x_in", [T, D], F32, kind="ExternalInput").ap()
    x_out = nc.dram_tensor("x_out", [T, D], F32, kind="ExternalOutput").ap()
    xmid = [nc.dram_tensor(f"xmid{i}", [T, D], F32).ap() for i in range(3)]
    chain = [x_in] + xmid + [x_out]
    with ExitStack() as stack:
        sched = Sched(nc, stack)
        ctx = Ctx(nc, sched, stack)
        build_l0(ctx, "L0_", chain[0], chain[1])
        build_l1(ctx, "L1_", chain[1], chain[2])
        build_l2(ctx, "L2_", chain[2], chain[3])
        build_l3(ctx, "L3_", chain[3], chain[4])
        sched.finish()
        with nc.Block() as block:
            sched.replay(block)
    return nc


def fused_inputs(inputs, xb):
    d = {"x_in": np.ascontiguousarray(xb)}
    for k, prep in enumerate((l0_inputs, l1_inputs, l2_inputs, l3_inputs)):
        for name, arr in prep(inputs, xb).items():
            if name != "x_in":
                d[f"L{k}_{name}"] = arr
    return d


FUSED = True


def kernel(**inputs):
    inputs = {k: np.asarray(v) for k, v in inputs.items()}
    x = inputs["L0_attn_x"]
    if FUSED:
        nc = build_fused()
        in_maps = [fused_inputs(inputs, x[b]) for b in range(8)]
        res = run_bass_kernel_spmd(nc, in_maps, core_ids=list(range(8)))
        return np.stack([res.results[b]["x_out"] for b in range(8)], axis=0)
    nc = build_l0()
    in_maps = [l0_inputs(inputs, x[b]) for b in range(8)]
    res = run_bass_kernel_spmd(nc, in_maps, core_ids=list(range(8)))
    x = np.stack([res.results[b]["x_out"] for b in range(8)], axis=0)
    for build, prep in ((build_l1, l1_inputs), (build_l2, l2_inputs),
                        (build_l3, l3_inputs)):
        nc = build()
        in_maps = [prep(inputs, x[b]) for b in range(8)]
        res = run_bass_kernel_spmd(nc, in_maps, core_ids=list(range(8)))
        x = np.stack([res.results[b]["x_out"] for b in range(8)], axis=0)
    return x


# version 11
# speedup vs baseline: 1.1580x; 1.1580x over previous
import numpy as np
import ml_dtypes
from contextlib import ExitStack

DBG = False
SERIALIZE_DMA = False

import concourse.bass as bass
import concourse.mybir as mybir
from concourse.bass_utils import run_bass_kernel_spmd

F32 = mybir.dt.float32
BF16 = mybir.dt.bfloat16
AF = mybir.ActivationFunctionType
ALU = mybir.AluOpType
AX = mybir.AxisListType
BF = ml_dtypes.bfloat16

T = 4096
D = 1024
NCC = D // 128
PADW = 4104
NTT = T // 128
NT5 = T // 512


ENGINES = ("tensor", "vector", "scalar", "gpsimd", "sync")
DMA_ENGINES = ("gpsimd", "sync", "scalar")


NDSLOT = 8


class Sched:
    def __init__(self, nc, stack):
        self.nc = nc
        self.q = {e: [] for e in ENGINES}
        self.cnt = {e: 0 for e in ENGINES}
        self.sem = {e: stack.enter_context(nc.semaphore(f"se_{e}")) for e in ENGINES}
        self.dsem = {e: [stack.enter_context(nc.semaphore(f"sd_{e}{i}"))
                         for i in range(NDSLOT)] for e in DMA_ENGINES}
        self.duse = {e: [0] * NDSLOT for e in DMA_ENGINES}
        self.didx = {e: 0 for e in DMA_ENGINES}
        self.known = {e: {} for e in ENGINES}
        self.lastw = {}
        self.readers = {}
        self.nwaits = 0

    def _wait(self, e, tok):
        kind, se, val = tok
        if kind == "e" and se == e == "tensor":
            return
        k = (kind, se)
        if self.known[e].get(k, 0) >= val:
            return
        self.known[e][k] = val
        sem = self.sem[se] if kind == "e" else self.dsem[se[0]][se[1]]
        self.q[e].append(("w", sem, val))
        self.nwaits += 1

    def emit(self, e, fn, reads=(), writes=(), dma=False, defer_inc=False):
        deps = []
        for r in reads:
            w = self.lastw.get(r)
            if w is not None:
                deps.append(w)
        for wk in writes:
            w = self.lastw.get(wk)
            if w is not None:
                deps.append(w)
            rd = self.readers.get(wk)
            if rd:
                for (kind, se), val in rd.items():
                    deps.append((kind, se, val))
        if e == "tensor" and not dma and getattr(self, "defer_pending", False):
            need = False
            for kind, se, val in deps:
                if kind == "e" and se == e == "tensor":
                    continue
                k = (kind, se)
                if self.known[e].get(k, 0) < val:
                    need = True
                    break
            if need:
                self.cnt[e] += 1
                self.q[e].append(("i", lambda eng: eng.nop(),
                                  self.sem[e], 1))
                self.defer_pending = False
        for d in deps:
            self._wait(e, d)
        if dma:
            slot = self.didx[e] % NDSLOT
            self.didx[e] += 1
            if self.duse[e][slot]:
                self._wait(e, ("d", (e, slot), 16 * self.duse[e][slot]))
            self.duse[e][slot] += 1
            tok = ("d", (e, slot), 16 * self.duse[e][slot])
            self.q[e].append(("i", fn, self.dsem[e][slot], 16))
            if SERIALIZE_DMA:
                self._wait(e, tok)
        elif defer_inc:
            tok = ("e", e, self.cnt[e] + 1)
            self.q[e].append(("n", fn))
            self.defer_pending = True
        else:
            self.cnt[e] += 1
            tok = ("e", e, self.cnt[e])
            self.q[e].append(("i", fn, self.sem[e], 1))
            if e == "tensor":
                self.defer_pending = False
        for r in reads:
            d = self.readers.setdefault(r, {})
            k = tok[:2]
            if d.get(k, 0) < tok[2]:
                d[k] = tok[2]
        for wk in writes:
            self.lastw[wk] = tok
            self.readers[wk] = {}
        return tok

    def barrier_on(self, e, toks):
        for tok in toks:
            self._wait(e, tok)

    def flush_defer(self):
        if getattr(self, "defer_pending", False):
            self.cnt["tensor"] += 1
            self.q["tensor"].append(
                ("i", lambda eng: eng.nop(), self.sem["tensor"], 1))
            self.defer_pending = False

    def global_barrier(self):
        self.flush_defer()
        snap_e = {e: self.cnt[e] for e in ENGINES}
        snap_d = {e: list(self.duse[e]) for e in DMA_ENGINES}
        for e in ENGINES:
            for o in ENGINES:
                if snap_e[o]:
                    self._wait(e, ("e", o, snap_e[o]))
            for o in DMA_ENGINES:
                for i in range(NDSLOT):
                    if snap_d[o][i]:
                        self._wait(e, ("d", (o, i), 16 * snap_d[o][i]))

    def finish(self):
        if getattr(self, "defer_pending", False):
            self.cnt["tensor"] += 1
            self.q["tensor"].append(
                ("i", lambda eng: eng.nop(), self.sem["tensor"], 1))
            self.defer_pending = False
        for e in DMA_ENGINES:
            for i in range(NDSLOT):
                if self.duse[e][i]:
                    self.q["sync"].append(("w", self.dsem[e][i],
                                           16 * self.duse[e][i]))
        for e in ENGINES:
            if e == "sync":
                continue
            if self.cnt[e]:
                self.q["sync"].append(("w", self.sem[e], self.cnt[e]))

    def replay(self, block):
        def mk(items):
            def f(eng):
                for it in items:
                    if it[0] == "w":
                        eng.wait_ge(it[1], it[2])
                    elif it[0] == "n":
                        it[1](eng)
                    else:
                        it[1](eng).then_inc(it[2], it[3])
            return f
        block.tensor(mk(self.q["tensor"]))
        block.vector(mk(self.q["vector"]))
        block.scalar(mk(self.q["scalar"]))
        block.gpsimd(mk(self.q["gpsimd"]))
        block.sync(mk(self.q["sync"]))


class Ctx:

    def __init__(self, nc, sched, stack):
        self.nc = nc
        self.s = sched
        self.stack = stack
        self.pf = [stack.enter_context(nc.psum_tensor(f"pf{i}", [128, 512], F32))
                   for i in range(6)]
        self.pb = [stack.enter_context(nc.psum_tensor(f"pb{i}", [128, 1024], BF16))
                   for i in range(2)]
        self._rot_f = 0
        self._rot_b = 0

    def pfree(self):
        i = self._rot_f % 3
        self._rot_f += 1
        return self.pf[i], ("psF", i)

    def pbank(self, i):
        return self.pf[i], ("psF", i)

    def pfree4(self):
        self._rot_f += 1
        i = self._rot_f % 4
        return self.pf[i], ("psF", i)

    def pbf(self):
        i = self._rot_b % 2
        self._rot_b += 1
        return self.pb[i], ("psB", i)

    _N = [0]

    def sb(self, name, shape, dtype):
        Ctx._N[0] += 1
        return self.stack.enter_context(
            self.nc.sbuf_tensor(f"{name}_{Ctx._N[0]}", shape, dtype))



def e_dma(ctx, qeng, dst, src, reads=(), writes=(), accum=False):
    if accum:
        return ctx.s.emit(qeng,
                          lambda e: e.dma_start(dst, src, accum_op=ALU.add),
                          reads=reads, writes=writes, dma=True)
    return ctx.s.emit(qeng, lambda e: e.dma_start(dst, src),
                      reads=reads, writes=writes, dma=True)


def e_mm(ctx, out, lhsT, rhs, start, stop, reads=(), writes=(), tile_position=None):
    return ctx.s.emit(
        "tensor",
        lambda e: e.matmul(out, lhsT, rhs, start=start, stop=stop,
                           tile_position=tile_position),
        reads=reads, writes=writes)


def e_tr(ctx, out, in_, ident, reads=(), writes=()):
    return ctx.s.emit("tensor", lambda e: e.transpose(out, in_, ident),
                      reads=reads, writes=writes)


def _imm(v):
    return np.float32(v) if isinstance(v, float) else v


def e_act(ctx, out, in_, func, reads=(), writes=(), bias=0.0, scale=1.0,
          accum_out=None):
    bias, scale = _imm(bias), _imm(scale)
    return ctx.s.emit(
        "scalar",
        lambda e: e.activation(out, in_, func, bias=bias, scale=scale,
                               accum_out=accum_out),
        reads=reads, writes=writes)


def e_tt(ctx, op, out, a, b, reads=(), writes=(), eng="vector"):
    return ctx.s.emit(eng, lambda e: e.tensor_tensor(out, a, b, op),
                      reads=reads, writes=writes)


def e_ts(ctx, out, in0, s1, op0, s2=None, op1=ALU.bypass, reads=(), writes=(),
         eng="vector", accum_out=None):
    return ctx.s.emit(
        eng,
        lambda e: e.tensor_scalar(out, in0, s1, s2, op0, op1,
                                  accum_out=accum_out),
        reads=reads, writes=writes)


def e_stt(ctx, out, in0, scalar, in1, op0, op1, reads=(), writes=(),
          eng="vector", accum_out=None):
    return ctx.s.emit(
        eng,
        lambda e: e.scalar_tensor_tensor(out, in0, scalar, in1, op0, op1,
                                         accum_out=accum_out),
        reads=reads, writes=writes)


def e_copy(ctx, out, in_, reads=(), writes=(), eng="vector"):
    return ctx.s.emit(eng, lambda e: e.tensor_copy(out, in_),
                      reads=reads, writes=writes)


def e_red(ctx, out, in_, op, reads=(), writes=(), negate=None):
    return ctx.s.emit(
        "vector",
        lambda e: e.tensor_reduce(out, in_, AX.X, op, negate=negate),
        reads=reads, writes=writes)


def e_memset(ctx, ap, val, reads=(), writes=(), eng="vector"):
    return ctx.s.emit(eng, lambda e: e.memset(ap, val),
                      reads=reads, writes=writes)


def e_scan(ctx, out, d0, d1, init, op0, op1, reads=(), writes=()):
    return ctx.s.emit(
        "vector",
        lambda e: e.tensor_tensor_scan(out, d0, d1, init, op0, op1),
        reads=reads, writes=writes)



def load_wcols(ctx, name, w_ap, c0, c1, o0, o1, part=128):
    ncc = (c1 - c0) // part
    wb = ctx.sb(name, [part, ncc, o1 - o0], BF16)
    key = ("w", name)
    src = w_ap[c0:c1, o0:o1].rearrange("(cc p) o -> p cc o", p=part)
    e_dma(ctx, "gpsimd", wb[:, :, :], src, writes=[key])
    return wb, key


class WPool:

    def __init__(self, ctx, name, nslots, shape):
        self.ctx = ctx
        self.bufs = [ctx.sb(f"{name}{i}", shape, BF16) for i in range(nslots)]
        self.keys = [(f"wp_{name}", i) for i in range(nslots)]
        self.i = 0

    def load(self, src_ap):
        i = self.i % len(self.bufs)
        self.i += 1
        wb, key = self.bufs[i], self.keys[i]
        shp = src_ap.shape
        if len(wb.shape) == 3:
            dst = wb[:, 0:shp[1], 0:shp[2]]
        else:
            dst = wb[:, 0:shp[1]]
        e_dma(self.ctx, "gpsimd", dst, src_ap, writes=[key])
        return wb, key


def wcols_src(w_ap, c0, c1, o0, o1, part=128):
    return w_ap[c0:c1, o0:o1].rearrange("(cc p) o -> p cc o", p=part)


def emit_ln(ctx, x_in, accs, x_out, xn, xnkey, stats=None, bufs=None):
    nc = ctx.nc
    ident = bufs["ident"]
    xa = bufs["epi_a"]
    nacc = len(accs)
    for tt in range(NTT):
        t0 = tt * 128
        sl = (slice(t0, t0 + 128), slice(0, D))
        ka = ("epi_a", tt % 3)
        e_dma(ctx, "sync", xa[tt % 3][:, :], x_in[sl], writes=[ka])
        for ai, (acc, acck) in enumerate(accs):
            kb = ("epi_b", ai, tt % 2)
            e_dma(ctx, "sync", bufs["epi_b"][ai][tt % 2][:, :], acc[sl],
                  writes=[kb], reads=[(acck, tt)])
            e_tt(ctx, ALU.add, xa[tt % 3][:, :], xa[tt % 3][:, :],
                 bufs["epi_b"][ai][tt % 2][:, :], reads=[ka, kb], writes=[ka])
        if x_out is not None:
            e_dma(ctx, "scalar", x_out[sl], xa[tt % 3][:, :], reads=[ka])
        if xn is None:
            continue
        sl2 = tt % 2
        mkey = ("epi_m", sl2)
        m = bufs["epi_m"][sl2]
        e_red(ctx, m[:, 0:1], xa[tt % 3][:, :], ALU.add, reads=[ka], writes=[mkey])
        e_ts(ctx, m[:, 1:2], m[:, 0:1], 1.0 / D, ALU.mult, reads=[mkey],
             writes=[mkey])
        xc = bufs["epi_xc"][sl2]
        e_ts(ctx, xc[:, :], xa[tt % 3][:, :], m[:, 1:2], ALU.subtract,
             reads=[ka, mkey], writes=[("epi_xc", sl2)])
        e_act(ctx, bufs["epi_sq"][sl2][:, :], xc[:, :], AF.Square,
              reads=[("epi_xc", sl2)], writes=[("epi_sq", sl2), ("epi_ss", sl2)],
              accum_out=bufs["epi_ss"][sl2][:, 0:1])
        e_act(ctx, m[:, 2:3], bufs["epi_ss"][sl2][:, 0:1], AF.Ln, bias=1e-5,
              scale=1.0 / D, reads=[("epi_ss", sl2)], writes=[mkey])
        e_act(ctx, m[:, 3:4], m[:, 2:3], AF.Exp, scale=-0.5, reads=[mkey],
              writes=[mkey])
        if stats is not None:
            e_copy(ctx, stats[0][:, tt:tt + 1], m[:, 1:2], reads=[mkey],
                   writes=[("stats",)])
            e_copy(ctx, stats[1][:, tt:tt + 1], m[:, 3:4], reads=[mkey],
                   writes=[("stats",)])
        xtm = bufs["epi_tm"][sl2]
        e_ts(ctx, xtm[:, :], xc[:, :], m[:, 3:4], ALU.mult,
             reads=[("epi_xc", sl2), mkey], writes=[("epi_tm", sl2)])
        pbt, pbk = ctx.pbf()
        for cc in range(NCC):
            e_tr(ctx, pbt[:, cc * 128:(cc + 1) * 128],
                 xtm[:, cc * 128:(cc + 1) * 128], ident[:, :],
                 reads=[("epi_tm", sl2), ("ident",)], writes=[pbk])
        e_copy(ctx, xn[:, :, 1 + t0:1 + t0 + 128],
               pbt[:, :].rearrange("p (cc t) -> p cc t", cc=NCC),
               reads=[pbk], writes=[(xnkey, tt)])


def xn_all(xnkey):
    return [(xnkey, tt) for tt in range(NTT)] + [(xnkey, "all")]



def emit_l0_attn(ctx, aps, xn, xnkey, bufs):
    nc = ctx.nc
    ident = bufs["ident"]
    XN = xn_all(xnkey)

    qp = ctx.sb("l0_qp", [128, T], BF16)
    k1p = ctx.sb("l0_k1p", [128, T], BF16)
    vp = ctx.sb("l0_vp", [128, T], BF16)
    k2p = ctx.sb("l0_k2p", [128, T], BF16)
    q2p = ctx.sb("l0_q2p", [128, T], BF16)
    o1p = ctx.sb("l0_o1p", [128, T], BF16)
    y = ctx.sb("l0_y", [128, NCC, T], BF16)
    kt = ctx.sb("l0_kt", [128, 128], BF16)
    vt = ctx.sb("l0_vt", [128, 128], BF16)
    k2t = ctx.sb("l0_k2t", [128, 128], BF16)
    xnt = ctx.sb("l0_xnt", [128, 128], BF16)
    et = ctx.sb("l0_et", [128, 128], BF16)
    asb = ctx.sb("l0_asb", [128, 2, 128], BF16)
    s1sb = ctx.sb("l0_s1sb", [128, 64], BF16)
    s2sb = ctx.sb("l0_s2sb", [128, 64], BF16)
    s1f = ctx.sb("l0_s1f", [128, 64], F32)
    s2f = ctx.sb("l0_s2f", [128, 64], F32)

    wq = WPool(ctx, "l0wq", 2, [128, NCC, 128])
    mask1 = bufs["mask_triu"]
    maskq2 = bufs["mask_triu8"]

    bk2s = ctx.sb("l0_bk2s", [128, NCC], F32)
    e_ts(ctx, bk2s[:, :], bufs["b_k2"][:, :], 0.02 / 32.0, ALU.mult,
         reads=[("b_k2",)], writes=[("bk2s",)])

    for pr in range(8):
        projs = [("q", aps["Wq"], qp, ("l0qp",), bufs["b_q"], "idb"),
                 ("k1", aps["Wk1"], k1p, ("l0k1p",), bufs["b_k1"], "idb"),
                 ("v", aps["Wv"], vp, ("l0vp",), bufs["b_v"], "idb"),
                 ("k2", aps["Wk2"], k2p, ("l0k2p",), bk2s, "sig")]
        for pname, wap, dst, dkey, bvec, mode in projs:
            wb, wkey = wq.load(wcols_src(wap, 0, D, pr * 128, pr * 128 + 128))
            for tt in range(NT5):
                ps, pk = ctx.pfree()
                for cc in range(NCC):
                    e_mm(ctx, ps[:, :], wb[:, cc, :],
                         xn[:, cc, 1 + tt * 512:1 + tt * 512 + 512],
                         start=(cc == 0), stop=(cc == NCC - 1),
                         reads=[wkey, (xnkey, "all")] + XN, writes=[pk])
                if mode == "idb":
                    e_act(ctx, dst[:, tt * 512:(tt + 1) * 512], ps[:, :],
                          AF.Identity, bias=bvec[:, pr:pr + 1],
                          reads=[pk, ("bias_l0",)], writes=[dkey])
                else:
                    e_act(ctx, dst[:, tt * 512:(tt + 1) * 512], ps[:, :],
                          AF.Sigmoid, bias=bvec[:, pr:pr + 1], scale=0.000625,
                          reads=[pk, ("bk2s",)], writes=[dkey])
        e_stt(ctx, q2p[:, :], qp[:, :], 0.02, qp[:, :], ALU.mult, ALU.min,
              reads=[("l0qp",)], writes=[("l0q2p",)])

        stb = [ctx.pbank(3), ctx.pbank(4)]
        e_memset(ctx, s1f[:, :], 0.0, writes=[("s1f",)])
        for c in range(32):
            lo = c * 128
            cs = slice(lo, lo + 128)
            pb, pbk = ctx.pbf()
            e_tr(ctx, pb[:, 0:128], k1p[:, cs], ident[:, :],
                 reads=[("l0k1p",), ("ident",)], writes=[pbk])
            e_tr(ctx, pb[:, 128:256], vp[:, cs], ident[:, :],
                 reads=[("l0vp",), ("ident",)], writes=[pbk])
            e_copy(ctx, kt[:, :], pb[:, 0:128], reads=[pbk], writes=[("kt",)])
            e_copy(ctx, vt[:, :], pb[:, 128:256], reads=[pbk], writes=[("vt",)])
            psO, kO = ctx.pbank(5)
            HS = [slice(0, 64), slice(64, 128)]
            psA2 = []
            for hl in range(2):
                hsl = HS[hl]
                if c > 0:
                    e_act(ctx, s1sb[hsl, :], s1f[hsl, :], AF.Identity,
                          reads=[("s1f",)], writes=[("s1sb", hl)])
                psA, kA = ctx.pfree()
                e_mm(ctx, psA[:, 0:128], k1p[hsl, cs], qp[hsl, cs],
                     start=True, stop=True, reads=[("l0k1p",), ("l0qp",)],
                     writes=[kA])
                psA2.append((psA, kA))
            for hl in range(2):
                e_tt(ctx, ALU.mult, asb[:, hl, :], psA2[hl][0][:, 0:128],
                     mask1[:, :], reads=[psA2[hl][1], ("mask",)],
                     writes=[("asb", hl)])
            for hl in range(2):
                hb = 64 * hl
                hsl = HS[hl]
                e_mm(ctx, psO[hsl, 0:128], vt[:, hb:hb + 64], asb[:, hl, :],
                     start=True, stop=(c == 0),
                     reads=[("vt",), ("asb", hl)], writes=[kO])
                if c > 0:
                    e_mm(ctx, psO[hsl, 0:128], s1sb[hsl, 0:64], qp[hsl, cs],
                         start=False, stop=True,
                         reads=[("s1sb", hl), ("l0qp",)], writes=[kO])
            for hl in range(2):
                hb = 64 * hl
                hsl = HS[hl]
                psS1, kS1 = stb[hl]
                e_mm(ctx, psS1[hsl, 0:64], kt[:, hb:hb + 64], vt[:, hb:hb + 64],
                     start=True, stop=True,
                     reads=[("kt",), ("vt",)], writes=[kS1])
            for hl in range(2):
                hsl = HS[hl]
                psS1, kS1 = stb[hl]
                e_tt(ctx, ALU.add, s1f[hsl, :], s1f[hsl, :], psS1[hsl, 0:64],
                     reads=[("s1f",), kS1], writes=[("s1f",)])
            e_act(ctx, o1p[:, cs], psO[:, 0:128], AF.Identity,
                  reads=[kO], writes=[("o1p",)])

        st2 = [ctx.pbank(3), ctx.pbank(4)]
        e_memset(ctx, s2f[:, :], 0.0, writes=[("s2f",)])
        cc = pr
        for c in range(32):
            lo = c * 128
            hi = min(lo + 128, T - 1)
            n = hi - lo
            ns = slice(lo, lo + n)
            pb, pbk = ctx.pbf()
            e_tr(ctx, pb[0:n, 0:128], k2p[:, ns], ident[:, :],
                 reads=[("l0k2p",), ("ident",)], writes=[pbk])
            e_tr(ctx, pb[0:n, 128:256], xn[:, cc, 2 + lo:2 + lo + n],
                 ident[:, :], reads=XN + [("ident",)], writes=[pbk])
            e_tr(ctx, pb[0:n, 256:384], o1p[:, ns], ident[:, :],
                 reads=[("o1p",), ("ident",)], writes=[pbk])
            e_copy(ctx, k2t[0:n, :], pb[0:n, 0:128], reads=[pbk],
                   writes=[("k2t",)])
            e_copy(ctx, xnt[0:n, :], pb[0:n, 128:256], reads=[pbk],
                   writes=[("xnt",)])
            e_tt(ctx, ALU.subtract, et[0:n, :], xnt[0:n, :],
                 pb[0:n, 256:384], reads=[("xnt",), pbk], writes=[("et",)])
            psO, kO = ctx.pbank(5)
            for hl in range(2):
                hb = 64 * hl
                hsl = slice(hb, hb + 64)
                psS2, kS2 = st2[hl]
                if c > 0:
                    e_act(ctx, s2sb[hsl, :], s2f[hsl, :], AF.Identity,
                          scale=0.125, reads=[("s2f",)], writes=[("s2sb", hl)])
                psA, kA = ctx.pfree()
                e_mm(ctx, psA[0:n, 0:n], k2p[hsl, ns], q2p[hsl, ns],
                     start=True, stop=True, reads=[("l0k2p",), ("l0q2p",)],
                     writes=[kA])
                e_tt(ctx, ALU.mult, asb[0:n, hl, 0:n], psA[0:n, 0:n],
                     maskq2[0:n, 0:n], reads=[kA, ("mask",)],
                     writes=[("asb", hl)])
                e_mm(ctx, psO[hsl, 0:n], et[0:n, hb:hb + 64], asb[0:n, hl, 0:n],
                     start=True, stop=(c == 0),
                     reads=[("et",), ("asb", hl)], writes=[kO])
                if c > 0:
                    e_mm(ctx, psO[hsl, 0:n], s2sb[hsl, 0:64], q2p[hsl, ns],
                         start=False, stop=True,
                         reads=[("s2sb", hl), ("l0q2p",)], writes=[kO])
                e_mm(ctx, psS2[hsl, 0:64], k2t[0:n, hb:hb + 64],
                     et[0:n, hb:hb + 64], start=True, stop=True,
                     reads=[("k2t",), ("et",)], writes=[kS2])
                e_tt(ctx, ALU.add, s2f[hsl, :], s2f[hsl, :], psS2[hsl, 0:64],
                     reads=[("s2f",), kS2], writes=[("s2f",)])
            e_tt(ctx, ALU.add, y[:, cc, lo + 1:lo + 1 + n], psO[:, 0:n],
                 o1p[:, lo + 1:lo + 1 + n], reads=[kO, ("o1p",)],
                 writes=[("y", cc)])
        e_copy(ctx, y[:, cc, 0:1], o1p[:, 0:1], reads=[("o1p",)],
               writes=[("y", cc)])
        if DBG and pr == 0:
            e_dma(ctx, "sync", aps["dbg_q"], qp[:, :], reads=[("l0qp",)])
            e_dma(ctx, "sync", aps["dbg_k1"], k1p[:, :], reads=[("l0k1p",)])
            e_dma(ctx, "sync", aps["dbg_v"], vp[:, :], reads=[("l0vp",)])
            e_dma(ctx, "sync", aps["dbg_k2"], k2p[:, :], reads=[("l0k2p",)])
            e_dma(ctx, "sync", aps["dbg_q2"], q2p[:, :], reads=[("l0q2p",)])
            e_dma(ctx, "sync", aps["dbg_o1"], o1p[:, :], reads=[("o1p",)])

    if DBG:
        e_dma(ctx, "sync", aps["dbg_y"], y[:, :, :],
              reads=[("y", c) for c in range(NCC)])
    wpool = WPool(ctx, "l0wp", 1, [128, NCC, 512])
    stage = bufs["stage"]
    for ob in range(2):
        wb, wkey = wpool.load(wcols_src(aps["Wp"], 0, D, ob * 512, ob * 512 + 512))
        for tt in range(NTT):
            ps, pk = ctx.pfree()
            for cc in range(NCC):
                e_mm(ctx, ps[:, :], y[:, cc, tt * 128:(tt + 1) * 128],
                     wb[:, cc, :], start=(cc == 0), stop=(cc == NCC - 1),
                     reads=[wkey, ("y", cc)], writes=[pk])
            skey = ("stage", tt % 2, ob)
            e_tt(ctx, ALU.add, stage[tt % 2][:, ob * 512:(ob + 1) * 512],
                 ps[:, :], bufs["bp_rep"][:, ob * 512:(ob + 1) * 512],
                 reads=[pk, ("bp_rep",)], writes=[skey])
            e_dma(ctx, "scalar",
                  aps["acc_a"][tt * 128:(tt + 1) * 128, ob * 512:(ob + 1) * 512],
                  stage[tt % 2][:, ob * 512:(ob + 1) * 512], reads=[skey],
                  writes=[("acc_a", tt)])


def emit_l0_ffn(ctx, aps, xn, xnkey, bufs):
    XN = xn_all(xnkey)
    xr = ctx.sb("l0_xr", [128, NCC, 512], BF16)
    xk = ctx.sb("l0_xk", [128, NCC, 512], BF16)
    tmp = ctx.sb("l0_tmp", [128, 512], F32)
    kact = ctx.sb("l0_kact", [128, NCC, 512], BF16)
    krel = ctx.sb("l0_krel", [128, 512], BF16)
    rtm = ctx.sb("l0_rtm", [128, 4, 1024], BF16)
    stage = bufs["stage"]

    wr = [load_wcols(ctx, f"l0wr{ob}", aps["WrT"], 0, D, ob * 512, ob * 512 + 512)
          for ob in range(2)]
    wk = [load_wcols(ctx, f"l0wk{ob}", aps["WkT"], 0, D, ob * 512, ob * 512 + 512)
          for ob in range(2)]
    wv = [load_wcols(ctx, f"l0wv{ob}", aps["WvT"], 0, D, ob * 512, ob * 512 + 512)
          for ob in range(2)]

    for tt in range(NT5):
        t0 = tt * 512
        for cc in range(NCC):
            cur = xn[:, cc, 1 + t0:1 + t0 + 512]
            prev = xn[:, cc, t0:t0 + 512]
            e_tt(ctx, ALU.subtract, tmp[:, :], cur, prev,
                 reads=XN, writes=[("l0tmp",)])
            e_stt(ctx, xr[:, cc, :], tmp[:, :], bufs["mu_r"][:, cc:cc + 1],
                  prev, ALU.mult, ALU.add,
                  reads=[("l0tmp",), ("mu",)] + XN, writes=[("l0xr", cc)])
            e_stt(ctx, xk[:, cc, :], tmp[:, :], bufs["mu_k"][:, cc:cc + 1],
                  prev, ALU.mult, ALU.add,
                  reads=[("l0tmp",), ("mu",)] + XN, writes=[("l0xk", cc)])
        if DBG and tt == 0:
            e_dma(ctx, "sync", aps["dbg_xr"], xr[:, :, :],
                  reads=[("l0xr", c) for c in range(NCC)])
            e_dma(ctx, "sync", aps["dbg_xk"], xk[:, :, :],
                  reads=[("l0xk", c) for c in range(NCC)])
        for t4 in range(4):
            tsl = slice(t4 * 128, t4 * 128 + 128)
            for ob in range(2):
                ps, pk = ctx.pfree()
                for cc in range(NCC):
                    e_mm(ctx, ps[:, :], xr[:, cc, tsl], wr[ob][0][:, cc, :],
                         start=(cc == 0), stop=(cc == NCC - 1),
                         reads=[("l0xr", cc), wr[ob][1]], writes=[pk])
                e_act(ctx, rtm[:, t4, ob * 512:(ob + 1) * 512], ps[:, :],
                      AF.Sigmoid, reads=[pk], writes=[("rtm", t4)])
        for ko in range(NCC):
            ps, pk = ctx.pfree()
            for cc in range(NCC):
                e_mm(ctx, ps[:, :], wk[ko // 4][0][:, cc,
                                                   (ko % 4) * 128:(ko % 4) * 128 + 128],
                     xk[:, cc, :], start=(cc == 0), stop=(cc == NCC - 1),
                     reads=[("l0xk", cc), wk[ko // 4][1]], writes=[pk])
            e_act(ctx, krel[:, :], ps[:, :], AF.Relu, reads=[pk],
                  writes=[("krel",)])
            e_act(ctx, kact[:, ko, :], krel[:, :], AF.Square,
                  reads=[("krel",)], writes=[("l0kact", ko)])
        if DBG and tt == 0:
            e_dma(ctx, "sync", aps["dbg_kact"], kact[:, :, :],
                  reads=[("l0kact", c) for c in range(NCC)])
            e_dma(ctx, "sync", aps["dbg_rtm"], rtm[:, :, :],
                  reads=[("rtm", t) for t in range(4)])
        for t4 in range(4):
            tsl = slice(t4 * 128, t4 * 128 + 128)
            for ob in range(2):
                ps, pk = ctx.pfree()
                for ko in range(NCC):
                    e_mm(ctx, ps[:, :], kact[:, ko, tsl], wv[ob][0][:, ko, :],
                         start=(ko == 0), stop=(ko == NCC - 1),
                         reads=[("l0kact", ko), wv[ob][1]], writes=[pk])
                skey = ("stage", t4 % 2, ob)
                e_tt(ctx, ALU.mult, stage[t4 % 2][:, ob * 512:(ob + 1) * 512],
                     ps[:, :], rtm[:, t4, ob * 512:(ob + 1) * 512],
                     reads=[pk, ("rtm", t4)], writes=[skey])
            e_dma(ctx, "scalar",
                  aps["acc_f"][t0 + t4 * 128:t0 + t4 * 128 + 128, :],
                  stage[t4 % 2][:, :],
                  reads=[("stage", t4 % 2, 0), ("stage", t4 % 2, 1)],
                  writes=[("acc_f", tt * 4 + t4)])



RS512 = float(1.0 / np.sqrt(512.0))


def emit_l1_attn(ctx, aps, xn, xnkey, bufs):
    XN = xn_all(xnkey)
    onesb = bufs["ones_bf"]
    onesf = bufs["ones_f32"]
    dmask = bufs["diagmask"]

    qf = ctx.sb("l1_qf", [128, 4, T], BF16)
    kf = ctx.sb("l1_kf", [128, 4, T], BF16)
    vtm = ctx.sb("l1_vtm", [128, NTT, 520], BF16)
    nug = ctx.sb("l1_nug", [1, T], BF16)
    kms = ctx.sb("l1_kms", [1, 4], F32)
    stage = bufs["stage"]

    with ExitStack() as wst:
        wctx = sub_ctx(ctx, wst)
        sq = wctx.sb("l1_sq", [128, 512], BF16)
        rowA = wctx.sb("l1_rowA", [1, T], BF16)
        rowB = nug
        wpool = WPool(wctx, "l1wq", 1, [128, NCC, 256])
        for kind, dst, dkey, scl, base in (("q", qf, ("l1qf",), RS512, 0),
                                           ("k", kf, ("l1kf",), 1.0, 512)):
            for qc in range(4):
                wb, wkey = wpool.load(
                    wcols_src(aps["wqkvT"], 0, D,
                              base + qc * 128, base + qc * 128 + 128))
                for tt in range(NT5):
                    ps, pk = ctx.pfree()
                    for cc in range(NCC):
                        e_mm(ctx, ps[:, :], wb[:, cc, 0:128],
                             xn[:, cc, 1 + tt * 512:1 + tt * 512 + 512],
                             start=(cc == 0), stop=(cc == NCC - 1),
                             reads=[wkey] + XN, writes=[pk])
                    e_act(ctx, dst[:, qc, tt * 512:(tt + 1) * 512], ps[:, :],
                          AF.Identity, scale=scl, reads=[pk], writes=[dkey])
        for vh in range(2):
            wv, wvk = wpool.load(wcols_src(aps["wqkvT"], 0, D,
                                           1024 + vh * 256, 1024 + vh * 256 + 256))
            for t3 in range(NTT):
                ps, pk = ctx.pfree()
                for cc in range(NCC):
                    e_mm(ctx, ps[:, 0:256],
                         xn[:, cc, 1 + t3 * 128:1 + t3 * 128 + 128],
                         wv[:, cc, 0:256], start=(cc == 0),
                         stop=(cc == NCC - 1), reads=[wvk] + XN, writes=[pk])
                e_act(ctx, vtm[:, t3, vh * 256:vh * 256 + 256], ps[:, 0:256],
                      AF.Identity, reads=[pk], writes=[("l1vtm",)])
        e_memset(ctx, vtm[:, :, 512:513], 1.0, writes=[("l1vtm",)])
        def norm2_row(srcf, srckey, dst, dkey):
            for tt in range(NT5):
                ps, pk = ctx.pfree()
                for qc in range(4):
                    e_act(ctx, sq[:, :], srcf[:, qc, tt * 512:(tt + 1) * 512],
                          AF.Square, reads=[srckey], writes=[("l1sq",)])
                    e_mm(ctx, ps[0:1, 0:512], onesb[:, 0:1], sq[:, :],
                         start=(qc == 0), stop=(qc == 3),
                         reads=[("l1sq",), ("ones",)], writes=[pk])
                e_copy(ctx, dst[0:1, tt * 512:(tt + 1) * 512], ps[0:1, 0:512],
                       reads=[pk], writes=[dkey])

        norm2_row(kf, ("l1kf",), rowA, ("l1rowA",))
        e_red(ctx, kms[0:1, 0:1], rowA[0:1, :], ALU.max, reads=[("l1rowA",)],
              writes=[("l1kms",)])
        e_act(ctx, kms[0:1, 1:2], kms[0:1, 0:1], AF.Ln, reads=[("l1kms",)],
              writes=[("l1kms",)])
        e_act(ctx, kms[0:1, 2:3], kms[0:1, 1:2], AF.Exp, scale=0.5,
              reads=[("l1kms",)], writes=[("l1kms",)])
        e_ts(ctx, kms[0:1, 3:4], kms[0:1, 2:3], -1.0, ALU.mult,
             reads=[("l1kms",)], writes=[("l1kms",)])
        norm2_row(qf, ("l1qf",), rowA, ("l1rowA",))
        e_act(ctx, rowB[0:1, :], rowA[0:1, :], AF.Ln, reads=[("l1rowA",)],
              writes=[("l1nug",)])
        e_act(ctx, rowA[0:1, :], rowB[0:1, :], AF.Exp, scale=0.5,
              reads=[("l1nug",)], writes=[("l1rowA",)])
        e_ts(ctx, nug[0:1, :], rowA[0:1, :], kms[0:1, 3:4], ALU.mult,
             reads=[("l1rowA",), ("l1kms",)], writes=[("l1nug",)])

    ctx.s.global_barrier()

    lsb = ctx.sb("l1_lsb", [1, 512], F32)
    linv = ctx.sb("l1_linv", [1, 512], F32)
    rep = ctx.sb("l1_rep", [128, 512], F32)
    pt = [ctx.sb(f"l1_pt{i}", [128, 512], BF16) for i in range(2)]
    osb = ctx.sb("l1_osb", [128, 4, 512], BF16)
    wo = [load_wcols(ctx, f"l1wo{ob}", aps["w_outT"], 0, 512,
                     ob * 512, ob * 512 + 512) for ob in range(2)]

    for qg in range(8):
        kcs = 4 * (qg + 1)
        tqs = slice(qg * 512, (qg + 1) * 512)
        e_memset(ctx, lsb[0:1, :], 0.0, writes=[("l1lsb",)])
        def l1_s(kci):
            Sb, Sk = ctx.pbank(4 + kci % 2)
            for dc in range(4):
                e_mm(ctx, Sb[:, :], kf[:, dc, kci * 128:(kci + 1) * 128],
                     qf[:, dc, tqs], start=(dc == 0), stop=False,
                     reads=[("l1kf",), ("l1qf",)], writes=[Sk])
            e_mm(ctx, Sb[:, :], onesb[0:1, 0:128], nug[0:1, tqs],
                 start=False, stop=True,
                 reads=[("ones",), ("l1nug",)], writes=[Sk])
            ptb = pt[kci % 2]
            ptk = ("l1pt", kci % 2)
            e_act(ctx, ptb[:, :], Sb[:, :], AF.Exp, reads=[Sk], writes=[ptk])
            if kci >= kcs - 4:
                e_tt(ctx, ALU.mult, ptb[:, :], ptb[:, :],
                     dmask[:, kci - (kcs - 4), :], reads=[ptk, ("dmask",)],
                     writes=[ptk])

        def l1_pv(kci):
            Sb, Sk = ctx.pbank(4 + kci % 2)
            ptb = pt[kci % 2]
            ptk = ("l1pt", kci % 2)
            for oc in range(4):
                po, pok = ctx.pbank(oc)
                e_mm(ctx, po[:, :], vtm[:, kci, oc * 128:(oc + 1) * 128],
                     ptb[:, :], start=(kci == 0), stop=(kci == kcs - 1),
                     reads=[("l1vtm",), ptk], writes=[pok])
            e_mm(ctx, Sb[0:1, 0:512], onesb[:, 0:1], ptb[:, :],
                 start=True, stop=True, reads=[("ones",), ptk], writes=[Sk])
            e_tt(ctx, ALU.add, lsb[0:1, :], lsb[0:1, :], Sb[0:1, 0:512],
                 reads=[("l1lsb",), Sk], writes=[("l1lsb",)])

        l1_s(0)
        for kci in range(1, kcs):
            l1_s(kci)
            l1_pv(kci - 1)
        l1_pv(kcs - 1)
        e_act(ctx, linv[0:1, :], lsb[0:1, :], AF.Ln, reads=[("l1lsb",)],
              writes=[("l1linv",)])
        e_act(ctx, linv[0:1, :], linv[0:1, :], AF.Exp, scale=-1.0,
              reads=[("l1linv",)], writes=[("l1linv",)])
        pr_, prk = ctx.pbank(4)
        e_mm(ctx, pr_[:, :], onesf[0:1, 0:128], linv[0:1, :],
             start=True, stop=True, reads=[("ones",), ("l1linv",)],
             writes=[prk])
        e_copy(ctx, rep[:, :], pr_[:, :], reads=[prk], writes=[("l1rep",)])
        for oc in range(4):
            po, pok = ctx.pbank(oc)
            e_tt(ctx, ALU.mult, osb[:, oc, :], po[:, :], rep[:, :],
                 reads=[pok, ("l1rep",)], writes=[("l1osb",)])
        for t4 in range(4):
            for ob in range(2):
                ps, pk = ctx.pbank(4 + ob)
                for oc in range(4):
                    e_mm(ctx, ps[:, :], osb[:, oc, t4 * 128:(t4 + 1) * 128],
                         wo[ob][0][:, oc, :], start=(oc == 0), stop=(oc == 3),
                         reads=[("l1osb",), wo[ob][1]], writes=[pk])
                skey = ("stage", t4 % 2, ob)
                e_tt(ctx, ALU.add, stage[t4 % 2][:, ob * 512:(ob + 1) * 512],
                     ps[:, :], bufs["bo_rep"][:, ob * 512:(ob + 1) * 512],
                     reads=[pk, ("borep",)], writes=[skey])
            tt = qg * 4 + t4
            e_dma(ctx, "scalar", aps["acc_a"][tt * 128:(tt + 1) * 128, :],
                  stage[t4 % 2][:, :],
                  reads=[("stage", t4 % 2, 0), ("stage", t4 % 2, 1)],
                  writes=[("acc_a", tt)])


def emit_l1_ffn(ctx, aps, xn, xnkey, bufs):
    XN = xn_all(xnkey)
    row1 = bufs["row1"]
    stage = bufs["stage"]

    gwb, gwk = load_wcols(ctx, "l1gw", aps["gwT"], 0, D, 0, 8)
    gbr = ctx.sb("l1_gbr", [128, 8], F32)
    e_dma(ctx, "gpsimd", gbr[:, :], aps["gb_rep"], writes=[("l1gbr",)])
    rw = ctx.sb("l1_rw", [128, NTT, 8], F32)
    e8 = ctx.sb("l1_e8", [128, 8], F32)
    sc8 = ctx.sb("l1_sc8", [128, 8], F32)
    bi8 = ctx.sb("l1_bi8", [128, 8], F32)
    g8 = ctx.sb("l1_g8", [128, 8], F32)
    r1 = ctx.sb("l1_r1", [128, 4], F32)

    for t3 in range(NTT):
        ps, pk = ctx.pfree()
        for cc in range(NCC):
            e_mm(ctx, ps[:, 0:8], xn[:, cc, 1 + t3 * 128:1 + t3 * 128 + 128],
                 gwb[:, cc, :], start=(cc == 0), stop=(cc == NCC - 1),
                 reads=[gwk] + XN, writes=[pk])
        e_red(ctx, r1[:, 0:1], ps[:, 0:8], ALU.max, negate=True,
              reads=[pk], writes=[("l1r1",)])
        e_act(ctx, e8[:, :], ps[:, 0:8], AF.Exp, bias=r1[:, 0:1],
              reads=[pk, ("l1r1",)], writes=[("l1e8",)])
        e_red(ctx, r1[:, 1:2], e8[:, :], ALU.add, reads=[("l1e8",)],
              writes=[("l1r1",)])
        ctx.s.emit("vector", lambda e, o=r1[:, 2:3], i=r1[:, 1:2]:
                   e.reciprocal(o, i), reads=[("l1r1",)], writes=[("l1r1",)])
        e_ts(ctx, sc8[:, :], e8[:, :], r1[:, 2:3], ALU.mult,
             reads=[("l1e8",), ("l1r1",)], writes=[("l1sc8",)])
        e_tt(ctx, ALU.add, bi8[:, :], sc8[:, :], gbr[:, :],
             reads=[("l1sc8",), ("l1gbr",)], writes=[("l1bi8",)])
        for e in range(8):
            e_ts(ctx, g8[:, :], bi8[:, :], bi8[:, e:e + 1], ALU.is_gt,
                 reads=[("l1bi8",)], writes=[("l1g8",)])
            e_red(ctx, r1[:, 3:4], g8[:, :], ALU.add, reads=[("l1g8",)],
                  writes=[("l1r1",)])
            e_ts(ctx, r1[:, 3:4], r1[:, 3:4], 2.0, ALU.is_lt,
                 reads=[("l1r1",)], writes=[("l1r1",)])
            e_ts(ctx, rw[:, t3, e:e + 1], sc8[:, e:e + 1], r1[:, 3:4],
                 ALU.mult, reads=[("l1sc8",), ("l1r1",)], writes=[("l1rw",)])

    w1b = ctx.sb("l1_w1b", [128, 9, 1408], BF16)
    w3b = ctx.sb("l1_w3b", [128, 9, 1408], BF16)
    w2b = ctx.sb("l1_w2b", [128, 12, 1024], BF16)
    hmid = ctx.sb("l1_hmid", [128, 12, 512], BF16)
    usb = ctx.sb("l1_usb", [128, 512], BF16)
    tmp = ctx.sb("l1_tmp", [128, 512], F32)
    e_memset(ctx, hmid[:, 11, :], 0.0, writes=[("l1hm", 11)])
    e_memset(ctx, hmid[0:1, 11, :], 1.0, writes=[("l1hm", 11)])

    for e in range(9):
        s1 = aps["w1a"][e * 1152:(e + 1) * 1152, :]
        s3 = aps["w3a"][e * 1152:(e + 1) * 1152, :]
        s2 = aps["w2a"][e * 1536:(e + 1) * 1536, :]
        e_dma(ctx, "gpsimd", w1b[:, :, :],
              s1.rearrange("(cc p) o -> p cc o", p=128), writes=[("l1w1b",)])
        e_dma(ctx, "gpsimd", w3b[:, :, :],
              s3.rearrange("(cc p) o -> p cc o", p=128), writes=[("l1w3b",)])
        e_dma(ctx, "gpsimd", w2b[:, :, :],
              s2.rearrange("(cc p) o -> p cc o", p=128), writes=[("l1w2b",)])
        for tt in range(NT5):
            t0 = tt * 512
            for hc in range(11):
                psU, pkU = ctx.pbank(tt % 2)
                psV, pkV = ctx.pbank(2 + tt % 2)
                for cc in range(9):
                    rhs = (xn[:, cc, 1 + t0:1 + t0 + 512] if cc < 8
                           else row1[:, :])
                    rds = (XN if cc < 8 else [("row1",)])
                    e_mm(ctx, psU[:, :], w1b[:, cc, hc * 128:(hc + 1) * 128],
                         rhs, start=(cc == 0), stop=(cc == 8),
                         reads=[("l1w1b",)] + rds, writes=[pkU])
                e_act(ctx, usb[:, :], psU[:, :], AF.Identity, reads=[pkU],
                      writes=[("l1usb",)])
                for cc in range(9):
                    rhs = (xn[:, cc, 1 + t0:1 + t0 + 512] if cc < 8
                           else row1[:, :])
                    rds = (XN if cc < 8 else [("row1",)])
                    e_mm(ctx, psV[:, :], w3b[:, cc, hc * 128:(hc + 1) * 128],
                         rhs, start=(cc == 0), stop=(cc == 8),
                         reads=[("l1w3b",)] + rds, writes=[pkV])
                e_tt(ctx, ALU.mult, tmp[:, :], psV[:, :], usb[:, :],
                     reads=[pkV, ("l1usb",)], writes=[("l1tmp",)])
                e_act(ctx, hmid[:, hc, :], tmp[:, :], AF.Silu,
                      reads=[("l1tmp",)], writes=[("l1hm", hc)])
            for t4 in range(4):
                t3 = tt * 4 + t4
                for ob in range(2):
                    ps, pk = ctx.pbank(4 + ob)
                    for hc in range(12):
                        e_mm(ctx, ps[:, :],
                             hmid[:, hc, t4 * 128:(t4 + 1) * 128],
                             w2b[:, hc, ob * 512:(ob + 1) * 512],
                             start=(hc == 0), stop=(hc == 11),
                             reads=[("l1w2b",), ("l1hm", hc)], writes=[pk])
                    skey = ("stage", t4 % 2, ob)
                    if e < 8:
                        e_ts(ctx, stage[t4 % 2][:, ob * 512:(ob + 1) * 512],
                             ps[:, :], rw[:, t3, e:e + 1], ALU.mult,
                             reads=[pk, ("l1rw",)], writes=[skey])
                    else:
                        e_copy(ctx, stage[t4 % 2][:, ob * 512:(ob + 1) * 512],
                               ps[:, :], reads=[pk], writes=[skey])
                e_dma(ctx, "gpsimd",
                      aps["acc_f"][t3 * 128:(t3 + 1) * 128, :],
                      stage[t4 % 2][:, :],
                      reads=[("stage", t4 % 2, 0), ("stage", t4 % 2, 1)],
                      writes=[("acc_f", t3)], accum=(e > 0))



def emit_l2_attn(ctx, aps, xn, xnkey, bufs):
    XN = xn_all(xnkey)
    onesb = bufs["ones_bf"]
    onesf = bufs["ones_f32"]
    nmask = bufs["negmask"]
    stage = bufs["stage"]

    with ExitStack() as st:
        cctx = sub_ctx(ctx, st)
        fa = cctx.sb("l2_fa", [16, T], F32)
        fb = cctx.sb("l2_fb", [16, T], F32)
        zr = cctx.sb("l2_zr", [16, 512], F32)
        csp = cctx.sb("l2_csp", [16, T], BF16)
        wf, wfk = load_wcols(cctx, "l2wf", aps["Wf"], 0, D, 0, 16)
        e_memset(cctx, zr[:, :], 0.0, writes=[("l2zr",)])
        for tt in range(NT5):
            ps, pk = ctx.pfree()
            for cc in range(NCC):
                e_mm(ctx, ps[0:16, :], wf[:, cc, :],
                     xn[:, cc, 1 + tt * 512:1 + tt * 512 + 512],
                     start=(cc == 0), stop=(cc == NCC - 1),
                     reads=[wfk] + XN, writes=[pk])
            e_copy(ctx, fa[:, tt * 512:(tt + 1) * 512], ps[0:16, :],
                   reads=[pk], writes=[("l2fa",)])
        e_act(ctx, fb[:, :], fa[:, :], AF.Abs, reads=[("l2fa",)],
              writes=[("l2fb",)])
        e_act(ctx, fb[:, :], fb[:, :], AF.Exp, scale=-1.0, reads=[("l2fb",)],
              writes=[("l2fb",)])
        e_act(ctx, fb[:, :], fb[:, :], AF.Ln, bias=1.0, reads=[("l2fb",)],
              writes=[("l2fb",)])
        e_ts(ctx, fa[:, :], fa[:, :], 0.0, ALU.min, reads=[("l2fa",)],
             writes=[("l2fa",)])
        e_tt(ctx, ALU.subtract, fa[:, :], fa[:, :], fb[:, :],
             reads=[("l2fa",), ("l2fb",)], writes=[("l2fa",)])
        for tt in range(NT5):
            sl = slice(tt * 512, (tt + 1) * 512)
            init = 0.0 if tt == 0 else fb[:, tt * 512 - 1:tt * 512]
            e_scan(ctx, fb[:, sl], fa[:, sl], zr[:, :], init,
                   ALU.add, ALU.add, reads=[("l2fa",), ("l2zr",), ("l2fb",)],
                   writes=[("l2fb",)])
        e_dma(ctx, "scalar", bufs["l2_c"][:, :], fb[:, :], reads=[("l2fb",)],
              writes=[("l2c",)])
        e_ts(ctx, fa[:, :], fb[:, :], -1.0, ALU.mult, reads=[("l2fb",)],
             writes=[("l2fa",)])
        for s_ in range(3):
            e_copy(ctx, csp[:, :], fa[:, :], reads=[("l2fa",)],
                   writes=[("l2csp",)])
            e_dma(ctx, "scalar", bufs["l2_csp"][:, s_ * T:(s_ + 1) * T],
                  csp[:, :], reads=[("l2csp",)], writes=[("l2cspP",)])
            if s_ < 2:
                e_tt(ctx, ALU.subtract, fa[:, :], fa[:, :], csp[:, :],
                     reads=[("l2fa",), ("l2csp",)], writes=[("l2fa",)])
    ctx.s.global_barrier()
    cfull = bufs["l2_c"]
    cspP = bufs["l2_csp"]
    kaug = ctx.sb("l2_kaug", [64, T], BF16)
    qaug = ctx.sb("l2_qaug", [64, 2048], BF16)
    kaux3 = ctx.sb("l2_kaux3", [3, T], BF16)
    ones4 = ctx.sb("l2_ones4", [4, 2048], BF16)
    e_memset(ctx, ones4[:, :], 1.0, writes=[("l2ones4",)])
    vtm = ctx.sb("l2_vtm", [128, NTT, 72], BF16)
    negu = ctx.sb("l2_negu", [1, 2048], BF16)
    rowA = ctx.sb("l2_rowA", [1, T], BF16)
    crow0 = ctx.sb("l2_crow0", [1, T], BF16)
    sq = ctx.sb("l2_sq", [128, 512], BF16)
    kms = ctx.sb("l2_kms", [1, 4], F32)
    linv = [ctx.sb("l2_linv", [128, 512], F32) for _ in range(2)]
    rep = [ctx.sb("l2_rep", [64, 512], F32) for _ in range(2)]
    osb = [ctx.sb("l2_osb", [64, 512], BF16) for _ in range(2)]
    pt = [ctx.sb(f"l2_pt{i}", [128, 512], BF16) for i in range(4)]
    outsb = ctx.sb("l2_outsb", [128, 16, D], BF16)
    wo_sb = ctx.sb("l2_wo", [64, D], BF16)
    wh = WPool(ctx, "l2wh", 2, [128, NCC, 192])

    for sweep in range(2):
        kr = 2048 * (sweep + 1)
        q0 = 2048 * sweep
        e_memset(ctx, outsb[:, :, :], 0.0, writes=[("l2out",)])
        for h in range(16):
            wb, wk = wh.load(wcols_src(aps["Wcat"], 0, D,
                                       192 * h, 192 * h + 192))
            e_dma(ctx, "gpsimd", wo_sb[:, :],
                  aps["wo_head"][:, h * D:(h + 1) * D], writes=[("l2wo",)])
            for tt in range(kr // 512):
                ps, pk = ctx.pfree()
                for cc in range(NCC):
                    e_mm(ctx, ps[0:64, :], wb[:, cc, 64:128],
                         xn[:, cc, 1 + tt * 512:1 + tt * 512 + 512],
                         start=(cc == 0), stop=(cc == NCC - 1),
                         reads=[wk] + XN, writes=[pk])
                e_act(ctx, kaug[:, tt * 512:(tt + 1) * 512], ps[0:64, :],
                      AF.Identity, reads=[pk], writes=[("l2k",)])
            e_dma(ctx, "sync", kaux3[0:3, 0:T],
                  cspP[h:h + 1, :].rearrange("h (s t) -> (h s) t", s=3),
                  reads=[("l2cspP",)], writes=[("l2kaux",)])
            e_dma(ctx, "gpsimd", crow0[0:1, :], cfull[h:h + 1, :],
                  reads=[("l2c",)], writes=[("l2crow0",)])
            for tt in range(4):
                ps, pk = ctx.pfree()
                for cc in range(NCC):
                    e_mm(ctx, ps[0:64, :], wb[:, cc, 0:64],
                         xn[:, cc, 1 + q0 + tt * 512:1 + q0 + tt * 512 + 512],
                         start=(cc == 0), stop=(cc == NCC - 1),
                         reads=[wk] + XN, writes=[pk])
                e_act(ctx, qaug[:, tt * 512:(tt + 1) * 512], ps[0:64, :],
                      AF.Identity, scale=0.125, reads=[pk], writes=[("l2q",)])

            for t3 in range(kr // 128):
                ps, pk = ctx.pfree()
                for cc in range(NCC):
                    e_mm(ctx, ps[:, 0:64],
                         xn[:, cc, 1 + t3 * 128:1 + t3 * 128 + 128],
                         wb[:, cc, 128:192], start=(cc == 0),
                         stop=(cc == NCC - 1), reads=[wk] + XN, writes=[pk])
                e_act(ctx, vtm[:, t3, 0:64], ps[:, 0:64], AF.Identity,
                      reads=[pk], writes=[("l2v",)])
            e_memset(ctx, vtm[:, 0:kr // 128, 64:65], 1.0, writes=[("l2v",)])
            for tt in range(kr // 512):
                ps, pk = ctx.pfree()
                e_act(ctx, sq[0:64, :], kaug[:, tt * 512:(tt + 1) * 512],
                      AF.Square, reads=[("l2k",)], writes=[("l2sq",)])
                e_mm(ctx, ps[0:1, 0:512], onesb[0:64, 0:1], sq[0:64, :],
                     start=True, stop=True, reads=[("l2sq",), ("ones",)],
                     writes=[pk])
                e_copy(ctx, rowA[0:1, tt * 512:(tt + 1) * 512],
                       ps[0:1, 0:512], reads=[pk], writes=[("l2rowA",)])
            e_red(ctx, kms[0:1, 0:1], rowA[0:1, 0:kr], ALU.max,
                  reads=[("l2rowA",)], writes=[("l2kms",)])
            e_act(ctx, kms[0:1, 1:2], kms[0:1, 0:1], AF.Ln,
                  reads=[("l2kms",)], writes=[("l2kms",)])
            e_act(ctx, kms[0:1, 2:3], kms[0:1, 1:2], AF.Exp, scale=0.5,
                  reads=[("l2kms",)], writes=[("l2kms",)])
            e_ts(ctx, kms[0:1, 3:4], kms[0:1, 2:3], -1.0, ALU.mult,
                 reads=[("l2kms",)], writes=[("l2kms",)])
            for tt in range(4):
                ps, pk = ctx.pfree()
                e_act(ctx, sq[0:64, :], qaug[:, tt * 512:(tt + 1) * 512],
                      AF.Square, reads=[("l2q",)], writes=[("l2sq",)])
                e_mm(ctx, ps[0:1, 0:512], onesb[0:64, 0:1], sq[0:64, :],
                     start=True, stop=True, reads=[("l2sq",), ("ones",)],
                     writes=[pk])
                e_copy(ctx, rowA[0:1, tt * 512:(tt + 1) * 512],
                       ps[0:1, 0:512], reads=[pk], writes=[("l2rowA",)])
            e_act(ctx, negu[0:1, :], rowA[0:1, 0:2048], AF.Ln,
                  reads=[("l2rowA",)], writes=[("l2negu",)])
            e_act(ctx, rowA[0:1, 0:2048], negu[0:1, :], AF.Exp, scale=0.5,
                  reads=[("l2negu",)], writes=[("l2rowA",)])
            e_ts(ctx, rowA[0:1, 0:2048], rowA[0:1, 0:2048], kms[0:1, 3:4],
                 ALU.mult, reads=[("l2rowA",), ("l2kms",)],
                 writes=[("l2rowA",)])
            e_tt(ctx, ALU.add, negu[0:1, :], rowA[0:1, 0:2048],
                 crow0[0:1, q0:q0 + 2048], reads=[("l2rowA",), ("l2crow0",)],
                 writes=[("l2negu",)])

            for lg in range(4):
                g = sweep * 4 + lg
                sl2 = lg % 2
                kcs = 4 * (g + 1)
                tqs = slice(lg * 512, (lg + 1) * 512)
                psO, pkO = ctx.pbank(4 + sl2)

                def l2_s(kci):
                    Sb, Sk = ctx.pbank(kci % 2)
                    e_mm(ctx, Sb[:, :], kaug[:, kci * 128:(kci + 1) * 128],
                         qaug[:, tqs], start=True, stop=False,
                         reads=[("l2k",), ("l2q",)], writes=[Sk])
                    e_mm(ctx, Sb[:, :], kaux3[:, kci * 128:(kci + 1) * 128],
                         ones4[0:3, tqs], start=False, stop=False,
                         reads=[("l2kaux",), ("l2ones4",)], writes=[Sk])
                    e_mm(ctx, Sb[:, :], onesb[0:1, 0:128], negu[0:1, tqs],
                         start=False, stop=True,
                         reads=[("ones",), ("l2negu",)], writes=[Sk])
                    if kci >= kcs - 4:
                        e_tt(ctx, ALU.add, Sb[:, :], Sb[:, :],
                             nmask[:, kci - (kcs - 4), :],
                             reads=[Sk, ("nmask",)], writes=[Sk])
                    ptb, ptk = pt[kci % 4], ("l2pt", kci % 4)
                    e_act(ctx, ptb[:, :], Sb[:, :], AF.Exp, reads=[Sk],
                          writes=[ptk])

                def l2_pv(kci):
                    ptb, ptk = pt[kci % 4], ("l2pt", kci % 4)
                    e_mm(ctx, psO[0:65, :], vtm[:, kci, 0:65], ptb[:, :],
                         start=(kci == 0), stop=(kci == kcs - 1),
                         reads=[("l2v",), ptk], writes=[pkO])

                l2_s(0)
                for kci in range(1, kcs):
                    l2_s(kci)
                    l2_pv(kci - 1)
                l2_pv(kcs - 1)
                e_act(ctx, linv[sl2][64:65, :], psO[64:65, :], AF.Ln,
                      reads=[pkO], writes=[("l2linv", sl2)])
                e_act(ctx, linv[sl2][64:65, :], linv[sl2][64:65, :], AF.Exp,
                      scale=-1.0, reads=[("l2linv", sl2)],
                      writes=[("l2linv", sl2)])
                psR, pkR = ctx.pfree()
                e_mm(ctx, psR[0:64, :], onesf[64:65, 0:64],
                     linv[sl2][64:65, :], start=True, stop=True,
                     reads=[("ones",), ("l2linv", sl2)], writes=[pkR])
                e_copy(ctx, rep[sl2][:, :], psR[0:64, :], reads=[pkR],
                       writes=[("l2rep", sl2)])
                e_tt(ctx, ALU.mult, osb[sl2][:, :], psO[0:64, :],
                     rep[sl2][:, :], reads=[pkO, ("l2rep", sl2)],
                     writes=[("l2osb", sl2)])
                for t4 in range(4):
                    for ob in range(2):
                        psP, pkP = ctx.pbank(2 + ob)
                        e_mm(ctx, psP[:, :],
                             osb[sl2][:, t4 * 128:(t4 + 1) * 128],
                             wo_sb[:, ob * 512:(ob + 1) * 512],
                             start=True, stop=True,
                             reads=[("l2osb", sl2), ("l2wo",)], writes=[pkP])
                        e_tt(ctx, ALU.add,
                             outsb[:, lg * 4 + t4, ob * 512:(ob + 1) * 512],
                             outsb[:, lg * 4 + t4, ob * 512:(ob + 1) * 512],
                             psP[:, :], reads=[("l2out",), pkP],
                             writes=[("l2out",)])
        for t in range(16):
            t3 = sweep * 16 + t
            skey = ("stage", t % 2, 0)
            e_copy(ctx, stage[t % 2][:, :], outsb[:, t, :],
                   reads=[("l2out",)], writes=[skey])
            e_dma(ctx, "scalar", aps["acc_a"][t3 * 128:(t3 + 1) * 128, :],
                  stage[t % 2][:, :], reads=[skey], writes=[("acc_a", t3)])



def _l2f_shift_rhs(xn, cc, c0, n):
    if cc < 4:
        return xn[:, cc, 1 + c0:1 + c0 + n]
    return xn[:, cc, c0:c0 + n]


def emit_l2_ffn(ctx, aps, xn, xnkey, bufs):
    XN = xn_all(xnkey)
    onesb = bufs["ones_bf"]
    onesf = bufs["ones_f32"]
    ident = bufs["ident"]
    nm_low = bufs["nm_low"]
    nm_up = bufs["nm_up"]
    row1 = bufs["row1"]
    stage = bufs["stage"]

    for l in range(2):
        with ExitStack() as ast:
            actx = sub_ctx(ctx, ast)
            qf = actx.sb("l2f_qf", [65, T], BF16)
            xcop = actx.sb("l2f_xcop", [128, NCC, 1032], BF16)
            kf = actx.sb("l2f_kf", [65, T], BF16)
            vtm = actx.sb("l2f_vtm", [128, NTT, 72], BF16)
            outsb = actx.sb("l2f_out", [128, 16, D], BF16)
            wo_sb = actx.sb("l2f_wo", [64, 8, D], BF16)
            qraw = actx.sb("l2f_qraw", [64, 512], BF16)
            qn_t = actx.sb("l2f_qn", [64, 512], BF16)
            sqh = actx.sb("l2f_sqh", [64, 512], BF16)
            nrow = actx.sb("l2f_nrow", [1, 512], F32)
            tabs = [actx.sb(f"l2f_tab{i}", [64, 512], BF16) for i in range(2)]
            rotm = actx.sb("l2f_rotm", [64, 64], BF16)
            qsks = actx.sb("l2f_qsks", [64, 2], F32)
            negm = [actx.sb("l2f_negm", [128, 1], BF16) for _ in range(2)]
            amax = [actx.sb("l2f_amax", [128, 2], F32) for _ in range(2)]
            linv = [actx.sb("l2f_linv", [128, 128], F32) for _ in range(2)]
            rep = [actx.sb("l2f_rep", [64, 128], F32) for _ in range(2)]
            osb = [actx.sb("l2f_osb", [64, 128], BF16) for _ in range(2)]
            pt2 = [actx.sb("l2f_pt", [128, 128], BF16) for _ in range(3)]
            wh = WPool(actx, "l2f_wh", 2, [128, NCC, 192])
            e_dma(ctx, "gpsimd", rotm[:, :], aps["rotm"], writes=[("rotm",)])
            e_dma(ctx, "gpsimd", wo_sb[:, :, :],
                  aps["wo2_head"][:, l * 8192:(l + 1) * 8192]
                  .rearrange("p (h o) -> p h o", h=8), writes=[("l2fwo",)])

            def rhs_sh(cc, c0, n, sweep):
                a = c0 - (1 if cc >= 4 else 0)
                if sweep == 1 and a < 2048:
                    return xcop[:, cc, a - 1535:a - 1535 + n], [("l2fxcop",)]
                return _l2f_shift_rhs(xn, cc, c0, n), XN

            def qk_stage(kind, h, t0, t1):
                dst = qf if kind == "q" else kf
                dkey = ("l2fqf",) if kind == "q" else ("l2fkf",)
                wof = 0 if kind == "q" else 64
                tb0 = 0 if kind == "q" else 2
                for tt in range((t1 - t0) // 512):
                    c0 = t0 + tt * 512
                    ps, pk = ctx.pfree()
                    for cc in range(NCC):
                        rhs, rds = rhs_sh(cc, c0, 512, sweep)
                        e_mm(ctx, ps[0:64, :], wb[:, cc, wof:wof + 64],
                             rhs, start=(cc == 0), stop=(cc == NCC - 1),
                             reads=[wk] + rds, writes=[pk])
                    e_act(ctx, qraw[:, :], ps[0:64, :], AF.Identity,
                          reads=[pk], writes=[("l2fqraw",)])
                    e_act(ctx, sqh[:, :], qraw[:, :], AF.Square,
                          reads=[("l2fqraw",)], writes=[("l2fsqh",)])
                    ps2, pk2 = ctx.pfree()
                    e_mm(ctx, ps2[0:1, 0:512], onesb[0:64, 0:1], sqh[:, :],
                         start=True, stop=True,
                         reads=[("l2fsqh",), ("ones",)], writes=[pk2])
                    e_act(ctx, nrow[0:1, :], ps2[0:1, 0:512], AF.Ln,
                          bias=1e-24, reads=[pk2], writes=[("l2fnrow",)])
                    e_act(ctx, nrow[0:1, :], nrow[0:1, :], AF.Exp, scale=-0.5,
                          reads=[("l2fnrow",)], writes=[("l2fnrow",)])
                    if kind == "q":
                        e_ts(ctx, nrow[0:1, :], nrow[0:1, :], 8.0, ALU.mult,
                             reads=[("l2fnrow",)], writes=[("l2fnrow",)])
                    psr, pkr = ctx.pfree()
                    e_mm(ctx, psr[0:64, 0:512], onesf[0:1, 0:64],
                         nrow[0:1, :], start=True, stop=True,
                         reads=[("ones",), ("l2fnrow",)], writes=[pkr])
                    e_stt(ctx, qn_t[:, :], qraw[:, :], qsks[:, tb0 // 2:tb0 // 2 + 1],
                          psr[0:64, 0:512], ALU.mult, ALU.mult,
                          reads=[("l2fqraw",), ("l2fqsks",), pkr],
                          writes=[("l2fqn",)])
                    psrot, pkrot = ctx.pfree()
                    e_mm(ctx, psrot[0:64, 0:512], rotm[:, :], qn_t[:, :],
                         start=True, stop=True, reads=[("rotm",), ("l2fqn",)],
                         writes=[pkrot])
                    e_dma(ctx, "gpsimd", tabs[0][:, :],
                          aps["xtab"][tb0 * 64:tb0 * 64 + 64, c0:c0 + 512],
                          writes=[("l2ftab", 0)])
                    e_dma(ctx, "gpsimd", tabs[1][:, :],
                          aps["xtab"][(tb0 + 1) * 64:(tb0 + 1) * 64 + 64,
                                      c0:c0 + 512],
                          writes=[("l2ftab", 1)])
                    e_tt(ctx, ALU.mult, qraw[:, :], qn_t[:, :], tabs[0][:, :],
                         reads=[("l2fqn",), ("l2ftab", 0)],
                         writes=[("l2fqraw",)])
                    e_tt(ctx, ALU.mult, sqh[:, :], psrot[0:64, 0:512],
                         tabs[1][:, :], reads=[pkrot, ("l2ftab", 1)],
                         writes=[("l2fsqh",)])
                    e_tt(ctx, ALU.add, dst[0:64, c0:c0 + 512], qraw[:, :],
                         sqh[:, :], reads=[("l2fqraw",), ("l2fsqh",)],
                         writes=[dkey])

            for sweep in range(2):
                q0 = 2048 * sweep
                k0 = max(0, q0 - 512)
                e_memset(ctx, outsb[:, :, :], 0.0, writes=[("l2fout",)])
                for h in range(8):
                    wb, wk = wh.load(
                        wcols_src(aps["wcat2"], l * D, l * D + D,
                                  192 * h, 192 * h + 192))
                    e_dma(ctx, "gpsimd", qsks[:, :],
                          aps["qsks"][:, 2 * l:2 * l + 2], writes=[("l2fqsks",)])
                    qk_stage("q", h, q0, q0 + 2048)
                    qk_stage("k", h, k0, q0 + 2048)
                    e_memset(ctx, kf[64:65, :], 1.0, writes=[("l2fkf",)])
                    for t3 in range(k0 // 128, (q0 + 2048) // 128):
                        ps, pk = ctx.pfree()
                        for cc in range(NCC):
                            rhs, rds = rhs_sh(cc, t3 * 128, 128, sweep)
                            e_mm(ctx, ps[:, 0:64], rhs,
                                 wb[:, cc, 128:192], start=(cc == 0),
                                 stop=(cc == NCC - 1), reads=[wk] + rds,
                                 writes=[pk])
                        e_act(ctx, vtm[:, t3, 0:64], ps[:, 0:64], AF.Identity,
                              reads=[pk], writes=[("l2fv",)])
                        e_memset(ctx, vtm[:, t3, 64:65], 1.0,
                                 writes=[("l2fv",)])
                    if DBG and l == 0 and sweep == 0 and h == 0:
                        e_dma(ctx, "sync", aps["dbg_qf"], qf[:, :],
                              reads=[("l2fqf",)])
                        e_dma(ctx, "sync", aps["dbg_kf"], kf[:, :],
                              reads=[("l2fkf",)])
                        e_dma(ctx, "sync", aps["dbg_vt"], vtm[:, :, :],
                              reads=[("l2fv",)])
                    def stage_a(qt):
                        sl2 = qt % 2
                        tq3 = sweep * 16 + qt
                        nkc = min(5, tq3 + 1)
                        kclo = tq3 - nkc + 1
                        tqs = slice(tq3 * 128, tq3 * 128 + 128)
                        pa, pka = ctx.pfree4()
                        pb_, pkb = ctx.pfree4()
                        if nkc > 1:
                            e_mm(ctx, pa[:, 0:(nkc - 1) * 128],
                                 qf[0:64, tqs],
                                 kf[0:64, kclo * 128:tq3 * 128],
                                 start=True, stop=True,
                                 reads=[("l2fqf",), ("l2fkf",)], writes=[pka])
                            if tq3 >= 4:
                                e_tt(ctx, ALU.add, pa[:, 0:128], pa[:, 0:128],
                                     nm_low[:, :], reads=[pka, ("nm",)],
                                     writes=[pka])
                        e_mm(ctx, pb_[:, 0:128], qf[0:64, tqs],
                             kf[0:64, tq3 * 128:tq3 * 128 + 128],
                             start=True, stop=True,
                             reads=[("l2fqf",), ("l2fkf",)], writes=[pkb])
                        e_tt(ctx, ALU.add, pb_[:, 0:128], pb_[:, 0:128],
                             nm_up[:, :], reads=[pkb, ("nm",)], writes=[pkb])
                        e_red(ctx, amax[sl2][:, 0:1], pb_[:, 0:128],
                              ALU.max, negate=True, reads=[pkb],
                              writes=[("l2famax", sl2)])
                        if nkc > 1:
                            e_red(ctx, amax[sl2][:, 1:2],
                                  pa[:, 0:(nkc - 1) * 128],
                                  ALU.max, negate=True, reads=[pka],
                                  writes=[("l2famax", sl2)])
                            e_tt(ctx, ALU.min, amax[sl2][:, 0:1],
                                 amax[sl2][:, 0:1], amax[sl2][:, 1:2],
                                 reads=[("l2famax", sl2)],
                                 writes=[("l2famax", sl2)])
                        e_copy(ctx, negm[sl2][:, 0:1], amax[sl2][:, 0:1],
                               reads=[("l2famax", sl2)],
                               writes=[("l2fnegm", sl2)])
                        pbt, pbtk = ctx.pbf()
                        e_tr(ctx, pbt[64:65, 0:128], negm[sl2][:, 0:1],
                             ident[:, :], reads=[("l2fnegm", sl2), ("ident",)],
                             writes=[pbtk])
                        e_act(ctx, qf[64:65, tqs], pbt[64:65, 0:128],
                              AF.Identity, reads=[pbtk],
                              writes=[("l2fq64", qt % 4)])

                    def stage_b(qt):
                        sl2 = qt % 2
                        tq3 = sweep * 16 + qt
                        nkc = min(5, tq3 + 1)
                        kclo = tq3 - nkc + 1
                        tqs = slice(tq3 * 128, tq3 * 128 + 128)
                        psO, pkO = ctx.pbank(4 + sl2)

                        def b_s(ik):
                            kc = kclo + ik
                            Sb, Sk = ctx.pfree4()
                            e_mm(ctx, Sb[:, 0:128],
                                 kf[:, kc * 128:(kc + 1) * 128],
                                 qf[:, tqs], start=True, stop=True,
                                 reads=[("l2fkf",), ("l2fqf",),
                                        ("l2fq64", qt % 4)], writes=[Sk])
                            if kc == tq3 - 4:
                                e_tt(ctx, ALU.add, Sb[:, 0:128], Sb[:, 0:128],
                                     nm_up[:, :], reads=[Sk, ("nm",)],
                                     writes=[Sk])
                            if kc == tq3:
                                e_tt(ctx, ALU.add, Sb[:, 0:128], Sb[:, 0:128],
                                     nm_low[:, :], reads=[Sk, ("nm",)],
                                     writes=[Sk])
                            e_act(ctx, pt2[ik % 3][:, :], Sb[:, 0:128],
                                  AF.Exp, reads=[Sk],
                                  writes=[("l2fpt", ik % 3)])

                        def b_pv(ik):
                            kc = kclo + ik
                            e_mm(ctx, psO[0:65, 0:128], vtm[:, kc, 0:65],
                                 pt2[ik % 3][:, :], start=(ik == 0),
                                 stop=(ik == nkc - 1),
                                 reads=[("l2fv",), ("l2fpt", ik % 3)],
                                 writes=[pkO])

                        b_s(0)
                        for ik in range(1, nkc):
                            b_s(ik)
                            b_pv(ik - 1)
                        b_pv(nkc - 1)

                    def stage_c(qt):
                        sl2 = qt % 2
                        psO, pkO = ctx.pbank(4 + sl2)
                        e_act(ctx, linv[sl2][64:65, 0:128],
                              psO[64:65, 0:128], AF.Ln, reads=[pkO],
                              writes=[("l2flinv", sl2)])
                        e_act(ctx, linv[sl2][64:65, 0:128],
                              linv[sl2][64:65, 0:128], AF.Exp, scale=-1.0,
                              reads=[("l2flinv", sl2)],
                              writes=[("l2flinv", sl2)])
                        psR, pkR = ctx.pfree4()
                        e_mm(ctx, psR[0:64, 0:128], onesf[64:65, 0:64],
                             linv[sl2][64:65, 0:128], start=True, stop=True,
                             reads=[("ones",), ("l2flinv", sl2)], writes=[pkR])
                        e_copy(ctx, rep[sl2][:, :], psR[0:64, 0:128],
                               reads=[pkR], writes=[("l2frep", sl2)])
                        e_tt(ctx, ALU.mult, osb[sl2][:, :], psO[0:64, 0:128],
                             rep[sl2][:, :], reads=[pkO, ("l2frep", sl2)],
                             writes=[("l2fosb", sl2)])
                        for ob in range(2):
                            psP, pkP = ctx.pfree4()
                            e_mm(ctx, psP[:, :], osb[sl2][:, :],
                                 wo_sb[:, h, ob * 512:(ob + 1) * 512],
                                 start=True, stop=True,
                                 reads=[("l2fosb", sl2), ("l2fwo",)],
                                 writes=[pkP])
                            e_tt(ctx, ALU.add,
                                 outsb[:, qt, ob * 512:(ob + 1) * 512],
                                 outsb[:, qt, ob * 512:(ob + 1) * 512],
                                 psP[:, :], reads=[("l2fout",), pkP],
                                 writes=[("l2fout",)])

                    stage_a(0)
                    stage_a(1)
                    for qt in range(16):
                        stage_b(qt)
                        if qt + 2 < 16:
                            stage_a(qt + 2)
                        stage_c(qt)
                if DBG and l == 0 and sweep == 0:
                    e_dma(ctx, "sync", aps["dbg_osb"], outsb[:, :, :],
                          reads=[("l2fout",)])
                if sweep == 0:
                    e_copy(ctx, xcop[:, :, 0:1025],
                           xn[:, :, 1 + 1535:1 + 1535 + 1025],
                           reads=XN, writes=[("l2fxcop",)])
                for qt in range(16):
                    t3 = sweep * 16 + qt
                    pbt, pbtk = ctx.pbf()
                    for j in range(NCC):
                        e_tr(ctx, pbt[:, j * 128:(j + 1) * 128],
                             outsb[:, qt, j * 128:(j + 1) * 128], ident[:, :],
                             reads=[("l2fout",), ("ident",)], writes=[pbtk])
                    e_tt(ctx, ALU.add, xn[:, :, 1 + t3 * 128:1 + t3 * 128 + 128],
                         xn[:, :, 1 + t3 * 128:1 + t3 * 128 + 128],
                         pbt[:, :].rearrange("p (j t) -> p j t", j=NCC),
                         reads=XN + [pbtk], writes=[(xnkey, t3)])
        if DBG and l == 0:
            e_dma(ctx, "sync", aps["dbg_xi0"], xn[:, :, :], reads=xn_all(xnkey))
        ctx.s.global_barrier()

        with ExitStack() as fst:
            fctx = sub_ctx(ctx, fst)
            xln = fctx.sb("l2f_xln", [128, NCC, 1024], BF16)
            hmid = fctx.sb("l2f_hmid", [128, 25, 1024], BF16)
            ggel = fctx.sb("l2f_gg", [128, 512], BF16)
            lnp = fctx.sb("l2f_lnp", [128, 2 * NCC], F32)
            rrow = fctx.sb("l2f_rrow", [1, 1024], F32)
            mrep = fctx.sb("l2f_mrep", [128, 512], BF16)
            rrep = fctx.sb("l2f_rrep", [128, 512], BF16)
            sqf = fctx.sb("l2f_sqf", [128, 512], BF16)
            outq_bufs = {}
            w1p = WPool(fctx, "l2f_w1p", 2, [128, 9, 512])
            w2p = WPool(fctx, "l2f_w2p", 2, [128, 13, 512])
            e_dma(ctx, "gpsimd", lnp[:, :],
                  aps["lnp"][:, 2 * NCC * l:2 * NCC * (l + 1)],
                  writes=[("l2flnp",)])
            e_memset(ctx, hmid[:, 24, :], 0.0, writes=[("l2fhm", 24)])
            e_memset(ctx, hmid[0:1, 24, :], 1.0, writes=[("l2fhm", 24)])
            for q4 in range(4):
                t0 = q4 * 1024
                for t5 in range(2):
                    c0 = t0 + t5 * 512
                    xsl = slice(t5 * 512, t5 * 512 + 512)
                    psS, pkS = ctx.pbank(4)
                    psQ, pkQ = ctx.pbank(5)
                    for cc in range(NCC):
                        e_mm(ctx, psS[0:1, 0:512], onesb[:, 0:1],
                             _l2f_shift_rhs(xn, cc, c0, 512),
                             start=(cc == 0), stop=(cc == NCC - 1),
                             reads=[("ones",)] + XN, writes=[pkS])
                    for cc in range(NCC):
                        e_act(ctx, sqf[:, :], _l2f_shift_rhs(xn, cc, c0, 512),
                              AF.Square, reads=XN, writes=[("l2fsqf",)])
                        e_mm(ctx, psQ[0:1, 0:512], onesb[:, 0:1], sqf[:, :],
                             start=(cc == 0), stop=(cc == NCC - 1),
                             reads=[("l2fsqf",), ("ones",)], writes=[pkQ])
                    e_ts(ctx, rrow[0:1, 0:512], psS[0:1, 0:512], 1.0 / D,
                         ALU.mult, reads=[pkS], writes=[("l2frrow",)])
                    e_act(ctx, rrow[0:1, 512:1024], rrow[0:1, 0:512],
                          AF.Square, reads=[("l2frrow",)],
                          writes=[("l2frrow",)])
                    e_ts(ctx, rrow[0:1, 512:1024], rrow[0:1, 512:1024], -1.0,
                         ALU.mult, reads=[("l2frrow",)], writes=[("l2frrow",)])
                    e_stt(ctx, rrow[0:1, 512:1024], psQ[0:1, 0:512], 1.0 / D,
                          rrow[0:1, 512:1024], ALU.mult, ALU.add,
                          reads=[pkQ, ("l2frrow",)], writes=[("l2frrow",)])
                    e_act(ctx, rrow[0:1, 512:1024], rrow[0:1, 512:1024],
                          AF.Ln, bias=1e-5, reads=[("l2frrow",)],
                          writes=[("l2frrow",)])
                    e_act(ctx, rrow[0:1, 512:1024], rrow[0:1, 512:1024],
                          AF.Exp, scale=-0.5, reads=[("l2frrow",)],
                          writes=[("l2frrow",)])
                    psM, pkM = ctx.pbank(4)
                    e_mm(ctx, psM[:, 0:512], onesf[0:1, 0:128],
                         rrow[0:1, 0:512], start=True, stop=True,
                         reads=[("ones",), ("l2frrow",)], writes=[pkM])
                    e_copy(ctx, mrep[:, :], psM[:, 0:512], reads=[pkM],
                           writes=[("l2fmrep",)])
                    psM2, pkM2 = ctx.pbank(5)
                    e_mm(ctx, psM2[:, 0:512], onesf[0:1, 0:128],
                         rrow[0:1, 512:1024], start=True, stop=True,
                         reads=[("ones",), ("l2frrow",)], writes=[pkM2])
                    e_copy(ctx, rrep[:, :], psM2[:, 0:512], reads=[pkM2],
                           writes=[("l2frrep",)])
                    for cc in range(NCC):
                        e_tt(ctx, ALU.subtract, xln[:, cc, xsl],
                             _l2f_shift_rhs(xn, cc, c0, 512), mrep[:, :],
                             reads=XN + [("l2fmrep",)], writes=[("l2fxln",)])
                        e_tt(ctx, ALU.mult, xln[:, cc, xsl], xln[:, cc, xsl],
                             rrep[:, :], reads=[("l2fxln",), ("l2frrep",)],
                             writes=[("l2fxln",)])
                        e_ts(ctx, xln[:, cc, xsl], xln[:, cc, xsl],
                             lnp[:, 2 * cc:2 * cc + 1], ALU.mult,
                             s2=lnp[:, 2 * cc + 1:2 * cc + 2], op1=ALU.add,
                             reads=[("l2fxln",), ("l2flnp",)],
                             writes=[("l2fxln",)])
                e_dma(ctx, "scalar",
                      aps["xln_scr"].rearrange("(cc p) t -> p cc t", p=128)
                      [:, :, t0:t0 + 1024], xln[:, :, :],
                      reads=[("l2fxln",)], writes=[("l2fxlnD", q4)])
            for q4 in range(4):
                t0 = q4 * 1024
                e_dma(ctx, "sync", xln[:, :, :],
                      aps["xln_scr"].rearrange("(cc p) t -> p cc t", p=128)
                      [:, :, t0:t0 + 1024],
                      reads=[("l2fxlnD", q4)], writes=[("l2fxln",)])
                if DBG and l == 0 and q4 == 0:
                    e_dma(ctx, "sync", aps["dbg_xln"], xln[:, :, :],
                          reads=[("l2fxln",)])
                for i in range(6):
                    wA, wAk = w1p.load(
                        aps["w1p"][l * 1152:(l + 1) * 1152, :]
                        .rearrange("(cc p) o -> p cc o", p=128)
                        [:, :, 512 * i:512 * i + 512])
                    wG, wGk = w1p.load(
                        aps["w1p"][l * 1152:(l + 1) * 1152, :]
                        .rearrange("(cc p) o -> p cc o", p=128)
                        [:, :, 3072 + 512 * i:3072 + 512 * i + 512])
                    for hc4 in range(4):
                        hc = 4 * i + hc4
                        for t5 in range(2):
                            xsl = slice(t5 * 512, t5 * 512 + 512)
                            psU, pkU = ctx.pbank(4)
                            psG, pkG = ctx.pbank(5)
                            for cc in range(9):
                                rhs = (xln[:, cc, xsl] if cc < 8
                                       else row1[:, :])
                                rds = ([("l2fxln",)] if cc < 8
                                       else [("row1",)])
                                e_mm(ctx, psU[:, :],
                                     wA[:, cc, hc4 * 128:(hc4 + 1) * 128],
                                     rhs, start=(cc == 0), stop=(cc == 8),
                                     reads=[wAk] + rds, writes=[pkU])
                            for cc in range(9):
                                rhs = (xln[:, cc, xsl] if cc < 8
                                       else row1[:, :])
                                rds = ([("l2fxln",)] if cc < 8
                                       else [("row1",)])
                                e_mm(ctx, psG[:, :],
                                     wG[:, cc, hc4 * 128:(hc4 + 1) * 128],
                                     rhs, start=(cc == 0), stop=(cc == 8),
                                     reads=[wGk] + rds, writes=[pkG])
                            e_act(ctx, ggel[:, :], psG[:, :], AF.Gelu,
                                  reads=[pkG], writes=[("l2fgg",)])
                            e_tt(ctx, ALU.mult, hmid[:, hc, xsl], psU[:, :],
                                 ggel[:, :], reads=[pkU, ("l2fgg",)],
                                 writes=[("l2fhm", hc)])
                if q4 == 0:
                    outq_bufs[0] = fctx.sb("l2f_outq", [128, 4, D], BF16)
                outq = outq_bufs[0]
                for half8 in range(2):
                    for ob in range(2):
                        for piece in range(2):
                            h0 = 13 * piece
                            h1 = min(25, h0 + 13)
                            w2, w2k = w2p.load(
                                aps["w2p"][l * 3200 + h0 * 128:
                                           l * 3200 + h1 * 128, :]
                                .rearrange("(cc p) o -> p cc o", p=128)
                                [:, :, ob * 512:ob * 512 + 512])
                            for t7 in range(half8 * 4, half8 * 4 + 4):
                                ps, pk = ctx.pbank(2 + (t7 % 4))
                                for hc in range(h0, h1):
                                    e_mm(ctx, ps[:, :],
                                         hmid[:, hc, t7 * 128:(t7 + 1) * 128],
                                         w2[:, hc - h0, :],
                                         start=(hc == 0), stop=(hc == 24),
                                         reads=[w2k, ("l2fhm", hc)],
                                         writes=[pk])
                                if piece == 1:
                                    e_copy(ctx, outq[:, t7 - half8 * 4,
                                                     ob * 512:(ob + 1) * 512],
                                           ps[:, :], reads=[pk],
                                           writes=[("l2foutq",)])
                    for t7 in range(half8 * 4, half8 * 4 + 4):
                        t3 = q4 * 8 + t7
                        pbt, pbtk = ctx.pbf()
                        for j in range(NCC):
                            e_tr(ctx, pbt[:, j * 128:(j + 1) * 128],
                                 outq[:, t7 - half8 * 4,
                                      j * 128:(j + 1) * 128], ident[:, :],
                                 reads=[("l2foutq",), ("ident",)],
                                 writes=[pbtk])
                        e_tt(ctx, ALU.add,
                             xn[:, :, 1 + t3 * 128:1 + t3 * 128 + 128],
                             xn[:, :, 1 + t3 * 128:1 + t3 * 128 + 128],
                             pbt[:, :].rearrange("p (j t) -> p j t", j=NCC),
                             reads=XN + [pbtk], writes=[(xnkey, t3)])
            if DBG and l == 0:
                e_dma(ctx, "sync", aps["dbg_xi1"], xn[:, :, :],
                      reads=xn_all(xnkey))
        ctx.s.global_barrier()

    for t3 in range(NTT):
        pbt, pbtk = ctx.pbf()
        for j in range(NCC):
            e_tr(ctx, pbt[:, j * 128:(j + 1) * 128],
                 xn[:, j, 1 + t3 * 128:1 + t3 * 128 + 128], ident[:, :],
                 reads=XN + [("ident",)], writes=[pbtk])
        skey = ("stage", t3 % 2, 0)
        e_copy(ctx, stage[t3 % 2][:, :], pbt[:, :], reads=[pbtk],
               writes=[skey])
        e_dma(ctx, "scalar", aps["acc_f"][t3 * 128:(t3 + 1) * 128, :],
              stage[t3 % 2][:, :], reads=[skey], writes=[("acc_f", t3)])


def build_l2(fctx=None, prefix="", x_in_ap=None, x_out_ap=None):
    nc = fctx.nc if fctx is not None else bass.Bass(
        "TRN2", target_bir_lowering=False)
    g = lambda n, sh: nc.dram_tensor(prefix + n, sh, F32,
                                     kind="ExternalInput").ap()
    gb = lambda n, sh: nc.dram_tensor(prefix + n, sh, BF16,
                                      kind="ExternalInput").ap()
    aps = {
        "x_in": x_in_ap if x_in_ap is not None else g("x_in", [T, D]),
        "Wcat": g("Wcat", [D, 16 * 192]),
        "Wf": g("Wf", [D, 16]),
        "wo_head": g("wo_head", [64, 16 * D]),
        "wcat2": g("wcat2", [2 * D, 8 * 192]),
        "wo2_head": g("wo2_head", [64, 2 * 8 * D]),
        "qsks": g("qsks", [64, 4]),
        "lnp": g("lnp", [128, 2 * 2 * NCC]),
        "w1p": g("w1p", [2 * 1152, 6144]),
        "w2p": g("w2p", [2 * 3200, 1024]),
        "xtab": g("xtab", [256, T]),
        "negmask": gb("negmask", [128, 4 * 512]),
        "nm_low": g("nm_low", [128, 128]),
        "nm_up": g("nm_up", [128, 128]),
        "ones_f32": g("ones_f32", [128, 136]),
        "ident": gb("ident", [128, 128]),
        "ones_bf": gb("ones_bf", [128, 640]),
        "row1": gb("row1", [128, 512]),
        "rotm": g("rotm", [64, 64]),
    }
    aps["negmask"] = aps["negmask"].rearrange("p (m q) -> p m q", m=4)
    aps["x_out"] = x_out_ap if x_out_ap is not None else nc.dram_tensor(
        prefix + "x_out", [T, D], F32, kind="ExternalOutput").ap()
    acck = "ExternalOutput" if DBG else "Internal"
    aps["acc_a"] = nc.dram_tensor(prefix + "acc_a", [T, D], F32,
                                  kind=acck).ap()
    aps["acc_f"] = nc.dram_tensor(prefix + "acc_f", [T, D], F32,
                                  kind=acck).ap()
    if DBG:
        aps["dbg_qf"] = nc.dram_tensor("dbg_qf", [65, T], BF16,
                                       kind="ExternalOutput").ap()
        aps["dbg_kf"] = nc.dram_tensor("dbg_kf", [65, T], BF16,
                                       kind="ExternalOutput").ap()
        aps["dbg_vt"] = nc.dram_tensor("dbg_vt", [128, NTT, 72], BF16,
                                       kind="ExternalOutput").ap()
        aps["dbg_osb"] = nc.dram_tensor("dbg_osb", [128, 16, D], BF16,
                                        kind="ExternalOutput").ap()
        aps["dbg_xi0"] = nc.dram_tensor("dbg_xi0", [128, NCC, PADW], BF16,
                                        kind="ExternalOutput").ap()
        aps["dbg_xi1"] = nc.dram_tensor("dbg_xi1", [128, NCC, PADW], BF16,
                                        kind="ExternalOutput").ap()
        aps["dbg_xln"] = nc.dram_tensor("dbg_xln", [128, NCC, 1024], BF16,
                                        kind="ExternalOutput").ap()
    aps["c_scr"] = nc.dram_tensor(prefix + "c_scr", [16, T], F32).ap()
    aps["xln_scr"] = nc.dram_tensor(prefix + "xln_scr", [D, T], BF16).ap()
    aps["csp_scr"] = nc.dram_tensor(prefix + "csp_scr", [16, 3 * T],
                                    BF16).ap()

    def body(ctx):
        sched = ctx.s
        bufs = make_common_bufs(ctx, aps, 2)
        for nm, key, shp, dt_ in (
                ("ones_bf", "ones", [128, 640], BF16),
                ("ones_f32", "ones", [128, 136], F32),
                ("negmask", "nmask", [128, 4, 512], BF16),
                ("nm_low", "nm", [128, 128], F32),
                ("nm_up", "nm", [128, 128], F32),
                ("row1", "row1", [128, 512], BF16)):
            bufs[nm] = ctx.sb(f"sb_{nm}", shp, dt_)
            e_dma(ctx, "gpsimd",
                  bufs[nm][:, :, :] if len(shp) == 3 else bufs[nm][:, :],
                  aps[nm], writes=[(key,)])
        bufs["l2_c"] = aps["c_scr"]
        bufs["l2_csp"] = aps["csp_scr"]

        xn = ctx.sb("xn", [128, NCC, PADW], BF16)
        xnkey = "xn"
        e_memset(ctx, xn[:, :, 0:1], 0.0, writes=[(xnkey, "all")])
        e_memset(ctx, xn[:, :, 1 + T:PADW], 0.0, writes=[(xnkey, "all")])

        with ExitStack() as st:
            pctx = sub_ctx(ctx, st)
            eb = make_epi_bufs(pctx, 0)
            emit_ln(pctx, aps["x_in"], [], None, xn, xnkey, bufs={**bufs, **eb})
        sched.global_barrier()
        with ExitStack() as st:
            emit_l2_attn(sub_ctx(ctx, st), aps, xn, xnkey, bufs)
        sched.global_barrier()
        with ExitStack() as st:
            emit_l2_ffn(sub_ctx(ctx, st), aps, xn, xnkey, bufs)
        sched.global_barrier()
        with ExitStack() as st:
            fctx = sub_ctx(ctx, st)
            eb = make_epi_bufs(fctx, 2)
            emit_ln(fctx, aps["x_in"],
                    [(aps["acc_a"], "acc_a"), (aps["acc_f"], "acc_f")],
                    aps["x_out"], None, xnkey, bufs={**bufs, **eb})
    if fctx is None:
        with ExitStack() as stack:
            sched = Sched(nc, stack)
            ctx = Ctx(nc, sched, stack)
            body(ctx)
            sched.finish()
            with nc.Block() as block:
                sched.replay(block)
        return nc
    with ExitStack() as lst:
        body(sub_ctx(fctx, lst))
    fctx.s.global_barrier()


def l2_host_tables():
    DIM_HEAD, WINDOW, n = 64, 512, T
    half_idx = np.arange(0, DIM_HEAD, 2, dtype=np.float64)
    inv_freq = 1.0 / 10000.0 ** (half_idx / DIM_HEAD)
    t = np.arange(n, dtype=np.float64)
    freqs = t[:, None] * inv_freq[None, :]
    freqs = np.concatenate([freqs, freqs], axis=-1)
    base = (half_idx + 0.4 * DIM_HEAD) / (1.4 * DIM_HEAD)
    power = (t - n // 2) / float(WINDOW // 2)
    sc = base[None, :] ** power[:, None]
    sc = np.concatenate([sc, sc], axis=-1)
    cos, sin = np.cos(freqs), np.sin(freqs)
    tabs = np.stack([cos * sc, sin * sc, cos / sc, sin / sc])
    return np.ascontiguousarray(
        tabs.transpose(0, 2, 1).reshape(256, T).astype(np.float32))


def l2_inputs(inputs, xb):
    c = host_consts()
    Wq, Wk, Wv = (inputs["L2_attn_Wq"], inputs["L2_attn_Wk"],
                  inputs["L2_attn_Wv"])
    wcat = np.concatenate(
        [np.concatenate([Wq[:, h * 64:(h + 1) * 64],
                         Wk[:, h * 64:(h + 1) * 64],
                         Wv[:, h * 64:(h + 1) * 64]], axis=1)
         for h in range(16)], axis=1)
    wo_head = np.ascontiguousarray(
        inputs["L2_attn_Wo"].reshape(16, 64, D).transpose(1, 0, 2)
        .reshape(64, 16 * D))
    wq2 = inputs["L2_ffn_wqkv"]
    wcat2 = np.concatenate(
        [np.concatenate(
            [np.concatenate([wq2[l][:, h * 64:(h + 1) * 64],
                             wq2[l][:, 512 + h * 64:512 + (h + 1) * 64],
                             wq2[l][:, 1024 + h * 64:1024 + (h + 1) * 64]],
                            axis=1) for h in range(8)], axis=1)
         for l in range(2)], axis=0)
    wo2 = inputs["L2_ffn_wo"]
    wo2_head = np.concatenate(
        [wo2[l].reshape(8, 64, D).transpose(1, 0, 2).reshape(64, 8 * D)
         for l in range(2)], axis=1)
    qsks = np.stack([inputs["L2_ffn_q_scale"][0], inputs["L2_ffn_k_scale"][0],
                     inputs["L2_ffn_q_scale"][1], inputs["L2_ffn_k_scale"][1]],
                    axis=1)
    lnp = np.zeros((128, 2 * 2 * NCC), np.float32)
    for l in range(2):
        gp = vec_p(inputs["L2_ffn_ln_g"][l])
        bp = vec_p(inputs["L2_ffn_ln_b"][l])
        for cc in range(NCC):
            lnp[:, 2 * NCC * l + 2 * cc] = gp[:, cc]
            lnp[:, 2 * NCC * l + 2 * cc + 1] = bp[:, cc]
    w1 = inputs["L2_ffn_w1"]; b1 = inputs["L2_ffn_b1"]
    w2 = inputs["L2_ffn_w2"]; b2 = inputs["L2_ffn_b2"]
    HID = 2730
    w1p = np.zeros((2 * 1152, 6144), np.float32)
    w2p = np.zeros((2 * 3200, 1024), np.float32)
    for l in range(2):
        w1p[l * 1152:l * 1152 + 1024, 0:HID] = w1[l][:, 0:HID]
        w1p[l * 1152 + 1024, 0:HID] = b1[l][0:HID]
        w1p[l * 1152:l * 1152 + 1024, 3072:3072 + HID] = w1[l][:, HID:]
        w1p[l * 1152 + 1024, 3072:3072 + HID] = b1[l][HID:]
        w2p[l * 3200:l * 3200 + HID, :] = w2[l]
        w2p[l * 3200 + 3072, :] = b2[l]
    p = np.arange(128)[:, None, None]
    mm_ = np.arange(4)[None, :, None]
    q = np.arange(512)[None, None, :]
    negmask = np.where((128 * mm_ + p) <= q, 0.0, -1e30).astype(BF)
    pq = np.arange(128)
    nm_low = np.where(pq[:, None] > pq[None, :], -1e30, 0.0).astype(np.float32)
    nm_up = np.where(pq[:, None] < pq[None, :], -1e30, 0.0).astype(np.float32)
    rotm = np.zeros((64, 64), np.float32)
    for d in range(32):
        rotm[32 + d, d] = -1.0
        rotm[d, 32 + d] = 1.0
    return {
        "x_in": np.ascontiguousarray(xb),
        "Wcat": np.ascontiguousarray(wcat),
        "Wf": inputs["L2_attn_Wf"],
        "wo_head": wo_head,
        "wcat2": np.ascontiguousarray(wcat2),
        "wo2_head": np.ascontiguousarray(wo2_head),
        "qsks": np.ascontiguousarray(qsks),
        "lnp": lnp, "w1p": w1p, "w2p": w2p,
        "xtab": l2_host_tables(),
        "negmask": negmask.reshape(128, 4 * 512),
        "nm_low": nm_low, "nm_up": nm_up, "rotm": rotm,
        "ident": c["ident"], "ones_bf": c["ones_bf"],
        "ones_f32": c["ones_f32"], "row1": c["row1"],
    }



def emit_l3_attn(ctx, aps, xn, xnkey, bufs):
    XN = xn_all(xnkey)
    onesb = bufs["ones_bf"]
    onesf = bufs["ones_f32"]
    dmask = bufs["diagmask"]
    stage = bufs["stage"]

    kaug = ctx.sb("l3_kaug", [65, T], BF16)
    qaug = ctx.sb("l3_qaug", [65, 2048], BF16)
    vtm = ctx.sb("l3_vtm", [128, NTT, 72], BF16)
    negu = ctx.sb("l3_negu", [1, 2048], BF16)
    rowA = ctx.sb("l3_rowA", [1, T], BF16)
    sq = ctx.sb("l3_sq", [128, 512], BF16)
    kms = ctx.sb("l3_kms", [1, 4], F32)
    linv = [ctx.sb("l3_linv", [128, 512], F32) for _ in range(2)]
    rep = [ctx.sb("l3_rep", [64, 512], F32) for _ in range(2)]
    osb = [ctx.sb("l3_osb", [64, 512], BF16) for _ in range(2)]
    pt = [ctx.sb(f"l3_pt{i}", [128, 512], BF16) for i in range(4)]
    outsb = ctx.sb("l3_outsb", [128, 16, D], BF16)
    brow = ctx.sb("l3_brow", [1, 192], BF16)
    bqs = ctx.sb("l3_bqs", [64, 48], F32)
    e_ts(ctx, bqs[:, :], bufs["bqkv_p"][:, :], 0.125, ALU.mult,
         reads=[("bqkvp",)], writes=[("l3bqs",)])
    wo_sb = ctx.sb("l3_wo", [64, 16, D], BF16)
    e_dma(ctx, "gpsimd", wo_sb[:, :, :],
          aps["wo_head"].rearrange("p (h o) -> p h o", h=16),
          writes=[("l3wo",)])
    wh = WPool(ctx, "l3wh", 2, [128, NCC, 192])

    for sweep in range(2):
        kr = 2048 * (sweep + 1)
        q0 = 2048 * sweep
        e_memset(ctx, outsb[:, :, :], 0.0, writes=[("l3out",)])
        for h in range(16):
            wb, wk = wh.load(wcols_src(aps["W_qkv"], 0, D,
                                       192 * h, 192 * h + 192))
            e_dma(ctx, "gpsimd", brow[0:1, :], aps["bqkv_rows"][h:h + 1, :],
                  writes=[("l3brow",)])
            for tt in range(kr // 512):
                ps, pk = ctx.pfree()
                for cc in range(NCC):
                    e_mm(ctx, ps[0:64, :], wb[:, cc, 64:128],
                         xn[:, cc, 1 + tt * 512:1 + tt * 512 + 512],
                         start=(cc == 0), stop=(cc == NCC - 1),
                         reads=[wk] + XN, writes=[pk])
                e_act(ctx, kaug[0:64, tt * 512:(tt + 1) * 512], ps[0:64, :],
                      AF.Identity, bias=bufs["bqkv_p"][:, 3 * h + 1:3 * h + 2],
                      reads=[pk, ("bqkvp",)], writes=[("l3k",)])
            e_memset(ctx, kaug[64:65, 0:kr], 1.0, writes=[("l3k",)])
            for tt in range(4):
                ps, pk = ctx.pfree()
                for cc in range(NCC):
                    e_mm(ctx, ps[0:64, :], wb[:, cc, 0:64],
                         xn[:, cc, 1 + q0 + tt * 512:1 + q0 + tt * 512 + 512],
                         start=(cc == 0), stop=(cc == NCC - 1),
                         reads=[wk] + XN, writes=[pk])
                e_act(ctx, qaug[0:64, tt * 512:(tt + 1) * 512], ps[0:64, :],
                      AF.Identity, scale=0.125,
                      bias=bqs[:, 3 * h:3 * h + 1],
                      reads=[pk, ("l3bqs",)], writes=[("l3q",)])
            for t3 in range(kr // 128):
                ps, pk = ctx.pfree()
                for cc in range(NCC):
                    e_mm(ctx, ps[:, 0:64], xn[:, cc, 1 + t3 * 128:1 + t3 * 128 + 128],
                         wb[:, cc, 128:192], start=(cc == 0), stop=False,
                         reads=[wk] + XN, writes=[pk])
                e_mm(ctx, ps[:, 0:64], onesb[0:1, 0:128],
                     brow[0:1, 128:192], start=False, stop=True,
                     reads=[("ones",), ("l3brow",)], writes=[pk])
                e_act(ctx, vtm[:, t3, 0:64], ps[:, 0:64], AF.Identity,
                      reads=[pk], writes=[("l3v",)])
            e_memset(ctx, vtm[:, 0:kr // 128, 64:65], 1.0, writes=[("l3v",)])
            for tt in range(kr // 512):
                ps, pk = ctx.pfree()
                e_act(ctx, sq[0:64, :], kaug[0:64, tt * 512:(tt + 1) * 512],
                      AF.Square, reads=[("l3k",)], writes=[("l3sq",)])
                e_mm(ctx, ps[0:1, 0:512], onesb[0:64, 0:1], sq[0:64, :],
                     start=True, stop=True, reads=[("l3sq",), ("ones",)],
                     writes=[pk])
                e_copy(ctx, rowA[0:1, tt * 512:(tt + 1) * 512], ps[0:1, 0:512],
                       reads=[pk], writes=[("l3rowA",)])
            e_red(ctx, kms[0:1, 0:1], rowA[0:1, 0:kr], ALU.max,
                  reads=[("l3rowA",)], writes=[("l3kms",)])
            e_act(ctx, kms[0:1, 1:2], kms[0:1, 0:1], AF.Ln, reads=[("l3kms",)],
                  writes=[("l3kms",)])
            e_act(ctx, kms[0:1, 2:3], kms[0:1, 1:2], AF.Exp, scale=0.5,
                  reads=[("l3kms",)], writes=[("l3kms",)])
            e_ts(ctx, kms[0:1, 3:4], kms[0:1, 2:3], -1.0, ALU.mult,
                 reads=[("l3kms",)], writes=[("l3kms",)])
            for tt in range(4):
                ps, pk = ctx.pfree()
                e_act(ctx, sq[0:64, :], qaug[0:64, tt * 512:(tt + 1) * 512],
                      AF.Square, reads=[("l3q",)], writes=[("l3sq",)])
                e_mm(ctx, ps[0:1, 0:512], onesb[0:64, 0:1], sq[0:64, :],
                     start=True, stop=True, reads=[("l3sq",), ("ones",)],
                     writes=[pk])
                e_copy(ctx, rowA[0:1, tt * 512:(tt + 1) * 512], ps[0:1, 0:512],
                       reads=[pk], writes=[("l3rowA",)])
            e_act(ctx, negu[0:1, :], rowA[0:1, 0:2048], AF.Ln,
                  reads=[("l3rowA",)], writes=[("l3negu",)])
            e_act(ctx, rowA[0:1, 0:2048], negu[0:1, :], AF.Exp, scale=0.5,
                  reads=[("l3negu",)], writes=[("l3rowA",)])
            e_ts(ctx, negu[0:1, :], rowA[0:1, 0:2048], kms[0:1, 3:4],
                 ALU.mult, reads=[("l3rowA",), ("l3kms",)], writes=[("l3negu",)])
            e_dma(ctx, "sync", qaug[64:65, :], negu[0:1, :],
                  reads=[("l3negu",)], writes=[("l3q",)])
            for lg in range(4):
                g = sweep * 4 + lg
                sl2 = lg % 2
                kcs = 4 * (g + 1)
                tqs = slice(lg * 512, (lg + 1) * 512)
                psO, pkO = ctx.pbank(4 + sl2)

                def l3_s(kci):
                    Sb, Sk = ctx.pbank(kci % 2)
                    e_mm(ctx, Sb[:, :], kaug[:, kci * 128:(kci + 1) * 128],
                         qaug[:, tqs], start=True, stop=True,
                         reads=[("l3k",), ("l3q",)], writes=[Sk])
                    ptb, ptk = pt[kci % 4], ("l3pt", kci % 4)
                    e_act(ctx, ptb[:, :], Sb[:, :], AF.Exp, reads=[Sk],
                          writes=[ptk])
                    if kci >= kcs - 4:
                        e_tt(ctx, ALU.mult, ptb[:, :], ptb[:, :],
                             dmask[:, kci - (kcs - 4), :],
                             reads=[ptk, ("dmask",)], writes=[ptk])

                def l3_pv(kci):
                    ptb, ptk = pt[kci % 4], ("l3pt", kci % 4)
                    e_mm(ctx, psO[0:65, :], vtm[:, kci, 0:65], ptb[:, :],
                         start=(kci == 0), stop=(kci == kcs - 1),
                         reads=[("l3v",), ptk], writes=[pkO])

                l3_s(0)
                for kci in range(1, kcs):
                    l3_s(kci)
                    l3_pv(kci - 1)
                l3_pv(kcs - 1)
                e_act(ctx, linv[sl2][64:65, :], psO[64:65, :], AF.Ln,
                      reads=[pkO], writes=[("l3linv", sl2)])
                e_act(ctx, linv[sl2][64:65, :], linv[sl2][64:65, :], AF.Exp,
                      scale=-1.0, reads=[("l3linv", sl2)],
                      writes=[("l3linv", sl2)])
                psR, pkR = ctx.pfree()
                e_mm(ctx, psR[0:64, :], onesf[64:65, 0:64], linv[sl2][64:65, :],
                     start=True, stop=True, reads=[("ones",), ("l3linv", sl2)],
                     writes=[pkR])
                e_copy(ctx, rep[sl2][:, :], psR[0:64, :], reads=[pkR],
                       writes=[("l3rep", sl2)])
                e_tt(ctx, ALU.mult, osb[sl2][:, :], psO[0:64, :],
                     rep[sl2][:, :], reads=[pkO, ("l3rep", sl2)],
                     writes=[("l3osb", sl2)])
                for t4 in range(4):
                    for ob in range(2):
                        psP, pkP = ctx.pbank(2 + ob)
                        e_mm(ctx, psP[:, :],
                             osb[sl2][:, t4 * 128:(t4 + 1) * 128],
                             wo_sb[:, h, ob * 512:(ob + 1) * 512],
                             start=True, stop=True,
                             reads=[("l3osb", sl2), ("l3wo",)], writes=[pkP])
                        e_tt(ctx, ALU.add,
                             outsb[:, lg * 4 + t4, ob * 512:(ob + 1) * 512],
                             outsb[:, lg * 4 + t4, ob * 512:(ob + 1) * 512],
                             psP[:, :], reads=[("l3out",), pkP],
                             writes=[("l3out",)])
        for t in range(16):
            t3 = sweep * 16 + t
            skey = ("stage", t % 2, 0)
            e_tt(ctx, ALU.add, stage[t % 2][:, :], outsb[:, t, :],
                 bufs["bo_rep"][:, :], reads=[("l3out",), ("borep",)],
                 writes=[skey])
            e_dma(ctx, "scalar", aps["acc_a"][t3 * 128:(t3 + 1) * 128, :],
                  stage[t % 2][:, :], reads=[skey], writes=[("acc_a", t3)])


def emit_quant_tile(ctx, vq, scr, key, nblk=32):
    s32, av, g2, xi = scr["s32"], scr["av"], scr["g2"], scr["xi"]
    W = nblk * 32
    v3 = vq[:, :].rearrange("p (b j) -> p b j", b=nblk)
    e_red(ctx, s32[:, 0:nblk], v3, ALU.max, reads=[key],
          writes=[("q_s32",)])
    e_ts(ctx, s32[:, 0:nblk], s32[:, 0:nblk], 1.0 / 7.5, ALU.mult,
         s2=1e-30, op1=ALU.max, reads=[("q_s32",)], writes=[("q_s32",)])
    ctx.s.emit("vector", lambda e: e.reciprocal(s32[:, nblk:2 * nblk],
                                                s32[:, 0:nblk]),
               reads=[("q_s32",)], writes=[("q_s32",)])
    for b in range(nblk):
        e_ts(ctx, vq[:, b * 32:(b + 1) * 32], vq[:, b * 32:(b + 1) * 32],
             s32[:, nblk + b:nblk + b + 1], ALU.mult,
             reads=[key, ("q_s32",)], writes=[key])
    e_act(ctx, av[:, 0:W], vq[:, :], AF.Abs, reads=[key], writes=[("q_av",)])
    e_ts(ctx, g2[:, 0:W], av[:, 0:W], 2.0, ALU.is_ge, reads=[("q_av",)],
         writes=[("q_g2",)])
    e_ts(ctx, av[:, 0:W], av[:, 0:W], 4.0, ALU.is_ge, reads=[("q_av",)],
         writes=[("q_av",)])
    e_ts(ctx, scr["isv"][:, 0:W], av[:, 0:W], -2.0, ALU.mult, s2=8.0,
         op1=ALU.add, reads=[("q_av",)], writes=[("q_isv",)])
    e_stt(ctx, scr["isv"][:, 0:W], g2[:, 0:W], -4.0, scr["isv"][:, 0:W],
          ALU.mult, ALU.add, reads=[("q_g2",), ("q_isv",)], writes=[("q_isv",)])
    e_ts(ctx, av[:, 0:W], av[:, 0:W], 0.25, ALU.mult, s2=0.125, op1=ALU.add,
         reads=[("q_av",)], writes=[("q_av",)])
    e_stt(ctx, g2[:, 0:W], g2[:, 0:W], 0.125, av[:, 0:W], ALU.mult, ALU.add,
          reads=[("q_g2",), ("q_av",)], writes=[("q_g2",)])
    e_tt(ctx, ALU.mult, vq[:, :], vq[:, :], scr["isv"][:, 0:W],
         reads=[key, ("q_isv",)], writes=[key])
    e_copy(ctx, xi[:, 0:W], vq[:, :], reads=[key], writes=[("q_xi",)])
    e_copy(ctx, vq[:, :], xi[:, 0:W], reads=[("q_xi",)], writes=[key])
    e_tt(ctx, ALU.mult, vq[:, :], vq[:, :], g2[:, 0:W], reads=[key, ("q_g2",)],
         writes=[key])
    for b in range(nblk):
        e_ts(ctx, vq[:, b * 32:(b + 1) * 32], vq[:, b * 32:(b + 1) * 32],
             s32[:, b:b + 1], ALU.mult, reads=[key, ("q_s32",)], writes=[key])


def emit_l3_ffn(ctx, aps, stats, bufs):
    ident = bufs["ident"]
    stage = bufs["stage"]
    scr = {"s32": ctx.sb("l3f_s32", [128, 64], F32),
           "av": ctx.sb("l3f_av", [128, 1024], F32),
           "g2": ctx.sb("l3f_g2", [128, 1024], F32),
           "isv": ctx.sb("l3f_isv", [128, 1024], F32),
           "xi": ctx.sb("l3f_xi", [128, 1024], mybir.dt.int32)}
    wtile = ctx.sb("l3f_wt", [128, 1024], F32)
    wdb = ctx.sb("l3f_wdb", [128, 1024], BF16)
    wdT = ctx.sb("l3f_wdT", [128, NCC, 1024], BF16)
    xtile = ctx.sb("l3f_xt", [128, 1024], F32)
    xdb = ctx.sb("l3f_xdb", [128, 1024], BF16)
    xdf = ctx.sb("l3f_xdf", [128, NCC, 128], BF16)

    for wc in range(NCC):
        e_dma(ctx, "gpsimd", wtile[:, :],
              aps["fw"][wc * 128:(wc + 1) * 128, :], writes=[("l3fw",)])
        emit_quant_tile(ctx, wtile, scr, ("l3fw",))
        e_copy(ctx, wdb[:, :], wtile[:, :], reads=[("l3fw",)],
               writes=[("l3wdb",)])
        pb, pbk = ctx.pbf()
        for j in range(NCC):
            e_tr(ctx, pb[:, j * 128:(j + 1) * 128],
                 wdb[:, j * 128:(j + 1) * 128], ident[:, :],
                 reads=[("l3wdb",), ("ident",)], writes=[pbk])
        e_copy(ctx, wdT[:, :, wc * 128:(wc + 1) * 128],
               pb[:, :].rearrange("p (j t) -> p j t", j=NCC),
               reads=[pbk], writes=[("l3wdT",)])

    for t3 in range(NTT):
        e_dma(ctx, "sync", xtile[:, :],
              aps["x_in"][t3 * 128:(t3 + 1) * 128, :], writes=[("l3xt",)])
        e_ts(ctx, xtile[:, :], xtile[:, :], stats[0][:, t3:t3 + 1],
             ALU.subtract, s2=None, op1=ALU.bypass,
             reads=[("l3xt",), ("stats",)], writes=[("l3xt",)])
        e_ts(ctx, xtile[:, :], xtile[:, :], stats[1][:, t3:t3 + 1],
             ALU.mult, reads=[("l3xt",), ("stats",)], writes=[("l3xt",)])
        emit_quant_tile(ctx, xtile, scr, ("l3xt",))
        e_copy(ctx, xdb[:, :], xtile[:, :], reads=[("l3xt",)],
               writes=[("l3xdb",)])
        pb, pbk = ctx.pbf()
        for j in range(NCC):
            e_tr(ctx, pb[:, j * 128:(j + 1) * 128],
                 xdb[:, j * 128:(j + 1) * 128], ident[:, :],
                 reads=[("l3xdb",), ("ident",)], writes=[pbk])
        e_copy(ctx, xdf[:, :, :],
               pb[:, :].rearrange("p (j t) -> p j t", j=NCC),
               reads=[pbk], writes=[("l3xdf",)])
        for ob in range(2):
            ps, pk = ctx.pbank(4 + ob)
            for cc in range(NCC):
                e_mm(ctx, ps[:, :], xdf[:, cc, :],
                     wdT[:, cc, ob * 512:(ob + 1) * 512],
                     start=(cc == 0), stop=(cc == NCC - 1),
                     reads=[("l3xdf",), ("l3wdT",)], writes=[pk])
            skey = ("stage", t3 % 2, ob)
            e_tt(ctx, ALU.add, stage[t3 % 2][:, ob * 512:(ob + 1) * 512],
                 ps[:, :], bufs["fb_rep"][:, ob * 512:(ob + 1) * 512],
                 reads=[pk, ("fbrep",)], writes=[skey])
        e_dma(ctx, "scalar", aps["acc_f"][t3 * 128:(t3 + 1) * 128, :],
              stage[t3 % 2][:, :],
              reads=[("stage", t3 % 2, 0), ("stage", t3 % 2, 1)],
              writes=[("acc_f", t3)])


def build_l3(fctx=None, prefix="", x_in_ap=None, x_out_ap=None):
    nc = fctx.nc if fctx is not None else bass.Bass(
        "TRN2", target_bir_lowering=False)
    g = lambda n, sh: nc.dram_tensor(prefix + n, sh, F32,
                                     kind="ExternalInput").ap()
    gb = lambda n, sh: nc.dram_tensor(prefix + n, sh, BF16,
                                      kind="ExternalInput").ap()
    aps = {
        "x_in": x_in_ap if x_in_ap is not None else g("x_in", [T, D]),
        "W_qkv": g("W_qkv", [D, 3 * D]),
        "bqkv_p": g("bqkv_p", [64, 48]),
        "bqkv_rows": g("bqkv_rows", [16, 192]),
        "wo_head": g("wo_head", [64, 16 * D]),
        "bo_rep": g("bo_rep", [128, D]),
        "fw": g("fw", [D, D]),
        "fb_rep": g("fb_rep", [128, D]),
        "ones_f32": g("ones_f32", [128, 136]),
        "ident": gb("ident", [128, 128]),
        "ones_bf": gb("ones_bf", [128, 640]),
        "diagmask": gb("diagmask", [128, 4 * 512]),
        "row1": gb("row1", [128, 512]),
    }
    aps["diagmask"] = aps["diagmask"].rearrange("p (m q) -> p m q", m=4)
    aps["x_out"] = x_out_ap if x_out_ap is not None else nc.dram_tensor(
        prefix + "x_out", [T, D], F32, kind="ExternalOutput").ap()
    acck = "ExternalOutput" if DBG else "Internal"
    aps["acc_a"] = nc.dram_tensor(prefix + "acc_a", [T, D], F32,
                                  kind=acck).ap()
    aps["acc_f"] = nc.dram_tensor(prefix + "acc_f", [T, D], F32,
                                  kind=acck).ap()

    def body(ctx):
        sched = ctx.s
        bufs = make_common_bufs(ctx, aps, 3)
        for nm, key, shp, dt_ in (
                ("ones_bf", "ones", [128, 640], BF16),
                ("ones_f32", "ones", [128, 136], F32),
                ("diagmask", "dmask", [128, 4, 512], BF16),
                ("row1", "row1", [128, 512], BF16),
                ("bo_rep", "borep", [128, D], F32),
                ("fb_rep", "fbrep", [128, D], F32),
                ("bqkv_p", "bqkvp", [64, 48], F32)):
            bufs[nm] = ctx.sb(f"sb_{nm}", shp, dt_)
            e_dma(ctx, "gpsimd",
                  bufs[nm][:, :, :] if len(shp) == 3 else bufs[nm][:, :],
                  aps[nm], writes=[(key,)])

        xn = ctx.sb("xn", [128, NCC, PADW], BF16)
        xnkey = "xn"
        e_memset(ctx, xn[:, :, 0:1], 0.0, writes=[(xnkey, "all")])
        e_memset(ctx, xn[:, :, 1 + T:PADW], 0.0, writes=[(xnkey, "all")])
        msb = ctx.sb("l3_msb", [128, NTT], F32)
        rsb = ctx.sb("l3_rsb", [128, NTT], F32)

        with ExitStack() as st:
            pctx = sub_ctx(ctx, st)
            eb = make_epi_bufs(pctx, 0)
            emit_ln(pctx, aps["x_in"], [], None, xn, xnkey,
                    stats=(msb, rsb), bufs={**bufs, **eb})
        sched.global_barrier()
        with ExitStack() as st:
            emit_l3_attn(sub_ctx(ctx, st), aps, xn, xnkey, bufs)
        sched.global_barrier()
        with ExitStack() as st:
            emit_l3_ffn(sub_ctx(ctx, st), aps, (msb, rsb), bufs)
        sched.global_barrier()
        with ExitStack() as st:
            fctx = sub_ctx(ctx, st)
            eb = make_epi_bufs(fctx, 2)
            emit_ln(fctx, aps["x_in"],
                    [(aps["acc_a"], "acc_a"), (aps["acc_f"], "acc_f")],
                    aps["x_out"], None, xnkey, bufs={**bufs, **eb})
    if fctx is None:
        with ExitStack() as stack:
            sched = Sched(nc, stack)
            ctx = Ctx(nc, sched, stack)
            body(ctx)
            sched.finish()
            with nc.Block() as block:
                sched.replay(block)
        return nc
    with ExitStack() as lst:
        body(sub_ctx(fctx, lst))
    fctx.s.global_barrier()



def make_common_bufs(ctx, aps, layer):
    bufs = {}
    bufs["ident"] = ctx.sb("sb_ident", [128, 128], BF16)
    e_dma(ctx, "gpsimd", bufs["ident"][:, :], aps["ident"], writes=[("ident",)])
    bufs["stage"] = [ctx.sb(f"stg{i}", [128, D], F32) for i in range(2)]
    return bufs


_EPI_N = [0]


def make_epi_bufs(ctx, nacc):
    n = _EPI_N[0]
    _EPI_N[0] += 1
    bufs = {}
    bufs["epi_a"] = [ctx.sb(f"epia{n}_{i}", [128, D], F32) for i in range(3)]
    bufs["epi_b"] = [[ctx.sb(f"epib{n}_{a}_{i}", [128, D], F32)
                      for i in range(2)] for a in range(nacc)]
    bufs["epi_m"] = [ctx.sb(f"epim{n}_{i}", [128, 4], F32) for i in range(2)]
    bufs["epi_xc"] = [ctx.sb(f"epixc{n}_{i}", [128, D], F32) for i in range(2)]
    bufs["epi_sq"] = [ctx.sb(f"episq{n}_{i}", [128, D], F32) for i in range(2)]
    bufs["epi_ss"] = [ctx.sb(f"episs{n}_{i}", [128, 1], F32) for i in range(2)]
    bufs["epi_tm"] = [ctx.sb(f"epitm{n}_{i}", [128, D], BF16) for i in range(2)]
    return bufs


def sub_ctx(ctx, stack):
    c = Ctx.__new__(Ctx)
    c.__dict__.update(ctx.__dict__)
    c.stack = stack
    return c


def build_l0(fctx=None, prefix="", x_in_ap=None, x_out_ap=None):
    nc = fctx.nc if fctx is not None else bass.Bass(
        "TRN2", target_bir_lowering=False)
    g = lambda n, sh: nc.dram_tensor(prefix + n, sh, F32,
                                     kind="ExternalInput").ap()
    aps = {
        "x_in": x_in_ap if x_in_ap is not None else g("x_in", [T, D]),
        "Wq": g("Wq", [D, D]), "Wk1": g("Wk1", [D, D]), "Wv": g("Wv", [D, D]),
        "Wk2": g("Wk2", [D, D]), "Wp": g("Wp", [D, D]),
        "WrT": g("WrT", [D, D]), "WkT": g("WkT", [D, D]), "WvT": g("WvT", [D, D]),
        "b_q": g("b_q", [128, NCC]), "b_k1": g("b_k1", [128, NCC]),
        "b_v": g("b_v", [128, NCC]), "b_k2": g("b_k2", [128, NCC]),
        "bp_rep": g("bp_rep", [128, D]),
        "mu_r": g("mu_r", [128, NCC]), "mu_k": g("mu_k", [128, NCC]),
        "mask_triu": g("mask_triu", [128, 128]),
        "mask_triu8": g("mask_triu8", [128, 128]),
    }
    aps["ident"] = nc.dram_tensor(prefix + "ident", [128, 128], BF16,
                                  kind="ExternalInput").ap()
    aps["x_out"] = x_out_ap if x_out_ap is not None else nc.dram_tensor(
        prefix + "x_out", [T, D], F32, kind="ExternalOutput").ap()
    acck = "ExternalOutput" if DBG else "Internal"
    aps["acc_a"] = nc.dram_tensor(prefix + "acc_a", [T, D], F32,
                                  kind=acck).ap()
    aps["acc_f"] = nc.dram_tensor(prefix + "acc_f", [T, D], F32,
                                  kind=acck).ap()
    if DBG:
        for nm, sh, dt in (("dbg_xn", [128, NCC, PADW], BF16),
                           ("dbg_q", [128, T], BF16), ("dbg_k1", [128, T], BF16),
                           ("dbg_v", [128, T], BF16), ("dbg_k2", [128, T], BF16),
                           ("dbg_q2", [128, T], BF16), ("dbg_o1", [128, T], BF16),
                           ("dbg_y", [128, NCC, T], BF16),
                           ("dbg_xr", [128, NCC, 512], BF16),
                           ("dbg_xk", [128, NCC, 512], BF16),
                           ("dbg_kact", [128, NCC, 512], BF16),
                           ("dbg_rtm", [128, 4, 1024], BF16)):
            aps[nm] = nc.dram_tensor(nm, sh, dt, kind="ExternalOutput").ap()

    def body(ctx):
        sched = ctx.s
        bufs = make_common_bufs(ctx, aps, 0)
        for nm, key in (("b_q", "bias_l0"), ("b_k1", "bias_l0"),
                        ("b_v", "bias_l0"), ("b_k2", "b_k2"),
                        ("mu_r", "mu"), ("mu_k", "mu")):
            bufs[nm] = ctx.sb(f"sb_{nm}", [128, NCC], F32)
            e_dma(ctx, "gpsimd", bufs[nm][:, :], aps[nm], writes=[(key,)])
        bufs["bp_rep"] = ctx.sb("sb_bp_rep", [128, D], F32)
        e_dma(ctx, "gpsimd", bufs["bp_rep"][:, :], aps["bp_rep"],
              writes=[("bp_rep",)])
        bufs["mask_triu"] = ctx.sb("sb_mask", [128, 128], F32)
        e_dma(ctx, "gpsimd", bufs["mask_triu"][:, :], aps["mask_triu"],
              writes=[("mask",)])
        bufs["mask_triu8"] = ctx.sb("sb_mask8", [128, 128], F32)
        e_dma(ctx, "gpsimd", bufs["mask_triu8"][:, :], aps["mask_triu8"],
              writes=[("mask",)])

        xn = ctx.sb("xn", [128, NCC, PADW], BF16)
        xnkey = "xn"
        e_memset(ctx, xn[:, :, 0:1], 0.0, writes=[(xnkey, "all")])
        e_memset(ctx, xn[:, :, 1 + T:PADW], 0.0, writes=[(xnkey, "all")])

        with ExitStack() as st:
            pctx = sub_ctx(ctx, st)
            eb = make_epi_bufs(pctx, 0)
            emit_ln(pctx, aps["x_in"], [], None, xn, xnkey, bufs={**bufs, **eb})
        sched.global_barrier()
        if DBG:
            e_dma(ctx, "sync", aps["dbg_xn"], xn[:, :, :], reads=xn_all(xnkey))
        with ExitStack() as attn_stack:
            emit_l0_attn(sub_ctx(ctx, attn_stack), aps, xn, xnkey, bufs)
        sched.global_barrier()
        with ExitStack() as ffn_stack:
            emit_l0_ffn(sub_ctx(ctx, ffn_stack), aps, xn, xnkey, bufs)
        sched.global_barrier()
        with ExitStack() as st:
            fctx = sub_ctx(ctx, st)
            eb = make_epi_bufs(fctx, 2)
            emit_ln(fctx, aps["x_in"],
                    [(aps["acc_a"], "acc_a"), (aps["acc_f"], "acc_f")],
                    aps["x_out"], None, xnkey, bufs={**bufs, **eb})
    if fctx is None:
        with ExitStack() as stack:
            sched = Sched(nc, stack)
            ctx = Ctx(nc, sched, stack)
            body(ctx)
            sched.finish()
            with nc.Block() as block:
                sched.replay(block)
        return nc
    with ExitStack() as lst:
        body(sub_ctx(fctx, lst))
    fctx.s.global_barrier()


def build_l1(fctx=None, prefix="", x_in_ap=None, x_out_ap=None):
    nc = fctx.nc if fctx is not None else bass.Bass(
        "TRN2", target_bir_lowering=False)
    g = lambda n, sh: nc.dram_tensor(prefix + n, sh, F32,
                                     kind="ExternalInput").ap()
    gb = lambda n, sh: nc.dram_tensor(prefix + n, sh, BF16,
                                      kind="ExternalInput").ap()
    aps = {
        "x_in": x_in_ap if x_in_ap is not None else g("x_in", [T, D]),
        "wqkvT": g("wqkvT", [D, 1536]),
        "w_outT": g("w_outT", [512, D]),
        "bo_rep": g("bo_rep", [128, D]),
        "gwT": g("gwT", [D, 8]),
        "gb_rep": g("gb_rep", [128, 8]),
        "w1a": g("w1a", [9 * 1152, 1408]),
        "w3a": g("w3a", [9 * 1152, 1408]),
        "w2a": g("w2a", [9 * 1536, 1024]),
        "ones_f32": g("ones_f32", [128, 136]),
        "ident": gb("ident", [128, 128]),
        "ones_bf": gb("ones_bf", [128, 640]),
        "diagmask": gb("diagmask", [128, 4 * 512]),
        "row1": gb("row1", [128, 512]),
    }
    aps["diagmask"] = aps["diagmask"].rearrange("p (m q) -> p m q", m=4)
    aps["x_out"] = x_out_ap if x_out_ap is not None else nc.dram_tensor(
        prefix + "x_out", [T, D], F32, kind="ExternalOutput").ap()
    acck = "ExternalOutput" if DBG else "Internal"
    aps["acc_a"] = nc.dram_tensor(prefix + "acc_a", [T, D], F32,
                                  kind=acck).ap()
    aps["acc_f"] = nc.dram_tensor(prefix + "acc_f", [T, D], F32,
                                  kind=acck).ap()

    def body(ctx):
        sched = ctx.s
        bufs = make_common_bufs(ctx, aps, 1)
        for nm, key in (("ones_bf", "ones"), ("ones_f32", "ones"),
                        ("diagmask", "dmask"), ("row1", "row1"),
                        ("bo_rep", "borep")):
            shp = {"ones_bf": [128, 640], "ones_f32": [128, 136],
                   "diagmask": [128, 4, 512], "row1": [128, 512],
                   "bo_rep": [128, D]}[nm]
            dt_ = F32 if nm in ("ones_f32", "bo_rep") else BF16
            bufs[nm] = ctx.sb(f"sb_{nm}", shp, dt_)
            e_dma(ctx, "gpsimd",
                  bufs[nm][:, :, :] if len(shp) == 3 else bufs[nm][:, :],
                  aps[nm], writes=[(key,)])

        xn = ctx.sb("xn", [128, NCC, PADW], BF16)
        xnkey = "xn"
        e_memset(ctx, xn[:, :, 0:1], 0.0, writes=[(xnkey, "all")])
        e_memset(ctx, xn[:, :, 1 + T:PADW], 0.0, writes=[(xnkey, "all")])

        with ExitStack() as st:
            pctx = sub_ctx(ctx, st)
            eb = make_epi_bufs(pctx, 0)
            emit_ln(pctx, aps["x_in"], [], None, xn, xnkey, bufs={**bufs, **eb})
        sched.global_barrier()
        with ExitStack() as st:
            emit_l1_attn(sub_ctx(ctx, st), aps, xn, xnkey, bufs)
        sched.global_barrier()
        with ExitStack() as st:
            emit_l1_ffn(sub_ctx(ctx, st), aps, xn, xnkey, bufs)
        sched.global_barrier()
        with ExitStack() as st:
            fctx = sub_ctx(ctx, st)
            eb = make_epi_bufs(fctx, 2)
            emit_ln(fctx, aps["x_in"],
                    [(aps["acc_a"], "acc_a"), (aps["acc_f"], "acc_f")],
                    aps["x_out"], None, xnkey, bufs={**bufs, **eb})
    if fctx is None:
        with ExitStack() as stack:
            sched = Sched(nc, stack)
            ctx = Ctx(nc, sched, stack)
            body(ctx)
            sched.finish()
            with nc.Block() as block:
                sched.replay(block)
        return nc
    with ExitStack() as lst:
        body(sub_ctx(fctx, lst))
    fctx.s.global_barrier()



def vec_p(v):
    return np.ascontiguousarray(v.reshape(NCC, 128).T)


def host_consts():
    c = {}
    c["ident"] = np.eye(128, dtype=BF)
    tri = np.triu(np.ones((128, 128), np.float32))
    c["mask_triu"] = tri
    c["mask_triu8"] = (tri * 0.125).astype(np.float32)
    c["ones_bf"] = np.ones((128, 640), dtype=BF)
    c["ones_f32"] = np.ones((128, 136), dtype=np.float32)
    p = np.arange(128)[:, None, None]
    mm_ = np.arange(4)[None, :, None]
    q = np.arange(512)[None, None, :]
    c["diagmask"] = ((128 * mm_ + p) <= q).astype(BF).reshape(128, 4 * 512)
    r1 = np.zeros((128, 512), dtype=BF)
    r1[0, :] = 1
    c["row1"] = r1
    return c


def _aug_pack(w, b, rows_to):
    K, N = w.shape
    out = np.zeros((rows_to, N), w.dtype)
    out[:K] = w
    out[K] = b
    return out


def l1_inputs(inputs, xb):
    c = host_consts()
    w1a = np.concatenate(
        [_aug_pack(inputs["L1_ffn_w1"][e], inputs["L1_ffn_b1"][e], 1152)
         for e in range(8)] +
        [_aug_pack(inputs["L1_ffn_sw1"], inputs["L1_ffn_sb1"], 1152)], axis=0)
    w3a = np.concatenate(
        [_aug_pack(inputs["L1_ffn_w3"][e], inputs["L1_ffn_b3"][e], 1152)
         for e in range(8)] +
        [_aug_pack(inputs["L1_ffn_sw3"], inputs["L1_ffn_sb3"], 1152)], axis=0)
    w2a = np.concatenate(
        [_aug_pack(inputs["L1_ffn_w2"][e], inputs["L1_ffn_b2"][e], 1536)
         for e in range(8)] +
        [_aug_pack(inputs["L1_ffn_sw2"], inputs["L1_ffn_sb2"], 1536)], axis=0)
    return {
        "x_in": np.ascontiguousarray(xb),
        "wqkvT": np.ascontiguousarray(inputs["L1_attn_w_qkv"].T),
        "w_outT": np.ascontiguousarray(inputs["L1_attn_w_out"].T),
        "bo_rep": np.ascontiguousarray(
            np.broadcast_to(inputs["L1_attn_b_out"], (128, D))),
        "gwT": np.ascontiguousarray(inputs["L1_ffn_gate_w"].T),
        "gb_rep": np.ascontiguousarray(
            np.broadcast_to(inputs["L1_ffn_gate_b"], (128, 8))),
        "w1a": w1a, "w3a": w3a, "w2a": w2a,
        "ident": c["ident"], "ones_bf": c["ones_bf"],
        "ones_f32": c["ones_f32"], "diagmask": c["diagmask"],
        "row1": c["row1"],
    }


def l0_inputs(inputs, xb):
    c = host_consts()
    return {
        "x_in": np.ascontiguousarray(xb),
        "Wq": inputs["L0_attn_Wq"], "Wk1": inputs["L0_attn_Wk1"],
        "Wv": inputs["L0_attn_Wv"], "Wk2": inputs["L0_attn_Wk2"],
        "Wp": inputs["L0_attn_Wp"],
        "WrT": np.ascontiguousarray(inputs["L0_ffn_W_r"].T),
        "WkT": np.ascontiguousarray(inputs["L0_ffn_W_k"].T),
        "WvT": np.ascontiguousarray(inputs["L0_ffn_W_v"].T),
        "b_q": vec_p(inputs["L0_attn_bq"]), "b_k1": vec_p(inputs["L0_attn_bk1"]),
        "b_v": vec_p(inputs["L0_attn_bv"]), "b_k2": vec_p(inputs["L0_attn_bk2"]),
        "bp_rep": np.ascontiguousarray(
            np.broadcast_to(inputs["L0_attn_bp"], (128, D))),
        "mu_r": vec_p(inputs["L0_ffn_mu_r"]), "mu_k": vec_p(inputs["L0_ffn_mu_k"]),
        "ident": c["ident"], "mask_triu": c["mask_triu"],
        "mask_triu8": c["mask_triu8"],
    }


def l3_inputs(inputs, xb):
    c = host_consts()
    return {
        "x_in": np.ascontiguousarray(xb),
        "W_qkv": inputs["L3_attn_W_qkv"],
        "bqkv_p": np.ascontiguousarray(
            inputs["L3_attn_b_qkv"].reshape(16, 3, 64).transpose(2, 0, 1)
            .reshape(64, 48)),
        "bqkv_rows": np.ascontiguousarray(
            inputs["L3_attn_b_qkv"].reshape(16, 192)),
        "wo_head": np.ascontiguousarray(
            inputs["L3_attn_W_out"].reshape(16, 64, D).transpose(1, 0, 2)
            .reshape(64, 16 * D)),
        "bo_rep": np.ascontiguousarray(
            np.broadcast_to(inputs["L3_attn_b_out"], (128, D))),
        "fw": inputs["L3_ffn_weight"],
        "fb_rep": np.ascontiguousarray(
            np.broadcast_to(inputs["L3_ffn_bias"], (128, D))),
        "ident": c["ident"], "ones_bf": c["ones_bf"],
        "ones_f32": c["ones_f32"], "diagmask": c["diagmask"],
        "row1": c["row1"],
    }


def build_fused():
    nc = bass.Bass("TRN2", target_bir_lowering=False)
    x_in = nc.dram_tensor("x_in", [T, D], F32, kind="ExternalInput").ap()
    x_out = nc.dram_tensor("x_out", [T, D], F32, kind="ExternalOutput").ap()
    xmid = [nc.dram_tensor(f"xmid{i}", [T, D], F32).ap() for i in range(3)]
    chain = [x_in] + xmid + [x_out]
    with ExitStack() as stack:
        sched = Sched(nc, stack)
        ctx = Ctx(nc, sched, stack)
        build_l0(ctx, "L0_", chain[0], chain[1])
        build_l1(ctx, "L1_", chain[1], chain[2])
        build_l2(ctx, "L2_", chain[2], chain[3])
        build_l3(ctx, "L3_", chain[3], chain[4])
        sched.finish()
        with nc.Block() as block:
            sched.replay(block)
    return nc


def fused_inputs(inputs, xb):
    d = {"x_in": np.ascontiguousarray(xb)}
    for k, prep in enumerate((l0_inputs, l1_inputs, l2_inputs, l3_inputs)):
        for name, arr in prep(inputs, xb).items():
            if name != "x_in":
                d[f"L{k}_{name}"] = arr
    return d


FUSED = True


def kernel(**inputs):
    inputs = {k: np.asarray(v) for k, v in inputs.items()}
    x = inputs["L0_attn_x"]
    if FUSED:
        nc = build_fused()
        in_maps = [fused_inputs(inputs, x[b]) for b in range(8)]
        res = run_bass_kernel_spmd(nc, in_maps, core_ids=list(range(8)))
        return np.stack([res.results[b]["x_out"] for b in range(8)], axis=0)
    nc = build_l0()
    in_maps = [l0_inputs(inputs, x[b]) for b in range(8)]
    res = run_bass_kernel_spmd(nc, in_maps, core_ids=list(range(8)))
    x = np.stack([res.results[b]["x_out"] for b in range(8)], axis=0)
    for build, prep in ((build_l1, l1_inputs), (build_l2, l2_inputs),
                        (build_l3, l3_inputs)):
        nc = build()
        in_maps = [prep(inputs, x[b]) for b in range(8)]
        res = run_bass_kernel_spmd(nc, in_maps, core_ids=list(range(8)))
        x = np.stack([res.results[b]["x_out"] for b in range(8)], axis=0)
    return x
